# version 11
# baseline (speedup 1.0000x reference)
import numpy as np
import ml_dtypes
import concourse.bass as bass
import concourse.mybir as mybir
from concourse.bass_utils import run_bass_kernel_spmd
from contextlib import ExitStack

F32 = mybir.dt.float32
BF16 = mybir.dt.bfloat16
AF = mybir.ActivationFunctionType
ALU = mybir.AluOpType
AX = mybir.AxisListType

R = 8
S = 8192
D = 4096
TOK = S // R
NT = TOK // 128
NJ = D // 128
NQH, NKV, G4, HD = 32, 8, 4, 128
EPS = 1e-5
ATT_SCALE = float(HD) ** -0.5
BH_MARGIN = 1.04
NE, RK = 8, 16
ER = NE * RK
NCH = S // 128
NGRP = NCH // 2
NEPOCH = R * G4 * 2
TOTG = NEPOCH * NGRP
BF = ml_dtypes.bfloat16
ENGINES = ["tensor", "vector", "scalar", "gpsimd", "sync"]


def _lhsT_image(W):
    dout, din = W.shape
    OC, J = dout // 128, din // 128
    W4 = W.reshape(OC, 128, J, 128)
    return np.ascontiguousarray(W4.transpose(0, 3, 2, 1)).reshape(OC, 128, J * 128)


def host_images(inp):
    Wq = np.asarray(inp["L0_attn_Wq"], np.float32)
    Wk = np.asarray(inp["L0_attn_Wk"], np.float32)
    Wv = np.asarray(inp["L0_attn_Wv"], np.float32)
    Wo = np.asarray(inp["L0_attn_Wo"], np.float32)
    bv = np.asarray(inp["L0_attn_bv"], np.float32)
    rw = np.asarray(inp["L0_ffn_router_w"], np.float32)
    rb = np.asarray(inp["L0_ffn_router_b"], np.float32)
    A = np.asarray(inp["L0_ffn_A"], np.float32)
    Bm = np.asarray(inp["L0_ffn_Bm"], np.float32)

    img = {}
    img["wq"] = _lhsT_image(Wq)
    img["wk"] = _lhsT_image(Wk)
    img["wv"] = _lhsT_image(Wv)
    Wo4 = Wo.reshape(8, 512, NJ, 128)
    img["wo"] = np.ascontiguousarray(Wo4.transpose(0, 2, 3, 1))
    Aflat = A.reshape(ER, D)
    img["at"] = np.ascontiguousarray(
        Aflat.reshape(ER, NJ, 128).transpose(2, 1, 0)).reshape(128, NJ * ER)
    img["bfl"] = np.ascontiguousarray(Bm.transpose(0, 2, 1)).reshape(ER, D)
    img["rw"] = np.ascontiguousarray(
        rw.reshape(NE, NJ, 128).transpose(2, 1, 0)).reshape(128, NJ * NE)
    img["rb"] = np.ascontiguousarray(rb.reshape(NE, 1))
    img["bvc"] = np.ascontiguousarray(bv.reshape(NKV, 128).T)
    img["idf"] = np.eye(128, dtype=np.float32)
    img["idb"] = np.eye(128, dtype=np.float32).astype(BF)
    e16 = np.zeros((NE, 128), np.float32)
    for e in range(NE):
        e16[e, e * RK:(e + 1) * RK] = 0.25
    img["e16"] = e16
    return img


class Phase:

    def __init__(self, c, name):
        self.c = c
        self.name = name
        self.ev = {e: [] for e in ENGINES}

    def op(self, eng, build=None, waits=(), inc=None, by=1, mark=None):
        c = self.c
        if inc is not None:
            c["cnt"][inc] += by
            if mark is not None:
                c["marks"][mark] = (inc, c["cnt"][inc])
        self.ev[eng].append((tuple(waits), build, inc, by))

    def emit(self):
        c = self.c
        nc, sem, marks = c["nc"], c["sem"], c["marks"]
        with nc.Block() as blk:
            for ename in ENGINES:
                evs = self.ev[ename]
                if not evs:
                    continue

                def make(evs=evs):
                    def f(eng):
                        for waits, build, incname, by in evs:
                            for w in waits:
                                kind, a, b_ = w
                                if kind == "abs":
                                    sname, v = a, b_
                                else:
                                    sname, mv = marks[a]
                                    v = mv + b_
                                if v > 0:
                                    eng.wait_ge(sem[sname], v)
                            instr = build(eng) if build is not None else None
                            if incname is not None:
                                tgt = instr if instr is not None else eng.nop()
                                tgt.then_inc(sem[incname], by)
                    return f
                getattr(blk, ename)(make())


def W(mark, delta=0):
    return ("mark", mark, delta)


def WV(sname, val):
    return ("abs", sname, val)


def build_nc(debug=False):
    nc = bass.Bass("TRN2", target_bir_lowering=False, debug=False)
    c = {"nc": nc, "cnt": {}, "marks": {}, "debug": debug}

    def dram(name, shape, dtype, kind=None):
        if kind:
            return nc.dram_tensor(name, shape, dtype, kind=kind).ap()
        return nc.dram_tensor(name, shape, dtype).ap()

    c["x_in"] = dram("x", [TOK, D], F32, "ExternalInput")
    c["wq_d"] = dram("wq", [NJ, 128, D], F32, "ExternalInput")
    c["wk_d"] = dram("wk", [NKV, 128, D], F32, "ExternalInput")
    c["wv_d"] = dram("wv", [NKV, 128, D], F32, "ExternalInput")
    c["wo_d"] = dram("wo", [8, NJ, 128, 512], F32, "ExternalInput")
    c["at_d"] = dram("at", [128, NJ * ER], F32, "ExternalInput")
    c["bfl_d"] = dram("bfl", [ER, D], F32, "ExternalInput")
    c["rw_d"] = dram("rw", [128, NJ * NE], F32, "ExternalInput")
    c["rb_d"] = dram("rb", [NE, 1], F32, "ExternalInput")
    c["bvc_d"] = dram("bvc", [128, NKV], F32, "ExternalInput")
    c["idf_d"] = dram("idf", [128, 128], F32, "ExternalInput")
    c["idb_d"] = dram("idb", [128, 128], BF16, "ExternalInput")
    c["e16_d"] = dram("e16", [NE, 128], F32, "ExternalInput")
    c["out_d"] = dram("out", [TOK, D], F32, "ExternalOutput")

    c["x1d"] = dram("x1d", [TOK, D], F32)
    c["x2d"] = dram("x2d", [TOK, D], F32)
    c["qtd"] = dram("qtd", [NQH, 128, TOK], BF16)
    c["ktd"] = dram("ktd", [NKV * 128, TOK], BF16)
    c["vtd"] = dram("vtd", [TOK, NKV * 129], BF16)
    c["ktg"] = dram("ktg", [R * NKV * 128, TOK], BF16)
    c["vtg"] = dram("vtg", [R * TOK, NKV * 129], BF16)
    c["wo_bf"] = dram("wo_bf", [8, NJ, 128, 512], BF16)
    if debug:
        c["dbg_x1"] = dram("dbg_x1", [TOK, D], F32, "ExternalOutput")
        c["dbg_x2"] = dram("dbg_x2", [TOK, D], F32, "ExternalOutput")
        c["dbg_ctx"] = dram("dbg_ctx", [NJ * 128, TOK], F32, "ExternalOutput")
        c["dbg_kt"] = dram("dbg_kt", [R * NKV * 128, TOK], F32, "ExternalOutput")
        c["dbg_vt"] = dram("dbg_vt", [R * TOK, NKV * 129], F32, "ExternalOutput")
        c["dbg_qt"] = dram("dbg_qt", [NQH * 128, TOK], F32, "ExternalOutput")
        c["dbg_mv"] = dram("dbg_mv", [128, 2 * NT], F32, "ExternalOutput")
        c["dbg_rs"] = dram("dbg_rs", [128, NT], F32, "ExternalOutput")
        c["dbg_ve"] = dram("dbg_ve", [128, NT], F32, "ExternalOutput")

    with ExitStack() as ST:
        E = ST.enter_context
        c["at_sb"] = E(nc.sbuf_tensor("at_sb", [128, NJ * ER], BF16))
        c["bfl_sb"] = E(nc.sbuf_tensor("bfl_sb", [ER, D], BF16))
        c["rw_sb"] = E(nc.sbuf_tensor("rw_sb", [128, NJ * NE], BF16))
        c["rb_sb"] = E(nc.sbuf_tensor("rb_sb", [NE, 1], F32))
        c["bv_sb"] = E(nc.sbuf_tensor("bv_sb", [128, NKV], F32))
        c["idf_sb"] = E(nc.sbuf_tensor("idf_sb", [128, 128], F32))
        c["idb_sb"] = E(nc.sbuf_tensor("idb_sb", [128, 128], BF16))
        c["e16_sb"] = E(nc.sbuf_tensor("e16_sb", [NE, 128], F32))
        c["ones_row"] = E(nc.sbuf_tensor("ones_row", [1, 128], BF16))
        c["ones_col"] = E(nc.sbuf_tensor("ones_col", [128, 1], BF16))
        for i in (1, 2, 3):
            c[f"mv{i}"] = E(nc.sbuf_tensor(f"mv{i}", [128, 2 * NT], F32))
            c[f"rstd{i}"] = E(nc.sbuf_tensor(f"rstd{i}", [128, NT], F32))
        c["stat6"] = E(nc.sbuf_tensor("stat6", [128, 48], F32))
        c["stat63"] = E(nc.sbuf_tensor("stat63", [128, NT * 48], F32))
        c["veps"] = E(nc.sbuf_tensor("veps", [128, NT], F32))
        c["lnv"] = E(nc.sbuf_tensor("lnv", [128, NT], F32))

        names = ["s_w", "s_x0", "s_x1", "s_st6", "s_y", "s_tr", "s_trv", "s_mm",
                 "s_rt", "s_rv", "s_ract", "s_hw", "s_pst", "s_bev",
                 "s_xn0", "s_xn1", "s_bdma0", "s_bdma1",
                 "s_rst", "s_cc", "s_kv0", "s_kv1", "s_st", "s_exp", "s_avm",
                 "s_evac", "s_ctr", "s_ctv", "s_sq", "s_nb",
                 "s_kst0", "s_kst1", "s_mm2", "s_kv20", "s_kv21",
                 "s_wo0", "s_wo1", "s_ph", "s_dbg", "s_gw",
                 "s_bd20", "s_bd21", "s_xa0", "s_xa1", "s_pj"]
        c["sem"] = {n: E(nc.semaphore(n)) for n in names}
        for n in names:
            c["cnt"][n] = 0

        nph = [0]

        def barrier():
            nph[0] += 1
            tgt = 5 * nph[0]
            semh = c["sem"]["s_ph"]
            with nc.Block() as blk:
                for ename in ENGINES:
                    def mk(eng):
                        eng.drain().then_inc(semh, 1)
                        eng.wait_ge(semh, tgt)
                    getattr(blk, ename)(mk)

        with ExitStack() as s1:
            c["xfull"] = s1.enter_context(
                nc.sbuf_tensor("xfull1", [128, NT * D], F32))
            with nc.named_scope("ln1"):
                emit_prologue(c)
                emit_ln1(c)
                barrier()
            if debug:
                emit_debug_stats(c)
                barrier()
            alloc_ffn(c, s1)
            with nc.named_scope("ffn1"):
                emit_ffn(c, call=0)
        if debug:
            barrier()
            emit_debug_copy(c, c["x1d"], c["dbg_x1"], TOK * D * 4)
            barrier()
        with nc.named_scope("lnf2"):
            emit_ln_finish(c, idx=2)
            barrier()
        with ExitStack() as s2:
            with nc.named_scope("proj"):
                emit_projections(c, s2)
                barrier()
        if debug:
            emit_debug_cast(c, c["ktg"], c["dbg_kt"], R * NKV * 128, TOK)
            emit_debug_cast(c, c["vtg"], c["dbg_vt"], R * TOK, NKV * 129)
            emit_debug_cast(c, c["qtd"].rearrange("a p t -> (a p) t"), c["dbg_qt"],
                            NQH * 128, TOK)
            barrier()
        with ExitStack() as s3:
            c["ctxT"] = s3.enter_context(nc.sbuf_tensor("ctxT", [128, NJ * TOK], BF16))
            with ExitStack() as s3a:
                with nc.named_scope("attn"):
                    emit_attention(c, s3a)
                    barrier()
            if debug:
                emit_debug_ctx(c)
                barrier()
            with ExitStack() as s3b:
                with nc.named_scope("oproj"):
                    emit_oproj(c, s3b)
        if debug:
            barrier()
            emit_debug_copy(c, c["x2d"], c["dbg_x2"], TOK * D * 4)
            barrier()
        with nc.named_scope("lnf3"):
            emit_ln_finish(c, idx=3)
            barrier()
        with ExitStack() as s4:
            c["xfull"] = s4.enter_context(
                nc.sbuf_tensor("xfull2", [128, NT * D], F32))
            alloc_ffn(c, s4, tag="z")
            with nc.named_scope("ffn2"):
                emit_ffn(c, call=1)
    return nc


def emit_prologue(c):
    nc, sem = c["nc"], c["sem"]
    ph = Phase(c, "pro")
    g, s, v, sc = "gpsimd", "sync", "vector", "scalar"
    ph.op(g, lambda e: e.dma_start(c["at_sb"][:], c["at_d"], max_dma_last_dim=8192),
          inc="s_gw", by=16)
    ph.op(g, lambda e: e.dma_start(c["bfl_sb"][:], c["bfl_d"], max_dma_last_dim=8192),
          inc="s_gw", by=16)
    ph.op(g, lambda e: e.dma_start(c["rw_sb"][:], c["rw_d"], max_dma_last_dim=8192),
          inc="s_gw", by=16, mark="gall")
    ph.op(s, lambda e: e.dma_start(c["rb_sb"][:], c["rb_d"]), inc="s_w", by=16)
    ph.op(s, lambda e: e.dma_start(c["bv_sb"][:], c["bvc_d"]), inc="s_w", by=16)
    ph.op(s, lambda e: e.dma_start(c["idf_sb"][:], c["idf_d"]), inc="s_w", by=16)
    ph.op(s, lambda e: e.dma_start(c["idb_sb"][:], c["idb_d"]), inc="s_w", by=16)
    ph.op(s, lambda e: e.dma_start(c["e16_sb"][:], c["e16_d"]), inc="s_w", by=16,
          mark="all")
    ph.op(v, lambda e: e.memset(c["ones_row"][:], 1.0))
    ph.op(v, lambda e: e.memset(c["ones_col"][:], 1.0))
    ph.emit()


def _rstd_chain(ph, c, mv, rstd, tag):
    v, sc = "vector", "scalar"
    var_view = mv[:].rearrange("p (m two) -> p m two", two=2)[:, :, 1:2]

    ph.op(v, lambda e: e.tensor_scalar(c["veps"][:], var_view, EPS, None,
                                       op0=ALU.add), inc="s_rst", mark=f"{tag}_ve")
    ph.op(sc, lambda e: e.activation(c["lnv"][:], c["veps"][:], AF.Ln),
          waits=[W(f"{tag}_ve")])
    ph.op(sc, lambda e: e.drain())
    ph.op(sc, lambda e: e.activation(rstd[:], c["lnv"][:], AF.Exp, scale=-0.5),
          inc="s_rst", mark=f"{tag}_done")


def emit_ln1(c):
    nc, sem = c["nc"], c["sem"]
    xfull = c["xfull"]
    ph = Phase(c, "ln1")
    for m in range(NT):
        wl = [] if m < 2 else [W(("ln1x", m - 2))]
        ph.op("sync", (lambda e, m=m: e.dma_start(
            xfull[:, m * D:(m + 1) * D],
            c["x_in"][m * 128:(m + 1) * 128, :])), waits=wl,
            inc=f"s_x{m % 2}", by=16, mark=("ln1x", m))
    for m in range(NT):
        ph.op("vector", None, waits=[W(("ln1x", m))])
        for q in range(8):
            ph.op("vector", (lambda e, m=m, q=q: e.bn_stats(
                c["stat6"][:, 6 * q:6 * q + 6],
                xfull[:, m * D + q * 512:m * D + (q + 1) * 512])))
        ph.op("vector", lambda e: e.drain())
        ph.op("vector", (lambda e, m=m: e.bn_aggr(
            c["mv1"][:, 2 * m:2 * m + 2], c["stat6"][:])))
        ph.op("vector", lambda e: e.drain(), inc="s_st6", mark=("ln1", m))
    _rstd_chain(ph, c, c["mv1"], c["rstd1"], "ln1")
    for ename in ENGINES:
        ph.op(ename, None, waits=[W("all"), W("gall")])
    ph.emit()


def emit_ln_finish(c, idx):
    mv, rstd = c[f"mv{idx}"], c[f"rstd{idx}"]
    ph = Phase(c, f"lnf{idx}")
    for m in range(NT):
        ph.op("vector", (lambda e, m=m: e.bn_aggr(
            mv[:, 2 * m:2 * m + 2], c["stat63"][:, m * 48:(m + 1) * 48])))
    ph.op("vector", lambda e: e.drain())
    _rstd_chain(ph, c, mv, rstd, f"lnf{idx}")
    ph.emit()


def alloc_ffn(c, st, tag=""):
    nc = c["nc"]
    E = st.enter_context
    c["ytm"] = E(nc.sbuf_tensor(tag + "f_ytm", [128, 2 * D], BF16))
    c["y1t"] = E(nc.sbuf_tensor(tag + "f_y1t", [128, 2 * 1024], BF16))
    c["hw_sb"] = E(nc.sbuf_tensor(tag + "f_hw", [128, TOK], BF16))
    c["wexp_sb"] = E(nc.sbuf_tensor(tag + "f_wexp", [128, TOK], F32))
    c["logits_sb"] = E(nc.sbuf_tensor(tag + "f_logits", [NE, TOK], F32))
    c["ltok"] = E(nc.sbuf_tensor(tag + "f_ltok", [128, 8], F32))
    c["l1"] = E(nc.sbuf_tensor(tag + "f_l1", [128, 1], F32))
    c["l1n"] = E(nc.sbuf_tensor(tag + "f_l1n", [128, 1], F32))
    c["m1"] = E(nc.sbuf_tensor(tag + "f_m1", [128, 8], F32))
    c["tmp8"] = E(nc.sbuf_tensor(tag + "f_tmp8", [128, 8], F32))
    c["l2"] = E(nc.sbuf_tensor(tag + "f_l2", [128, 1], F32))
    c["mtop2"] = E(nc.sbuf_tensor(tag + "f_mtop2", [128, 8], F32))
    c["u_sb"] = E(nc.sbuf_tensor(tag + "f_u", [128, 8], F32))
    c["uw"] = E(nc.sbuf_tensor(tag + "f_uw", [128, 8], F32))
    c["ssum"] = E(nc.sbuf_tensor(tag + "f_ssum", [128, 1], F32))
    c["rr"] = E(nc.sbuf_tensor(tag + "f_rr", [128, 1], F32))
    c["wtok"] = E(nc.sbuf_tensor(tag + "f_wtok", [128, 64], F32))
    c["wT8"] = E(nc.sbuf_tensor(tag + "f_wT8", [NE, TOK], F32))
    c["ostage"] = E(nc.sbuf_tensor(tag + "f_ostage", [128, 2 * 1024], F32))
    c["hps"] = E(nc.psum_tensor(tag + "f_hps", [128, 1024], F32))
    c["rpsA"] = E(nc.psum_tensor(tag + "f_rpsA", [NE, 512], F32))
    c["rpsB"] = E(nc.psum_tensor(tag + "f_rpsB", [NE, 512], F32))
    c["wrp"] = E(nc.psum_tensor(tag + "f_wrp", [128, 1024], F32))
    c["tpsA"] = E(nc.psum_tensor(tag + "f_tpsA", [128, 1024], BF16))
    c["tpsB"] = E(nc.psum_tensor(tag + "f_tpsB", [128, 1024], BF16))


def emit_ffn(c, call):
    nc, sem = c["nc"], c["sem"]
    xfull = c["xfull"]
    sink = c["x1d"] if call == 0 else c["out_d"]
    mv = c["mv1"] if call == 0 else c["mv3"]
    rstd = c["rstd1"] if call == 0 else c["rstd3"]
    tps = [c["tpsA"], c["tpsB"]]
    rps = [c["rpsA"], c["rpsB"]]
    bps = [c["hps"][:, 0:512], c["hps"][:, 512:1024],
           c["wrp"][:, 0:512], c["wrp"][:, 512:1024]]
    bpt = [c["hps"], c["wrp"]]
    P = f"f{call}"
    ph = Phase(c, P)
    NBP = NT * 4
    t, v, sc, g, s = "tensor", "vector", "scalar", "gpsimd", "sync"

    if call == 1:
        for m in range(NT):
            wl = [] if m < 2 else [W((P, "x", m - 2))]
            ph.op(s, (lambda e, m=m: e.dma_start(
                xfull[:, m * D:(m + 1) * D],
                c["x2d"][m * 128:(m + 1) * 128, :])), waits=wl,
                inc=f"s_x{m % 2}", by=16, mark=(P, "x", m))

    def ynorm(m):
        wl = [W((P, "x", m))] if call == 1 else []
        if m >= 2:
            wl.append(W((P, "tr", (m - 2, 3))))
        ph.op(v, (lambda e, m=m: e.tensor_scalar(
            c["ytm"][:, (m % 2) * D:(m % 2 + 1) * D],
            xfull[:, m * D:(m + 1) * D],
            mv[:, 2 * m:2 * m + 1], rstd[:, m:m + 1],
            op0=ALU.subtract, op1=ALU.mult)), waits=wl, inc="s_y",
            mark=(P, "y", m))

    def chain(m):
        ph.op(v, (lambda e, m=m: e.tensor_scalar(
            c["logits_sb"][0:NE, m * 128:(m + 1) * 128],
            rps[m // 4][0:NE, (m % 4) * 128:(m % 4 + 1) * 128],
            c["rb_sb"][0:NE, 0:1], None, op0=ALU.add)),
            waits=[W((P, "mm", (m // 4) * 16 + 15))],
            inc="s_rv", mark=(P, "lg", m))
        ph.op(v, (lambda e, m=m: e.tensor_copy(c["ltok"][:], c["wrp"][:, 0:8])),
              waits=[W((P, "ltr", m))])
        ph.op(v, lambda e: e.drain())
        ph.op(v, lambda e: e.tensor_reduce(c["l1"][:], c["ltok"][:], AX.X, ALU.max))
        ph.op(v, lambda e: e.drain())
        ph.op(v, lambda e: e.tensor_scalar(c["l1n"][:], c["l1"][:], -1.0, None,
                                           op0=ALU.mult))
        ph.op(v, lambda e: e.drain(), inc="s_rv", mark=(P, "l1n", m))
        ph.op(v, lambda e: e.tensor_scalar(c["m1"][:], c["ltok"][:], c["l1"][:],
                                           None, op0=ALU.is_ge))
        ph.op(v, lambda e: e.drain())
        ph.op(v, lambda e: e.scalar_tensor_tensor(c["tmp8"][:], c["m1"][:], -1e30,
                                                  c["ltok"][:], op0=ALU.mult,
                                                  op1=ALU.add))
        ph.op(v, lambda e: e.drain())
        ph.op(v, lambda e: e.tensor_reduce(c["l2"][:], c["tmp8"][:], AX.X, ALU.max))
        ph.op(v, lambda e: e.drain())
        ph.op(v, lambda e: e.tensor_scalar(c["mtop2"][:], c["ltok"][:], c["l2"][:],
                                           None, op0=ALU.is_ge))
        ph.op(v, lambda e: e.drain())
        ph.op(v, (lambda e: e.tensor_tensor(c["uw"][:], c["u_sb"][:], c["mtop2"][:],
                                            op=ALU.mult)),
              waits=[W((P, "u", m))])
        ph.op(v, lambda e: e.drain())
        ph.op(v, lambda e: e.tensor_reduce(c["ssum"][:], c["uw"][:], AX.X, ALU.add))
        ph.op(v, lambda e: e.drain())
        ph.op(v, lambda e: e.reciprocal(c["rr"][:], c["ssum"][:]))
        ph.op(v, lambda e: e.drain())
        ph.op(v, (lambda e, m=m: e.tensor_scalar(
            c["wtok"][:, m * 8:(m + 1) * 8], c["uw"][:], c["rr"][:],
            None, op0=ALU.mult)))
        ph.op(v, lambda e: e.drain(), inc="s_rv", mark=(P, "w", m))

    ynorm(0)
    ynorm(1)
    for m in range(NT):
        for jg in range(4):
            kg = m * 4 + jg
            wl = [W((P, "tr", (m, jg)))]
            if kg >= 2:
                wl.append(W((P, "mm", kg - 2)))
            ph.op(v, (lambda e, kg=kg: e.tensor_copy(
                c["y1t"][:, (kg % 2) * 1024:(kg % 2 + 1) * 1024],
                tps[kg % 2][:, 0:1024])), waits=wl, inc="s_trv",
                mark=(P, "trv", kg))
        if m + 2 < NT:
            ynorm(m + 2)
        if m >= 4:
            chain(m - 4)
    for m in range(4, NT):
        chain(m)
    for m in range(NT):
        ph.op(v, (lambda e, m=m: e.tensor_copy(
            c["wT8"][0:NE, m * 128:(m + 1) * 128], c["wrp"][0:NE, 528:656])),
            waits=[W((P, "wtr", m))], inc="s_rv", mark=(P, "w8", m))
    ph.op(v, (lambda e: e.tensor_copy(c["wexp_sb"][:], c["wrp"][:, 0:1024])),
          waits=[W((P, "wexp"))])
    ph.op(v, lambda e: e.drain())
    ph.op(v, (lambda e: e.tensor_tensor(c["hw_sb"][:], c["hps"][:, 0:1024],
                                        c["wexp_sb"][:], op=ALU.mult)),
          inc="s_hw", mark=(P, "hw"))
    for p in range(NBP):
        m, n2 = p // 4, p % 4
        wl = [W((P, "b", 2 * p + 1))]
        if p >= 2:
            wl.append(W((P, "st", p - 2)))
        if call == 0 and p >= 1:
            _bn_pair(ph, c, P, p - 1)
            ph.op(v, lambda e: e.drain())
        ph.op(v, (lambda e, p=p, m=m, n2=n2: e.tensor_tensor(
            c["ostage"][:, (p % 2) * 1024:(p % 2 + 1) * 1024],
            bpt[p % 2][:, 0:1024],
            xfull[:, m * D + n2 * 1024:m * D + (n2 + 1) * 1024], op=ALU.add)),
            waits=wl)
        ph.op(v, lambda e: e.drain(), inc="s_bev", mark=(P, "bev", p))
    if call == 0:
        _bn_pair(ph, c, P, NBP - 1)

    for m in range(NT):
        for jg in range(4):
            kg = m * 4 + jg
            for jj in range(8):
                j = jg * 8 + jj
                wl = []
                if jj == 0:
                    if jg == 0:
                        wl.append(W((P, "y", m)))
                    if kg >= 2:
                        wl.append(W((P, "trv", kg - 2)))
                ph.op(t, (lambda e, m=m, j=j, jj=jj, kg=kg: e.transpose(
                    tps[kg % 2][:, jj * 128:(jj + 1) * 128],
                    c["ytm"][:, (m % 2) * D + j * 128:(m % 2) * D + (j + 1) * 128],
                    c["idb_sb"][:])), waits=wl,
                    inc=("s_tr" if jj == 7 else None),
                    mark=((P, "tr", (m, jg)) if jj == 7 else None))
            ph.op(t, None, waits=[W((P, "trv", kg))])
            for jj in range(8):
                j = jg * 8 + jj
                st_flag = (j == 0 and m % 4 == 0)
                ph.op(t, (lambda e, m=m, j=j, jj=jj, kg=kg, st_flag=st_flag: e.matmul(
                    c["hps"][:, m * 128:(m + 1) * 128],
                    c["at_sb"][:, j * ER:(j + 1) * ER],
                    c["y1t"][:, (kg % 2) * 1024 + jj * 128:(kg % 2) * 1024 + (jj + 1) * 128],
                    start=st_flag, stop=(j == NJ - 1), skip_group_check=True)))
                ph.op(t, (lambda e, m=m, j=j, jj=jj, kg=kg, st_flag=st_flag: e.matmul(
                    rps[m // 4][0:NE, (m % 4) * 128:(m % 4 + 1) * 128],
                    c["rw_sb"][:, j * NE:(j + 1) * NE],
                    c["y1t"][:, (kg % 2) * 1024 + jj * 128:(kg % 2) * 1024 + (jj + 1) * 128],
                    start=st_flag, stop=(j == NJ - 1), skip_group_check=True)),
                    inc=("s_mm" if jj == 7 else None),
                    mark=((P, "mm", kg) if jj == 7 else None))
        if m >= 4:
            _ltr(ph, c, P, m - 4)
    for m in range(4, NT):
        _ltr(ph, c, P, m)
    for m in range(NT):
        _wtr(ph, c, P, m)
    ph.op(t, (lambda e: e.matmul(c["wrp"][:, 0:512], c["e16_sb"][:],
                                 c["wT8"][0:NE, 0:512], start=True, stop=True)),
          waits=[W((P, "w8", NT - 1))])
    ph.op(t, (lambda e: e.matmul(c["wrp"][:, 512:1024], c["e16_sb"][:],
                                 c["wT8"][0:NE, 512:1024], start=True, stop=True)),
          inc="s_pst", mark=(P, "wexp"))
    for k in range(NT * 8):
        m, n = k // 8, k % 8
        wl = [W((P, "hw"))] if k == 0 else []
        if k >= 4:
            wl.append(W((P, "bev", k // 2 - 2)))
        ph.op(t, (lambda e, k=k, m=m, n=n: e.matmul(
            bps[k % 4],
            c["hw_sb"][:, m * 128:(m + 1) * 128],
            c["bfl_sb"][:, n * 512:(n + 1) * 512],
            start=True, stop=True)), waits=wl, inc="s_pst", mark=(P, "b", k))

    for m in range(NT):
        ph.op(sc, (lambda e, m=m: e.activation(
            c["u_sb"][:], c["ltok"][:], AF.Exp, bias=c["l1n"][:])),
            waits=[W((P, "l1n", m))], inc="s_ract", mark=(P, "u", m))

    for p in range(NBP):
        m, n2 = p // 4, p % 4
        ph.op(g, (lambda e, p=p, m=m, n2=n2: e.dma_start(
            sink[m * 128:(m + 1) * 128, n2 * 1024:(n2 + 1) * 1024],
            c["ostage"][:, (p % 2) * 1024:(p % 2 + 1) * 1024])),
            waits=[W((P, "bev", p))], inc=f"s_bdma{p % 2}", by=16,
            mark=(P, "st", p))
    ph.op(g, None, waits=[W((P, "st", NBP - 2)), W((P, "st", NBP - 1))])
    ph.emit()


def _bn_pair(ph, c, P, p):
    m, n2 = p // 4, p % 4
    ph.op("vector", (lambda e, p=p, m=m, n2=n2: e.bn_stats(
        c["stat63"][:, m * 48 + 2 * n2 * 6:m * 48 + (2 * n2 + 1) * 6],
        c["ostage"][:, (p % 2) * 1024:(p % 2) * 1024 + 512])))
    ph.op("vector", (lambda e, p=p, m=m, n2=n2: e.bn_stats(
        c["stat63"][:, m * 48 + (2 * n2 + 1) * 6:m * 48 + (2 * n2 + 2) * 6],
        c["ostage"][:, (p % 2) * 1024 + 512:(p % 2 + 1) * 1024])))


def _ltr(ph, c, P, m):
    ph.op("tensor", (lambda e, m=m: e.transpose(
        c["wrp"][:, 0:8], c["logits_sb"][0:NE, m * 128:(m + 1) * 128],
        c["idf_sb"][0:NE, 0:NE])),
        waits=[W((P, "lg", m))], inc="s_rt", mark=(P, "ltr", m))


def _wtr(ph, c, P, m):
    wl = [W((P, "w", m))]
    if m >= 1:
        wl.append(W((P, "w8", m - 1)))
    ph.op("tensor", (lambda e, m=m: e.transpose(
        c["wrp"][0:NE, 528:656], c["wtok"][:, m * 8:(m + 1) * 8],
        c["idf_sb"][:])),
        waits=wl, inc="s_rt", mark=(P, "wtr", m))


def emit_projections(c, st):
    nc, sem = c["nc"], c["sem"]
    E = st.enter_context
    y2T = E(nc.sbuf_tensor("p_y2T", [128, NJ * TOK], BF16))
    xring = E(nc.sbuf_tensor("p_xr", [128, 2 * D], F32))
    ytm = E(nc.sbuf_tensor("p_ytm", [128, 2 * D], BF16))
    wslab = E(nc.sbuf_tensor("p_ws", [128, 2 * D], BF16))
    kvst = E(nc.sbuf_tensor("p_kvst", [128, 2 * 1024], BF16))
    vtile = E(nc.sbuf_tensor("p_vt", [128, 1024], BF16))
    vstage = E(nc.sbuf_tensor("p_vs", [128, NT * 1032], BF16))
    tpsA = E(nc.psum_tensor("p_tpsA", [128, 1024], BF16))
    tpsB = E(nc.psum_tensor("p_tpsB", [128, 1024], BF16))
    pps = [E(nc.psum_tensor("p_pps0", [128, 1024], F32)),
           E(nc.psum_tensor("p_pps1", [128, 1024], F32))]
    tps = [tpsA, tpsB]
    P = "pj"
    ph = Phase(c, P)
    t, v, g, s = "tensor", "vector", "gpsimd", "sync"
    NPROJ = 48
    NK = NT * NJ

    def slab_src(i):
        if i < 8:
            return c["wk_d"][i, :, :]
        if i < 16:
            return c["wv_d"][i - 8, :, :]
        return c["wq_d"][i - 16, :, :]

    for m in range(NT):
        wl = [] if m < 2 else [W((P, "y", m - 2))]
        ph.op(s, (lambda e, m=m: e.dma_start(
            xring[:, (m % 2) * D:(m % 2 + 1) * D],
            c["x1d"][m * 128:(m + 1) * 128, :])), waits=wl, inc=f"s_x{m % 2}",
            by=16, mark=(P, "x", m))

    ph.op(v, (lambda e: e.memset(
        vstage[:].rearrange("p (h t f) -> p h t f", h=NKV, f=129)[:, :, :, 128:129],
        1.0)), inc="s_rv", mark=(P, "ones"))

    def ynorm(m):
        wl = [W((P, "x", m))]
        if m >= 2:
            wl.append(W((P, "tr", (m - 2, 3))))
        ph.op(v, (lambda e, m=m: e.tensor_scalar(
            ytm[:, (m % 2) * D:(m % 2 + 1) * D],
            xring[:, (m % 2) * D:(m % 2 + 1) * D],
            c["mv2"][:, 2 * m:2 * m + 1], c["rstd2"][:, m:m + 1],
            op0=ALU.subtract, op1=ALU.mult)), waits=wl, inc="s_y", mark=(P, "y", m))

    ynorm(0)
    ynorm(1)
    y2T_v = y2T[:].rearrange("p (j t) -> p j t", t=TOK)
    for m in range(NT):
        for jg in range(4):
            kg = m * 4 + jg
            ph.op(v, (lambda e, m=m, jg=jg, kg=kg: e.tensor_copy(
                y2T_v[:, jg * 8:(jg + 1) * 8, m * 128:(m + 1) * 128],
                tps[kg % 2][:].rearrange("p (a b) -> p a b", b=128))),
                waits=[W((P, "tr", (m, jg)))], inc="s_trv", mark=(P, "trv", kg))
        if m + 2 < NT:
            ynorm(m + 2)
    for i in range(NPROJ):
        if i < 8:
            wl = [W((P, "mm2", i))]
            if i >= 2:
                wl.append(W((P, "kst", i - 2)))
            ph.op(v, (lambda e, i=i: e.tensor_copy(
                kvst[:, (i % 2) * 1024:(i % 2 + 1) * 1024], pps[i % 2][:, 0:1024])),
                waits=wl, inc="s_rv", mark=(P, "ev", i))
        elif i < 16:
            vc = i - 8
            ph.op(v, (lambda e, i=i, vc=vc: e.tensor_scalar(
                vtile[:], pps[i % 2][:, 0:1024], c["bv_sb"][:, vc:vc + 1], None,
                op0=ALU.add)), waits=[W((P, "mm2", i))],
                inc="s_rv", mark=(P, "ev", i))
            kgv = 32 + vc
            vst_v = vstage[:].rearrange("p (h t f) -> p h t f", h=NKV, f=129)
            ph.op(v, (lambda e, vc=vc, kgv=kgv, vst_v=vst_v: e.tensor_copy(
                vst_v[:, vc, :, 0:128],
                tps[kgv % 2][:].rearrange("p (a b) -> p a b", b=128))),
                waits=[W((P, "tr", (8, vc)))], inc="s_trv", mark=(P, "trv", kgv))
        else:
            qi = i - 16
            wl = [W((P, "mm2", i))]
            if qi >= 2:
                wl.append(W((P, "qst", qi - 2)))
            else:
                wl.append(W((P, "kvdone")))
            ph.op(v, (lambda e, i=i: e.tensor_copy(
                kvst[:, (i % 2) * 1024:(i % 2 + 1) * 1024], pps[i % 2][:, 0:1024])),
                waits=wl, inc="s_rv", mark=(P, "ev", i))

    for m in range(NT):
        for jg in range(4):
            kg = m * 4 + jg
            for jj in range(8):
                j = jg * 8 + jj
                wl = []
                if jj == 0:
                    if jg == 0:
                        wl.append(W((P, "y", m)))
                    if kg >= 2:
                        wl.append(W((P, "trv", kg - 2)))
                ph.op(t, (lambda e, m=m, j=j, jj=jj, kg=kg: e.transpose(
                    tps[kg % 2][:, jj * 128:(jj + 1) * 128],
                    ytm[:, (m % 2) * D + j * 128:(m % 2) * D + (j + 1) * 128],
                    c["idb_sb"][:])), waits=wl,
                    inc=("s_tr" if jj == 7 else None),
                    mark=((P, "tr", (m, jg)) if jj == 7 else None))
    for i in range(NPROJ):
        wl = [W((P, "kv2", i))]
        if i == 0:
            wl.append(W((P, "trv", 31)))
        if i >= 2:
            wl.append(W((P, "ev", i - 2)))
        ph.op(t, None, waits=wl)
        for j in range(NJ):
            for hh in range(2):
                last = (j == NJ - 1 and hh == 1)
                ph.op(t, (lambda e, i=i, j=j, hh=hh: e.matmul(
                    pps[i % 2][:, hh * 512:(hh + 1) * 512],
                    wslab[:, (i % 2) * D + j * 128:(i % 2) * D + (j + 1) * 128],
                    y2T[:, j * TOK + hh * 512:j * TOK + (hh + 1) * 512],
                    start=(j == 0), stop=(j == NJ - 1), skip_group_check=True)),
                    inc=("s_mm2" if last else None),
                    mark=((P, "mm2", i) if last else None))
        if 8 <= i < 16:
            vc = i - 8
            kgv = 32 + vc
            for t_ in range(NT):
                wl = []
                if t_ == 0:
                    wl.append(W((P, "ev", i)))
                    if kgv >= 2:
                        wl.append(W((P, "trv", kgv - 2)))
                ph.op(t, (lambda e, t_=t_, kgv=kgv: e.transpose(
                    tps[kgv % 2][:, t_ * 128:(t_ + 1) * 128],
                    vtile[:, t_ * 128:(t_ + 1) * 128],
                    c["idb_sb"][:])), waits=wl,
                    inc=("s_tr" if t_ == NT - 1 else None),
                    mark=((P, "tr", (8, vc)) if t_ == NT - 1 else None))

    for i in range(NPROJ):
        wl = [] if i < 2 else [W((P, "mm2", i - 2))]
        ph.op(g, (lambda e, i=i: e.dma_start(
            wslab[:, (i % 2) * D:(i % 2 + 1) * D], slab_src(i),
            max_dma_last_dim=8192)), waits=wl, inc=f"s_kv2{i % 2}", by=16,
            mark=(P, "kv2", i))
    ph.op(g, (lambda e: e.collective_compute(
        "AllGather", ALU.bypass, replica_groups=[list(range(R))],
        ins=[c["ktd"].opt()], outs=[c["ktg"].opt()])),
        waits=[W((P, "kvdone"))],
        inc="s_cc", mark=(P, "cck"))
    ph.op(g, (lambda e: e.collective_compute(
        "AllGather", ALU.bypass, replica_groups=[list(range(R))],
        ins=[c["vtd"].opt()], outs=[c["vtg"].opt()])),
        inc="s_cc", mark=(P, "ccv"))

    for i in range(8):
        ph.op(s, (lambda e, i=i: e.dma_start(
            c["ktd"][i * 128:(i + 1) * 128, :],
            kvst[:, (i % 2) * 1024:(i % 2 + 1) * 1024])),
            waits=[W((P, "ev", i))], inc=f"s_kst{i % 2}", by=16,
            mark=(P, "kst", i))
    for t_ in range(NT):
        wl = [W((P, "trv", 39)), W((P, "ones"))]
        if t_ >= 2:
            wl.append(W((P, "vst", t_ - 2)))
        else:
            wl.append(W((P, "kst", 6 + t_)))
        ph.op(s, (lambda e, t_=t_: e.dma_start(
            c["vtd"][t_ * 128:(t_ + 1) * 128, :],
            vstage[:, t_ * 1032:(t_ + 1) * 1032])), waits=wl,
            inc=f"s_kst{t_ % 2}", by=16, mark=(P, "vst", t_))
    ph.op(s, None, waits=[W((P, "vst", NT - 2)), W((P, "vst", NT - 1))],
          inc="s_pj", mark=(P, "kvdone"))
    for qi in range(32):
        ph.op(s, (lambda e, qi=qi: e.dma_start(
            c["qtd"][qi, :, :], kvst[:, ((16 + qi) % 2) * 1024:((16 + qi) % 2 + 1) * 1024])),
            waits=[W((P, "ev", 16 + qi))], inc=f"s_kst{qi % 2}", by=16,
            mark=(P, "qst", qi))
    ph.op(s, None, waits=[W((P, "qst", 30)), W((P, "qst", 31))])
    ph.emit()


def emit_attention(c, st):
    nc, sem = c["nc"], c["sem"]
    E = st.enter_context
    kT = E(nc.sbuf_tensor("a_kT", [128, 2 * S], BF16))
    vh = E(nc.sbuf_tensor("a_vh", [128, 2 * NCH * 129], BF16))
    qT = E(nc.sbuf_tensor("a_qT", [128, 2 * G4 * TOK], BF16))
    ptil = E(nc.sbuf_tensor("a_ptil", [128, 3 * 1536], BF16))
    ksq = E(nc.sbuf_tensor("a_ksq", [128, S], BF16))
    qsq = E(nc.sbuf_tensor("a_qsq", [128, G4 * TOK], BF16))
    cstage = E(nc.sbuf_tensor("a_cs", [128, 4 * 128], BF16))
    negbh = E(nc.sbuf_tensor("a_nb", [128, G4], F32))
    nt1 = E(nc.sbuf_tensor("a_nt1", [128, 1], F32))
    ntk = E(nc.sbuf_tensor("a_ntk", [1, 1], F32))
    ntq = E(nc.sbuf_tensor("a_ntq", [1, 1], F32))
    nts = E(nc.sbuf_tensor("a_nts", [1, 1], F32))
    ntb = E(nc.sbuf_tensor("a_ntb", [1, 1], BF16))
    rcol = E(nc.sbuf_tensor("a_rc", [128, 4], F32))
    stA = E(nc.psum_tensor("a_stA", [128, 1536], F32))
    stB = E(nc.psum_tensor("a_stB", [128, 1536], F32))
    av01 = E(nc.psum_tensor("a_av01", [128, 512], F32))
    av23 = E(nc.psum_tensor("a_av23", [128, 512], F32))
    stX = [stA, stB]
    avX = [av01, av01, av23, av23]
    avo = [0, 129, 0, 129]
    P = "at"
    ph = Phase(c, P)
    t, v, sc, g, s = "tensor", "vector", "scalar", "gpsimd", "sync"
    BSC = ATT_SCALE * BH_MARGIN * 0.5

    GSIZES = [3] * 20 + [2, 2]
    GPE = len(GSIZES)
    GOFF = [0]
    for z in GSIZES:
        GOFF.append(GOFF[-1] + z)

    def ep(h, gq, s_):
        return (h * G4 + gq) * 2 + s_

    for h in range(R):
        par = h % 2
        wl = [W(("pj", "ccv"))]
        if h >= 2:
            wl.append(W((P, "avm", (ep(h - 2, G4 - 1, 1) + 1) * GPE - 1)))
        ph.op(s, None, waits=wl)
        for r in range(R):
            ph.op(s, (lambda e, h=h, r=r, par=par: e.dma_start(
                kT[:, par * S + r * TOK:par * S + (r + 1) * TOK],
                c["ktg"][r * NKV * 128 + h * 128:r * NKV * 128 + (h + 1) * 128, :])),
                inc=f"s_kv{h % 2}", by=16)
        for r in range(R):
            ph.op(s, (lambda e, h=h, r=r, par=par: e.dma_start(
                vh[:, par * NCH * 129 + r * 8 * 129:
                   par * NCH * 129 + (r + 1) * 8 * 129],
                c["vtg"][r * TOK + h * 128:r * TOK + (h + 1) * 128, :])),
                inc=f"s_kv{h % 2}", by=16)
        for gq in range(G4):
            mk = (P, "kv", h) if gq == G4 - 1 else None
            ph.op(s, (lambda e, h=h, gq=gq, par=par: e.dma_start(
                qT[:, par * G4 * TOK + gq * TOK:par * G4 * TOK + (gq + 1) * TOK],
                c["qtd"][h * G4 + gq, :, :])), inc=f"s_kv{h % 2}", by=16, mark=mk)

    for n in range(8):
        wl = [] if n < 2 else [W((P, "wocast", n - 2))]
        ph.op(g, (lambda e, n=n: e.dma_start(
            c["wo_bf"][n], c["wo_d"][n], max_dma_last_dim=8192)), waits=wl,
            inc=f"s_xn{n % 2}", by=16, mark=(P, "wocast", n))

    avq = []

    def _av_pop(desc):
        i, gi, par, gh, s_ = desc
        Gi = i * GPE + gi
        sz = GSIZES[gi]
        wl = [W((P, "exp", Gi))]
        if gi == 0 and i >= 1:
            wl.append(W((P, "ctv", i - 1)))
        nmm = 0
        tot = sz * 4
        for cpos in range(sz):
            ch = GOFF[gi] + cpos
            for q in range(4):
                st_flag = (gi == 0 and cpos == 0 and (q % 2 == 0))
                nmm += 1
                ph.op(t, (lambda e, Gi=Gi, ch=ch, cpos=cpos, q=q, par=par,
                          st_flag=st_flag: e.matmul(
                    avX[q][:, avo[q]:avo[q] + 129],
                    ptil[:, (Gi % 3) * 1536 + cpos * 512 + q * 128:
                         (Gi % 3) * 1536 + cpos * 512 + (q + 1) * 128],
                    vh[:, par * NCH * 129 + ch * 129:par * NCH * 129 + (ch + 1) * 129],
                    start=st_flag, stop=False, skip_group_check=True)),
                    waits=wl if (cpos == 0 and q == 0) else (),
                    inc=("s_avm" if nmm == tot else None),
                    mark=((P, "avm", Gi) if nmm == tot else None))
        if gi == GPE - 1:
            trt = av01[:].bitcast(BF16)
            for q in range(4):
                ph.op(t, (lambda e, q=q, trt=trt: e.transpose(
                    trt[:, q * 128:(q + 1) * 128],
                    cstage[:, q * 128:(q + 1) * 128],
                    c["idb_sb"][:])),
                    waits=[W((P, "cs", i))] if q == 0 else (),
                    inc=("s_ctr" if q == 3 else None),
                    mark=((P, "ctr", i) if q == 3 else None))

    def squares(h):
        par = h % 2
        wl = [W((P, "kv", h))]
        if h >= 1:
            wl.append(W((P, "norm", h - 1)))
        ph.op(v, (lambda e, par=par: e.tensor_tensor(
            ksq[:], kT[:, par * S:(par + 1) * S], kT[:, par * S:(par + 1) * S],
            op=ALU.mult)), waits=wl)
        ph.op(v, (lambda e, par=par: e.tensor_tensor(
            qsq[:], qT[:, par * G4 * TOK:(par + 1) * G4 * TOK],
            qT[:, par * G4 * TOK:(par + 1) * G4 * TOK], op=ALU.mult)),
            inc="s_sq", mark=(P, "sq", h))

    squares(0)
    for h in range(R):
        par = h % 2
        gh0 = h * G4
        while avq:
            _av_pop(avq.pop(0))
        wl = [W((P, "sq", h))]
        if h >= 1:
            lastg = (ep(h - 1, G4 - 1, 1) + 1) * GPE - 1
            wl.append(W((P, "exp", lastg)))
            wl.append(W((P, "nb", (h - 1, G4 - 1))))
        ph.op(t, None, waits=wl)
        for cc in range(NCH):
            ph.op(t, (lambda e, cc=cc, first=(cc == 0): e.matmul(
                stA[:, cc:cc + 1], ksq[:, cc * 128:(cc + 1) * 128],
                c["ones_col"][:], start=first, stop=False, skip_group_check=True)))
        for gq in range(G4):
            for t_ in range(NT):
                last = (gq == G4 - 1 and t_ == NT - 1)
                ph.op(t, (lambda e, gq=gq, t_=t_: e.matmul(
                    stA[:, 64 + gq * NT + t_:64 + gq * NT + t_ + 1],
                    qsq[:, gq * TOK + t_ * 128:gq * TOK + (t_ + 1) * 128],
                    c["ones_col"][:], start=False, stop=False,
                    skip_group_check=True)),
                    inc=("s_st" if last else None),
                    mark=((P, "norm", h) if last else None))
        ph.op(v, (lambda e: e.tensor_reduce(nt1[:], stA[:, 0:64], AX.X, ALU.max)),
              waits=[W((P, "norm", h))])
        ph.op(v, lambda e: e.drain(), inc="s_sq", mark=(P, "kcol", h))
        ph.op(t, (lambda e: e.transpose(stA[0:1, 96:224], nt1[:], c["idf_sb"][:])),
              waits=[W((P, "kcol", h))], inc="s_st", mark=(P, "ktr", h))
        ph.op(v, (lambda e: e.tensor_reduce(ntk[0:1, 0:1], stA[0:1, 96:224],
                                            AX.X, ALU.max)),
              waits=[W((P, "ktr", h))])
        ph.op(v, lambda e: e.drain())
        for gq in range(G4):
            ph.op(v, (lambda e, gq=gq: e.tensor_reduce(
                nt1[:], stA[:, 64 + gq * NT:64 + (gq + 1) * NT], AX.X, ALU.max)))
            ph.op(v, lambda e: e.drain(), inc="s_sq", mark=(P, "qcol", (h, gq)))
            ph.op(t, (lambda e: e.transpose(stA[0:1, 224:352], nt1[:],
                                            c["idf_sb"][:])),
                  waits=[W((P, "qcol", (h, gq)))], inc="s_st",
                  mark=(P, "qtr", (h, gq)))
            ph.op(v, (lambda e: e.tensor_reduce(ntq[0:1, 0:1], stA[0:1, 224:352],
                                                AX.X, ALU.max)),
                  waits=[W((P, "qtr", (h, gq)))])
            ph.op(v, lambda e: e.drain())
            ph.op(v, (lambda e: e.tensor_tensor(nts[0:1, 0:1], ntq[0:1, 0:1],
                                                ntk[0:1, 0:1], op=ALU.add)))
            ph.op(v, lambda e: e.drain())
            ph.op(v, (lambda e: e.tensor_scalar(ntb[0:1, 0:1], nts[0:1, 0:1],
                                                BSC, None, op0=ALU.mult)))
            ph.op(v, lambda e: e.drain(), inc="s_sq", mark=(P, "ntb", (h, gq)))
            ph.op(t, (lambda e, gq=gq: e.matmul(
                stA[:, 480 + gq:481 + gq], c["ones_row"][:], ntb[0:1, 0:1],
                start=False, stop=False, skip_group_check=True)),
                waits=[W((P, "ntb", (h, gq)))], inc="s_st",
                mark=(P, "bc", (h, gq)))
            ph.op(v, (lambda e, gq=gq: e.tensor_scalar(
                negbh[:, gq:gq + 1], stA[:, 480 + gq:481 + gq], -1.0, None,
                op0=ALU.mult)),
                waits=[W((P, "bc", (h, gq)))])
            ph.op(v, lambda e: e.drain(), inc="s_nb", mark=(P, "nb", (h, gq)))

        for gq in range(G4):
            for s_ in range(2):
                i = ep(h, gq, s_)
                for gi in range(GPE):
                    Gi = i * GPE + gi
                    sz = GSIZES[gi]
                    wl = []
                    if Gi >= 2:
                        wl.append(W((P, "exp", Gi - 2)))
                    if gi == 0:
                        wl.append(W((P, "nb", (h, G4 - 1))))
                        wl.append(W((P, "kv", h)))
                    for cpos in range(sz):
                        ch = GOFF[gi] + cpos
                        last = (cpos == sz - 1)
                        ph.op(t, (lambda e, Gi=Gi, ch=ch, cpos=cpos, par=par,
                                  gq=gq, s_=s_: e.matmul(
                            stX[Gi % 2][:, cpos * 512:(cpos + 1) * 512],
                            kT[:, par * S + ch * 128:par * S + (ch + 1) * 128],
                            qT[:, par * G4 * TOK + gq * TOK + s_ * 512:
                               par * G4 * TOK + gq * TOK + s_ * 512 + 512],
                            start=True, stop=True)),
                            waits=wl if cpos == 0 else (),
                            inc=("s_st" if last else None),
                            mark=((P, "stg", Gi) if last else None))
                    avq.append((i, gi, par, gh0 + gq, s_))
                    if len(avq) > 2:
                        _av_pop(avq.pop(0))
                ph.op(v, lambda e: e.drain(),
                      waits=([W((P, "avm", (i + 1) * GPE - 1))] +
                             ([W((P, "ctr", i - 1))] if i >= 1 else [])))
                ph.op(v, lambda e: e.reciprocal(rcol[:, 0:1], av01[:, 128:129]))
                ph.op(v, lambda e: e.reciprocal(rcol[:, 1:2], av01[:, 257:258]))
                ph.op(v, lambda e: e.reciprocal(rcol[:, 2:3], av23[:, 128:129]))
                ph.op(v, lambda e: e.reciprocal(rcol[:, 3:4], av23[:, 257:258]))
                ph.op(v, lambda e: e.drain())
                for q in range(4):
                    ph.op(v, (lambda e, q=q: e.tensor_scalar(
                        cstage[:, q * 128:(q + 1) * 128],
                        avX[q][:, avo[q]:avo[q] + 128], rcol[:, q:q + 1], None,
                        op0=ALU.mult)),
                        inc=("s_evac" if q == 3 else None),
                        mark=((P, "cs", i) if q == 3 else None))
                gh = gh0 + gq
                ph.op(v, (lambda e, gh=gh, s_=s_: e.tensor_copy(
                    c["ctxT"][:, gh * TOK + s_ * 512:gh * TOK + (s_ + 1) * 512],
                    av01[:].bitcast(BF16)[:, 0:512])),
                    waits=[W((P, "ctr", i))], inc="s_ctv", mark=(P, "ctv", i))
                if gq == G4 - 1 and s_ == 0 and h + 1 < R:
                    squares(h + 1)

        for gq in range(G4):
            for s_ in range(2):
                i = ep(h, gq, s_)
                for gi in range(GPE):
                    Gi = i * GPE + gi
                    sz = GSIZES[gi]
                    wl = [W((P, "stg", Gi))]
                    if Gi >= 3:
                        wl.append(W((P, "avm", Gi - 3)))
                    if gi == 0:
                        wl.append(W((P, "nb", (h, gq))))
                    ph.op(sc, (lambda e, Gi=Gi, gq=gq, sz=sz: e.activation(
                        ptil[:, (Gi % 3) * 1536:(Gi % 3) * 1536 + sz * 512],
                        stX[Gi % 2][:, 0:sz * 512], AF.Exp,
                        bias=negbh[:, gq:gq + 1], scale=ATT_SCALE)),
                        waits=wl, inc="s_exp", mark=(P, "exp", Gi))
    while avq:
        _av_pop(avq.pop(0))
    ph.emit()



def emit_oproj(c, st):
    nc, sem = c["nc"], c["sem"]
    E = st.enter_context
    woslab = E(nc.sbuf_tensor("o_ws", [128, 2 * NJ * 512], BF16))
    x1ring = E(nc.sbuf_tensor("o_x1", [128, 2 * 512], F32))
    x2st = E(nc.sbuf_tensor("o_x2", [128, 2 * 512], F32))
    ops = [E(nc.psum_tensor("o_op0", [128, 512], F32)),
           E(nc.psum_tensor("o_op1", [128, 512], F32))]
    P = "op"
    ph = Phase(c, P)
    t, v, g, s = "tensor", "vector", "gpsimd", "sync"
    NB = 64

    for n in range(8):
        wl = [W(("at", "wocast", n))]
        if n >= 2:
            wl.append(W((P, "mm", (n - 2) * 8 + 7)))
        ph.op("scalar", (lambda e, n=n: e.dma_start(
            woslab[:, (n % 2) * NJ * 512:(n % 2 + 1) * NJ * 512]
            .rearrange("p (j f) -> p j f", f=512),
            c["wo_bf"][n, :, :, :].rearrange("j pp f -> pp j f"))), waits=wl,
            inc=f"s_wo{n % 2}", by=16, mark=(P, "wo", n))

    for k in range(NB + 2):
        if k < NB:
            n, m = k // 8, k % 8
            wl = [] if k < 2 else [W((P, "bev", k - 2))]
            ph.op(s, (lambda e, n=n, m=m, k=k: e.dma_start(
                x1ring[:, (k % 2) * 512:(k % 2 + 1) * 512],
                c["x1d"][m * 128:(m + 1) * 128, n * 512:(n + 1) * 512])),
                waits=wl, inc=f"s_xa{k % 2}", by=16, mark=(P, "x1", k))
        if k >= 2:
            k2 = k - 2
            n2, m2 = k2 // 8, k2 % 8
            ph.op(s, (lambda e, n2=n2, m2=m2, k2=k2: e.dma_start(
                c["x2d"][m2 * 128:(m2 + 1) * 128, n2 * 512:(n2 + 1) * 512],
                x2st[:, (k2 % 2) * 512:(k2 % 2 + 1) * 512])),
                waits=[W((P, "bev", k2))], inc=f"s_bd2{k2 % 2}", by=16,
                mark=(P, "st", k2))
    ph.op(s, None, waits=[W((P, "st", NB - 2)), W((P, "st", NB - 1))])

    for n in range(8):
        for m in range(8):
            k = n * 8 + m
            wl = [W((P, "wo", n))] if m == 0 else []
            if k >= 2:
                wl.append(W((P, "bev", k - 2)))
            ph.op(t, None, waits=wl)
            for j in range(NJ):
                last = (j == NJ - 1)
                ph.op(t, (lambda e, n=n, m=m, j=j, k=k: e.matmul(
                    ops[k % 2][:, 0:512],
                    c["ctxT"][:, j * TOK + m * 128:j * TOK + (m + 1) * 128],
                    woslab[:, (n % 2) * NJ * 512 + j * 512:
                           (n % 2) * NJ * 512 + (j + 1) * 512],
                    start=(j == 0), stop=(j == NJ - 1))),
                    inc=("s_pst" if last else None),
                    mark=((P, "mm", k) if last else None))

    for k in range(NB):
        n, m = k // 8, k % 8
        wl = [W((P, "mm", k)), W((P, "x1", k))]
        if k >= 2:
            wl.append(W((P, "st", k - 2)))
        ph.op(v, (lambda e, k=k: e.tensor_tensor(
            x2st[:, (k % 2) * 512:(k % 2 + 1) * 512], ops[k % 2][:, 0:512],
            x1ring[:, (k % 2) * 512:(k % 2 + 1) * 512], op=ALU.add)), waits=wl)
        ph.op(v, lambda e: e.drain())
        ph.op(v, (lambda e, n=n, m=m, k=k: e.bn_stats(
            c["stat63"][:, m * 48 + n * 6:m * 48 + (n + 1) * 6],
            x2st[:, (k % 2) * 512:(k % 2 + 1) * 512])), inc="s_bev",
            mark=(P, "bev", k))
    ph.emit()


def emit_debug_stats(c):
    ph = Phase(c, "dbgs")
    k0 = c["cnt"]["s_dbg"]
    ph.op("gpsimd", (lambda e: e.dma_start(c["dbg_mv"], c["mv1"][:])),
          inc="s_dbg", by=16)
    ph.op("gpsimd", (lambda e: e.dma_start(c["dbg_rs"], c["rstd1"][:])),
          inc="s_dbg", by=16)
    ph.op("gpsimd", (lambda e: e.dma_start(c["dbg_ve"], c["veps"][:])),
          inc="s_dbg", by=16)
    ph.op("gpsimd", None, waits=[WV("s_dbg", c["cnt"]["s_dbg"])])
    ph.emit()


def emit_debug_copy(c, src, dst, nbytes):
    ph = Phase(c, "dbg")
    ph.op("gpsimd", (lambda e: e.dma_start(dst, src)), inc="s_dbg", by=16)
    ph.op("gpsimd", None, waits=[WV("s_dbg", c["cnt"]["s_dbg"])])
    ph.emit()


DBG_N = [0]


def emit_debug_cast(c, src_bf, dst_f32, rows, cols):
    nc = c["nc"]
    ph = Phase(c, "dbgc")
    DBG_N[0] += 1
    with ExitStack() as st:
        buf = st.enter_context(nc.sbuf_tensor(f"dbgbuf{DBG_N[0]}", [128, cols], F32))
        for rr in range(rows // 128):
            k0 = c["cnt"]["s_dbg"]
            ph.op("gpsimd", (lambda e, rr=rr: e.dma_start(
                buf[:], src_bf[rr * 128:(rr + 1) * 128, :], max_dma_last_dim=2048)),
                inc="s_dbg", by=16)
            ph.op("gpsimd", None, waits=[WV("s_dbg", k0 + 16)])
            ph.op("gpsimd", (lambda e, rr=rr: e.dma_start(
                dst_f32[rr * 128:(rr + 1) * 128, :], buf[:])), inc="s_dbg", by=16)
            k1 = c["cnt"]["s_dbg"]
            ph.op("gpsimd", None, waits=[WV("s_dbg", k1)])
        ph.emit()


def emit_debug_ctx(c):
    nc = c["nc"]
    ph = Phase(c, "dbgx")
    with ExitStack() as st:
        buf = st.enter_context(nc.sbuf_tensor("dbgbufx", [128, TOK], F32))
        for j in range(NJ):
            k0 = c["cnt"]["s_dbg"]
            ph.op("gpsimd", (lambda e, j=j: e.tensor_copy(
                buf[:], c["ctxT"][:, j * TOK:(j + 1) * TOK])))
            ph.op("gpsimd", (lambda e, j=j: e.dma_start(
                c["dbg_ctx"][j * 128:(j + 1) * 128, :], buf[:])), inc="s_dbg", by=16)
            ph.op("gpsimd", None, waits=[WV("s_dbg", k0 + 16)])
        ph.emit()


def kernel(**inputs):
    img = host_images(inputs)
    X = np.asarray(inputs["L0_attn_hidden_states"], np.float32).reshape(S, D)

    nc = build_nc(debug=False)
    in_maps = []
    for cid in range(R):
        in_maps.append({
            "x": np.ascontiguousarray(X[cid * TOK:(cid + 1) * TOK, :]),
            "wq": img["wq"], "wk": img["wk"], "wv": img["wv"], "wo": img["wo"],
            "at": img["at"], "bfl": img["bfl"], "rw": img["rw"], "rb": img["rb"],
            "bvc": img["bvc"], "idf": img["idf"], "idb": img["idb"],
            "e16": img["e16"],
        })
    res = run_bass_kernel_spmd(nc, in_maps, core_ids=list(range(R)))
    out = np.concatenate([res.results[i]["out"] for i in range(R)], axis=0)
    return out.reshape(1, S, D)


# version 12
# speedup vs baseline: 1.0565x; 1.0565x over previous
import numpy as np
import ml_dtypes
import concourse.bass as bass
import concourse.mybir as mybir
from concourse.bass_utils import run_bass_kernel_spmd
from contextlib import ExitStack

F32 = mybir.dt.float32
BF16 = mybir.dt.bfloat16
AF = mybir.ActivationFunctionType
ALU = mybir.AluOpType
AX = mybir.AxisListType

R = 8
S = 8192
D = 4096
TOK = S // R
NT = TOK // 128
NJ = D // 128
NQH, NKV, G4, HD = 32, 8, 4, 128
EPS = 1e-5
ATT_SCALE = float(HD) ** -0.5
BH_MARGIN = 1.04
NE, RK = 8, 16
ER = NE * RK
NCH = S // 128
NGRP = NCH // 2
NEPOCH = R * G4 * 2
TOTG = NEPOCH * NGRP
BF = ml_dtypes.bfloat16
ENGINES = ["tensor", "vector", "scalar", "gpsimd", "sync"]


def _lhsT_image(W):
    dout, din = W.shape
    OC, J = dout // 128, din // 128
    W4 = W.reshape(OC, 128, J, 128)
    return np.ascontiguousarray(W4.transpose(0, 3, 2, 1)).reshape(OC, 128, J * 128)


def host_images(inp):
    Wq = np.asarray(inp["L0_attn_Wq"], np.float32)
    Wk = np.asarray(inp["L0_attn_Wk"], np.float32)
    Wv = np.asarray(inp["L0_attn_Wv"], np.float32)
    Wo = np.asarray(inp["L0_attn_Wo"], np.float32)
    bv = np.asarray(inp["L0_attn_bv"], np.float32)
    rw = np.asarray(inp["L0_ffn_router_w"], np.float32)
    rb = np.asarray(inp["L0_ffn_router_b"], np.float32)
    A = np.asarray(inp["L0_ffn_A"], np.float32)
    Bm = np.asarray(inp["L0_ffn_Bm"], np.float32)

    img = {}
    img["wq"] = _lhsT_image(Wq)
    img["wk"] = _lhsT_image(Wk)
    img["wv"] = _lhsT_image(Wv)
    Wo4 = Wo.reshape(8, 512, NJ, 128)
    img["wo"] = np.ascontiguousarray(Wo4.transpose(0, 2, 3, 1))
    Aflat = A.reshape(ER, D)
    img["at"] = np.ascontiguousarray(
        Aflat.reshape(ER, NJ, 128).transpose(2, 1, 0)).reshape(128, NJ * ER)
    img["bfl"] = np.ascontiguousarray(Bm.transpose(0, 2, 1)).reshape(ER, D)
    img["rw"] = np.ascontiguousarray(
        rw.reshape(NE, NJ, 128).transpose(2, 1, 0)).reshape(128, NJ * NE)
    img["rb"] = np.ascontiguousarray(rb.reshape(NE, 1))
    img["bvc"] = np.ascontiguousarray(bv.reshape(NKV, 128).T)
    img["idf"] = np.eye(128, dtype=np.float32)
    img["idb"] = np.eye(128, dtype=np.float32).astype(BF)
    e16 = np.zeros((NE, 128), np.float32)
    for e in range(NE):
        e16[e, e * RK:(e + 1) * RK] = 0.25
    img["e16"] = e16
    return img


class Phase:

    def __init__(self, c, name):
        self.c = c
        self.name = name
        self.ev = {e: [] for e in ENGINES}

    def op(self, eng, build=None, waits=(), inc=None, by=1, mark=None):
        c = self.c
        if inc is not None:
            c["cnt"][inc] += by
            if mark is not None:
                c["marks"][mark] = (inc, c["cnt"][inc])
        self.ev[eng].append((tuple(waits), build, inc, by))

    def emit(self):
        c = self.c
        nc, sem, marks = c["nc"], c["sem"], c["marks"]
        with nc.Block() as blk:
            for ename in ENGINES:
                evs = self.ev[ename]
                if not evs:
                    continue

                def make(evs=evs):
                    def f(eng):
                        for waits, build, incname, by in evs:
                            for w in waits:
                                kind, a, b_ = w
                                if kind == "abs":
                                    sname, v = a, b_
                                else:
                                    sname, mv = marks[a]
                                    v = mv + b_
                                if v > 0:
                                    eng.wait_ge(sem[sname], v)
                            instr = build(eng) if build is not None else None
                            if incname is not None:
                                tgt = instr if instr is not None else eng.nop()
                                tgt.then_inc(sem[incname], by)
                    return f
                getattr(blk, ename)(make())


def W(mark, delta=0):
    return ("mark", mark, delta)


def WV(sname, val):
    return ("abs", sname, val)


def build_nc(debug=False):
    nc = bass.Bass("TRN2", target_bir_lowering=False, debug=False)
    c = {"nc": nc, "cnt": {}, "marks": {}, "debug": debug}

    def dram(name, shape, dtype, kind=None):
        if kind:
            return nc.dram_tensor(name, shape, dtype, kind=kind).ap()
        return nc.dram_tensor(name, shape, dtype).ap()

    c["x_in"] = dram("x", [TOK, D], F32, "ExternalInput")
    c["wq_d"] = dram("wq", [NJ, 128, D], F32, "ExternalInput")
    c["wk_d"] = dram("wk", [NKV, 128, D], F32, "ExternalInput")
    c["wv_d"] = dram("wv", [NKV, 128, D], F32, "ExternalInput")
    c["wo_d"] = dram("wo", [8, NJ, 128, 512], F32, "ExternalInput")
    c["at_d"] = dram("at", [128, NJ * ER], F32, "ExternalInput")
    c["bfl_d"] = dram("bfl", [ER, D], F32, "ExternalInput")
    c["rw_d"] = dram("rw", [128, NJ * NE], F32, "ExternalInput")
    c["rb_d"] = dram("rb", [NE, 1], F32, "ExternalInput")
    c["bvc_d"] = dram("bvc", [128, NKV], F32, "ExternalInput")
    c["idf_d"] = dram("idf", [128, 128], F32, "ExternalInput")
    c["idb_d"] = dram("idb", [128, 128], BF16, "ExternalInput")
    c["e16_d"] = dram("e16", [NE, 128], F32, "ExternalInput")
    c["out_d"] = dram("out", [TOK, D], F32, "ExternalOutput")

    c["x1d"] = dram("x1d", [TOK, D], F32)
    c["x2d"] = dram("x2d", [TOK, D], F32)
    c["qtd"] = dram("qtd", [NQH, 128, TOK], BF16)
    c["ktd"] = dram("ktd", [NKV * 128, TOK], BF16)
    c["vtd"] = dram("vtd", [TOK, NKV * 129], BF16)
    c["ktg"] = dram("ktg", [R * NKV * 128, TOK], BF16)
    c["vtg"] = dram("vtg", [R * TOK, NKV * 129], BF16)
    c["wo_bf"] = dram("wo_bf", [8, NJ, 128, 512], BF16)
    if debug:
        c["dbg_x1"] = dram("dbg_x1", [TOK, D], F32, "ExternalOutput")
        c["dbg_x2"] = dram("dbg_x2", [TOK, D], F32, "ExternalOutput")
        c["dbg_ctx"] = dram("dbg_ctx", [NJ * 128, TOK], F32, "ExternalOutput")
        c["dbg_kt"] = dram("dbg_kt", [R * NKV * 128, TOK], F32, "ExternalOutput")
        c["dbg_vt"] = dram("dbg_vt", [R * TOK, NKV * 129], F32, "ExternalOutput")
        c["dbg_qt"] = dram("dbg_qt", [NQH * 128, TOK], F32, "ExternalOutput")
        c["dbg_mv"] = dram("dbg_mv", [128, 2 * NT], F32, "ExternalOutput")
        c["dbg_rs"] = dram("dbg_rs", [128, NT], F32, "ExternalOutput")
        c["dbg_ve"] = dram("dbg_ve", [128, NT], F32, "ExternalOutput")

    with ExitStack() as ST:
        E = ST.enter_context
        c["at_sb"] = E(nc.sbuf_tensor("at_sb", [128, NJ * ER], BF16))
        c["bfl_sb"] = E(nc.sbuf_tensor("bfl_sb", [ER, D], BF16))
        c["rw_sb"] = E(nc.sbuf_tensor("rw_sb", [128, NJ * NE], BF16))
        c["rb_sb"] = E(nc.sbuf_tensor("rb_sb", [NE, 1], F32))
        c["bv_sb"] = E(nc.sbuf_tensor("bv_sb", [128, NKV], F32))
        c["idf_sb"] = E(nc.sbuf_tensor("idf_sb", [128, 128], F32))
        c["idb_sb"] = E(nc.sbuf_tensor("idb_sb", [128, 128], BF16))
        c["e16_sb"] = E(nc.sbuf_tensor("e16_sb", [NE, 128], F32))
        c["ones_row"] = E(nc.sbuf_tensor("ones_row", [1, 128], BF16))
        c["ones_col"] = E(nc.sbuf_tensor("ones_col", [128, 1], BF16))
        for i in (1, 2, 3):
            c[f"mv{i}"] = E(nc.sbuf_tensor(f"mv{i}", [128, 2 * NT], F32))
            c[f"rstd{i}"] = E(nc.sbuf_tensor(f"rstd{i}", [128, NT], F32))
        c["stat6"] = E(nc.sbuf_tensor("stat6", [128, 48], F32))
        c["stat63"] = E(nc.sbuf_tensor("stat63", [128, NT * 48], F32))
        c["veps"] = E(nc.sbuf_tensor("veps", [128, NT], F32))
        c["lnv"] = E(nc.sbuf_tensor("lnv", [128, NT], F32))

        names = ["s_w", "s_x0", "s_x1", "s_st6", "s_y", "s_tr", "s_trv", "s_mm",
                 "s_rt", "s_rv", "s_ract", "s_hw", "s_pst", "s_bev",
                 "s_xn0", "s_xn1", "s_bdma0", "s_bdma1",
                 "s_rst", "s_cc", "s_kv0", "s_kv1", "s_st", "s_exp", "s_avm",
                 "s_evac", "s_ctr", "s_ctv", "s_sq", "s_nb",
                 "s_kst0", "s_kst1", "s_mm2", "s_kv20", "s_kv21",
                 "s_wo0", "s_wo1", "s_ph", "s_dbg", "s_gw",
                 "s_bd20", "s_bd21", "s_xa0", "s_xa1", "s_pj"]
        c["sem"] = {n: E(nc.semaphore(n)) for n in names}
        for n in names:
            c["cnt"][n] = 0

        nph = [0]

        def barrier():
            nph[0] += 1
            tgt = 5 * nph[0]
            semh = c["sem"]["s_ph"]
            with nc.Block() as blk:
                for ename in ENGINES:
                    def mk(eng):
                        eng.drain().then_inc(semh, 1)
                        eng.wait_ge(semh, tgt)
                    getattr(blk, ename)(mk)

        with ExitStack() as s1:
            c["xfull"] = s1.enter_context(
                nc.sbuf_tensor("xfull1", [128, NT * D], F32))
            with nc.named_scope("ln1"):
                emit_prologue(c)
                emit_ln1(c)
                barrier()
            if debug:
                emit_debug_stats(c)
                barrier()
            alloc_ffn(c, s1)
            with nc.named_scope("ffn1"):
                emit_ffn(c, call=0)
        if debug:
            barrier()
            emit_debug_copy(c, c["x1d"], c["dbg_x1"], TOK * D * 4)
            barrier()
        with nc.named_scope("lnf2"):
            emit_ln_finish(c, idx=2)
            barrier()
        with ExitStack() as s2:
            with nc.named_scope("proj"):
                emit_projections(c, s2)
                barrier()
        if debug:
            emit_debug_cast(c, c["ktg"], c["dbg_kt"], R * NKV * 128, TOK)
            emit_debug_cast(c, c["vtg"], c["dbg_vt"], R * TOK, NKV * 129)
            emit_debug_cast(c, c["qtd"].rearrange("a p t -> (a p) t"), c["dbg_qt"],
                            NQH * 128, TOK)
            barrier()
        with ExitStack() as s3:
            c["ctxT"] = s3.enter_context(nc.sbuf_tensor("ctxT", [128, NJ * TOK], BF16))
            with ExitStack() as s3a:
                with nc.named_scope("attn"):
                    emit_attention(c, s3a)
                    barrier()
            if debug:
                emit_debug_ctx(c)
                barrier()
            with ExitStack() as s3b:
                with nc.named_scope("oproj"):
                    emit_oproj(c, s3b)
        if debug:
            barrier()
            emit_debug_copy(c, c["x2d"], c["dbg_x2"], TOK * D * 4)
            barrier()
        with nc.named_scope("lnf3"):
            emit_ln_finish(c, idx=3)
            barrier()
        with ExitStack() as s4:
            c["xfull"] = s4.enter_context(
                nc.sbuf_tensor("xfull2", [128, NT * D], F32))
            alloc_ffn(c, s4, tag="z")
            with nc.named_scope("ffn2"):
                emit_ffn(c, call=1)
    return nc


def emit_prologue(c):
    nc, sem = c["nc"], c["sem"]
    ph = Phase(c, "pro")
    g, s, v, sc = "gpsimd", "sync", "vector", "scalar"
    ph.op(g, lambda e: e.dma_start(c["at_sb"][:], c["at_d"], max_dma_last_dim=8192),
          inc="s_gw", by=16)
    ph.op(g, lambda e: e.dma_start(c["bfl_sb"][:], c["bfl_d"], max_dma_last_dim=8192),
          inc="s_gw", by=16)
    ph.op(g, lambda e: e.dma_start(c["rw_sb"][:], c["rw_d"], max_dma_last_dim=8192),
          inc="s_gw", by=16, mark="gall")
    ph.op(s, lambda e: e.dma_start(c["rb_sb"][:], c["rb_d"]), inc="s_w", by=16)
    ph.op(s, lambda e: e.dma_start(c["bv_sb"][:], c["bvc_d"]), inc="s_w", by=16)
    ph.op(s, lambda e: e.dma_start(c["idf_sb"][:], c["idf_d"]), inc="s_w", by=16)
    ph.op(s, lambda e: e.dma_start(c["idb_sb"][:], c["idb_d"]), inc="s_w", by=16)
    ph.op(s, lambda e: e.dma_start(c["e16_sb"][:], c["e16_d"]), inc="s_w", by=16,
          mark="all")
    ph.op(v, lambda e: e.memset(c["ones_row"][:], 1.0))
    ph.op(v, lambda e: e.memset(c["ones_col"][:], 1.0))
    ph.emit()


def _rstd_chain(ph, c, mv, rstd, tag):
    v, sc = "vector", "scalar"
    var_view = mv[:].rearrange("p (m two) -> p m two", two=2)[:, :, 1:2]

    ph.op(v, lambda e: e.tensor_scalar(c["veps"][:], var_view, EPS, None,
                                       op0=ALU.add), inc="s_rst", mark=f"{tag}_ve")
    ph.op(sc, lambda e: e.activation(c["lnv"][:], c["veps"][:], AF.Ln),
          waits=[W(f"{tag}_ve")])
    ph.op(sc, lambda e: e.drain())
    ph.op(sc, lambda e: e.activation(rstd[:], c["lnv"][:], AF.Exp, scale=-0.5),
          inc="s_rst", mark=f"{tag}_done")


def emit_ln1(c):
    nc, sem = c["nc"], c["sem"]
    xfull = c["xfull"]
    ph = Phase(c, "ln1")
    for m in range(NT):
        wl = [] if m < 2 else [W(("ln1x", m - 2))]
        ph.op("sync", (lambda e, m=m: e.dma_start(
            xfull[:, m * D:(m + 1) * D],
            c["x_in"][m * 128:(m + 1) * 128, :])), waits=wl,
            inc=f"s_x{m % 2}", by=16, mark=("ln1x", m))
    for m in range(NT):
        ph.op("vector", None, waits=[W(("ln1x", m))])
        for q in range(8):
            ph.op("vector", (lambda e, m=m, q=q: e.bn_stats(
                c["stat6"][:, 6 * q:6 * q + 6],
                xfull[:, m * D + q * 512:m * D + (q + 1) * 512])))
        ph.op("vector", lambda e: e.drain())
        ph.op("vector", (lambda e, m=m: e.bn_aggr(
            c["mv1"][:, 2 * m:2 * m + 2], c["stat6"][:])))
        ph.op("vector", lambda e: e.drain(), inc="s_st6", mark=("ln1", m))
    _rstd_chain(ph, c, c["mv1"], c["rstd1"], "ln1")
    for ename in ENGINES:
        ph.op(ename, None, waits=[W("all"), W("gall")])
    ph.emit()


def emit_ln_finish(c, idx):
    mv, rstd = c[f"mv{idx}"], c[f"rstd{idx}"]
    ph = Phase(c, f"lnf{idx}")
    for m in range(NT):
        ph.op("vector", (lambda e, m=m: e.bn_aggr(
            mv[:, 2 * m:2 * m + 2], c["stat63"][:, m * 48:(m + 1) * 48])))
    ph.op("vector", lambda e: e.drain())
    _rstd_chain(ph, c, mv, rstd, f"lnf{idx}")
    ph.emit()


def alloc_ffn(c, st, tag=""):
    nc = c["nc"]
    E = st.enter_context
    c["ytm"] = E(nc.sbuf_tensor(tag + "f_ytm", [128, 2 * D], BF16))
    c["y1t"] = E(nc.sbuf_tensor(tag + "f_y1t", [128, 2 * 1024], BF16))
    c["hw_sb"] = E(nc.sbuf_tensor(tag + "f_hw", [128, TOK], BF16))
    c["wexp_sb"] = E(nc.sbuf_tensor(tag + "f_wexp", [128, TOK], F32))
    c["logits_sb"] = E(nc.sbuf_tensor(tag + "f_logits", [NE, TOK], F32))
    c["ltok"] = E(nc.sbuf_tensor(tag + "f_ltok", [128, 8], F32))
    c["l1"] = E(nc.sbuf_tensor(tag + "f_l1", [128, 1], F32))
    c["l1n"] = E(nc.sbuf_tensor(tag + "f_l1n", [128, 1], F32))
    c["m1"] = E(nc.sbuf_tensor(tag + "f_m1", [128, 8], F32))
    c["tmp8"] = E(nc.sbuf_tensor(tag + "f_tmp8", [128, 8], F32))
    c["l2"] = E(nc.sbuf_tensor(tag + "f_l2", [128, 1], F32))
    c["mtop2"] = E(nc.sbuf_tensor(tag + "f_mtop2", [128, 8], F32))
    c["u_sb"] = E(nc.sbuf_tensor(tag + "f_u", [128, 8], F32))
    c["uw"] = E(nc.sbuf_tensor(tag + "f_uw", [128, 8], F32))
    c["ssum"] = E(nc.sbuf_tensor(tag + "f_ssum", [128, 1], F32))
    c["rr"] = E(nc.sbuf_tensor(tag + "f_rr", [128, 1], F32))
    c["wtok"] = E(nc.sbuf_tensor(tag + "f_wtok", [128, 64], F32))
    c["wT8"] = E(nc.sbuf_tensor(tag + "f_wT8", [NE, TOK], F32))
    c["ostage"] = E(nc.sbuf_tensor(tag + "f_ostage", [128, 2 * 1024], F32))
    c["hps"] = E(nc.psum_tensor(tag + "f_hps", [128, 1024], F32))
    c["rpsA"] = E(nc.psum_tensor(tag + "f_rpsA", [NE, 512], F32))
    c["rpsB"] = E(nc.psum_tensor(tag + "f_rpsB", [NE, 512], F32))
    c["wrp"] = E(nc.psum_tensor(tag + "f_wrp", [128, 1024], F32))
    c["tpsA"] = E(nc.psum_tensor(tag + "f_tpsA", [128, 1024], BF16))
    c["tpsB"] = E(nc.psum_tensor(tag + "f_tpsB", [128, 1024], BF16))


def emit_ffn(c, call):
    nc, sem = c["nc"], c["sem"]
    xfull = c["xfull"]
    sink = c["x1d"] if call == 0 else c["out_d"]
    mv = c["mv1"] if call == 0 else c["mv3"]
    rstd = c["rstd1"] if call == 0 else c["rstd3"]
    tps = [c["tpsA"], c["tpsB"]]
    rps = [c["rpsA"], c["rpsB"]]
    bps = [c["hps"][:, 0:512], c["hps"][:, 512:1024],
           c["wrp"][:, 0:512], c["wrp"][:, 512:1024]]
    bpt = [c["hps"], c["wrp"]]
    P = f"f{call}"
    ph = Phase(c, P)
    NBP = NT * 4
    t, v, sc, g, s = "tensor", "vector", "scalar", "gpsimd", "sync"

    if call == 1:
        for m in range(NT):
            wl = [] if m < 2 else [W((P, "x", m - 2))]
            ph.op(s, (lambda e, m=m: e.dma_start(
                xfull[:, m * D:(m + 1) * D],
                c["x2d"][m * 128:(m + 1) * 128, :])), waits=wl,
                inc=f"s_x{m % 2}", by=16, mark=(P, "x", m))

    def ynorm(m):
        wl = [W((P, "x", m))] if call == 1 else []
        if m >= 2:
            wl.append(W((P, "tr", (m - 2, 3))))
        ph.op(v, (lambda e, m=m: e.tensor_scalar(
            c["ytm"][:, (m % 2) * D:(m % 2 + 1) * D],
            xfull[:, m * D:(m + 1) * D],
            mv[:, 2 * m:2 * m + 1], rstd[:, m:m + 1],
            op0=ALU.subtract, op1=ALU.mult)), waits=wl, inc="s_y",
            mark=(P, "y", m))

    def chain(m):
        ph.op(v, (lambda e, m=m: e.tensor_scalar(
            c["logits_sb"][0:NE, m * 128:(m + 1) * 128],
            rps[m // 4][0:NE, (m % 4) * 128:(m % 4 + 1) * 128],
            c["rb_sb"][0:NE, 0:1], None, op0=ALU.add)),
            waits=[W((P, "mm", (m // 4) * 16 + 15))],
            inc="s_rv", mark=(P, "lg", m))
        ph.op(v, (lambda e, m=m: e.tensor_copy(c["ltok"][:], c["wrp"][:, 0:8])),
              waits=[W((P, "ltr", m))])
        ph.op(v, lambda e: e.drain())
        ph.op(v, lambda e: e.tensor_reduce(c["l1"][:], c["ltok"][:], AX.X, ALU.max))
        ph.op(v, lambda e: e.drain())
        ph.op(v, lambda e: e.tensor_scalar(c["l1n"][:], c["l1"][:], -1.0, None,
                                           op0=ALU.mult))
        ph.op(v, lambda e: e.drain(), inc="s_rv", mark=(P, "l1n", m))
        ph.op(v, lambda e: e.tensor_scalar(c["m1"][:], c["ltok"][:], c["l1"][:],
                                           None, op0=ALU.is_ge))
        ph.op(v, lambda e: e.drain())
        ph.op(v, lambda e: e.scalar_tensor_tensor(c["tmp8"][:], c["m1"][:], -1e30,
                                                  c["ltok"][:], op0=ALU.mult,
                                                  op1=ALU.add))
        ph.op(v, lambda e: e.drain())
        ph.op(v, lambda e: e.tensor_reduce(c["l2"][:], c["tmp8"][:], AX.X, ALU.max))
        ph.op(v, lambda e: e.drain())
        ph.op(v, lambda e: e.tensor_scalar(c["mtop2"][:], c["ltok"][:], c["l2"][:],
                                           None, op0=ALU.is_ge))
        ph.op(v, lambda e: e.drain())
        ph.op(v, (lambda e: e.tensor_tensor(c["uw"][:], c["u_sb"][:], c["mtop2"][:],
                                            op=ALU.mult)),
              waits=[W((P, "u", m))])
        ph.op(v, lambda e: e.drain())
        ph.op(v, lambda e: e.tensor_reduce(c["ssum"][:], c["uw"][:], AX.X, ALU.add))
        ph.op(v, lambda e: e.drain())
        ph.op(v, lambda e: e.reciprocal(c["rr"][:], c["ssum"][:]))
        ph.op(v, lambda e: e.drain())
        ph.op(v, (lambda e, m=m: e.tensor_scalar(
            c["wtok"][:, m * 8:(m + 1) * 8], c["uw"][:], c["rr"][:],
            None, op0=ALU.mult)))
        ph.op(v, lambda e: e.drain(), inc="s_rv", mark=(P, "w", m))

    ynorm(0)
    ynorm(1)
    for m in range(NT):
        for jg in range(4):
            kg = m * 4 + jg
            wl = [W((P, "tr", (m, jg)))]
            if kg >= 2:
                wl.append(W((P, "mm", kg - 2)))
            ph.op(v, (lambda e, kg=kg: e.tensor_copy(
                c["y1t"][:, (kg % 2) * 1024:(kg % 2 + 1) * 1024],
                tps[kg % 2][:, 0:1024])), waits=wl, inc="s_trv",
                mark=(P, "trv", kg))
        if m + 2 < NT:
            ynorm(m + 2)
        if m >= 4:
            chain(m - 4)
    for m in range(4, NT):
        chain(m)
    for m in range(NT):
        ph.op(v, (lambda e, m=m: e.tensor_copy(
            c["wT8"][0:NE, m * 128:(m + 1) * 128], c["wrp"][0:NE, 528:656])),
            waits=[W((P, "wtr", m))], inc="s_rv", mark=(P, "w8", m))
    ph.op(v, (lambda e: e.tensor_copy(c["wexp_sb"][:], c["wrp"][:, 0:1024])),
          waits=[W((P, "wexp"))])
    ph.op(v, lambda e: e.drain())
    ph.op(v, (lambda e: e.tensor_tensor(c["hw_sb"][:], c["hps"][:, 0:1024],
                                        c["wexp_sb"][:], op=ALU.mult)),
          inc="s_hw", mark=(P, "hw"))
    for p in range(NBP):
        m, n2 = p // 4, p % 4
        wl = [W((P, "b", 2 * p + 1))]
        if p >= 2:
            wl.append(W((P, "st", p - 2)))
        if call == 0 and p >= 1:
            _bn_pair(ph, c, P, p - 1)
            ph.op(v, lambda e: e.drain())
        ph.op(v, (lambda e, p=p, m=m, n2=n2: e.tensor_tensor(
            c["ostage"][:, (p % 2) * 1024:(p % 2 + 1) * 1024],
            bpt[p % 2][:, 0:1024],
            xfull[:, m * D + n2 * 1024:m * D + (n2 + 1) * 1024], op=ALU.add)),
            waits=wl)
        ph.op(v, lambda e: e.drain(), inc="s_bev", mark=(P, "bev", p))
    if call == 0:
        _bn_pair(ph, c, P, NBP - 1)

    for m in range(NT):
        for jg in range(4):
            kg = m * 4 + jg
            for jj in range(8):
                j = jg * 8 + jj
                wl = []
                if jj == 0:
                    if jg == 0:
                        wl.append(W((P, "y", m)))
                    if kg >= 2:
                        wl.append(W((P, "trv", kg - 2)))
                ph.op(t, (lambda e, m=m, j=j, jj=jj, kg=kg: e.transpose(
                    tps[kg % 2][:, jj * 128:(jj + 1) * 128],
                    c["ytm"][:, (m % 2) * D + j * 128:(m % 2) * D + (j + 1) * 128],
                    c["idb_sb"][:])), waits=wl,
                    inc=("s_tr" if jj == 7 else None),
                    mark=((P, "tr", (m, jg)) if jj == 7 else None))
            ph.op(t, None, waits=[W((P, "trv", kg))])
            for jj in range(8):
                j = jg * 8 + jj
                st_flag = (j == 0 and m % 4 == 0)
                ph.op(t, (lambda e, m=m, j=j, jj=jj, kg=kg, st_flag=st_flag: e.matmul(
                    c["hps"][:, m * 128:(m + 1) * 128],
                    c["at_sb"][:, j * ER:(j + 1) * ER],
                    c["y1t"][:, (kg % 2) * 1024 + jj * 128:(kg % 2) * 1024 + (jj + 1) * 128],
                    start=st_flag, stop=(j == NJ - 1), skip_group_check=True)))
                ph.op(t, (lambda e, m=m, j=j, jj=jj, kg=kg, st_flag=st_flag: e.matmul(
                    rps[m // 4][0:NE, (m % 4) * 128:(m % 4 + 1) * 128],
                    c["rw_sb"][:, j * NE:(j + 1) * NE],
                    c["y1t"][:, (kg % 2) * 1024 + jj * 128:(kg % 2) * 1024 + (jj + 1) * 128],
                    start=st_flag, stop=(j == NJ - 1), skip_group_check=True)),
                    inc=("s_mm" if jj == 7 else None),
                    mark=((P, "mm", kg) if jj == 7 else None))
        if m >= 4:
            _ltr(ph, c, P, m - 4)
    for m in range(4, NT):
        _ltr(ph, c, P, m)
    for m in range(NT):
        _wtr(ph, c, P, m)
    ph.op(t, (lambda e: e.matmul(c["wrp"][:, 0:512], c["e16_sb"][:],
                                 c["wT8"][0:NE, 0:512], start=True, stop=True)),
          waits=[W((P, "w8", NT - 1))])
    ph.op(t, (lambda e: e.matmul(c["wrp"][:, 512:1024], c["e16_sb"][:],
                                 c["wT8"][0:NE, 512:1024], start=True, stop=True)),
          inc="s_pst", mark=(P, "wexp"))
    for k in range(NT * 8):
        m, n = k // 8, k % 8
        wl = [W((P, "hw"))] if k == 0 else []
        if k >= 4:
            wl.append(W((P, "bev", k // 2 - 2)))
        ph.op(t, (lambda e, k=k, m=m, n=n: e.matmul(
            bps[k % 4],
            c["hw_sb"][:, m * 128:(m + 1) * 128],
            c["bfl_sb"][:, n * 512:(n + 1) * 512],
            start=True, stop=True)), waits=wl, inc="s_pst", mark=(P, "b", k))

    for m in range(NT):
        ph.op(sc, (lambda e, m=m: e.activation(
            c["u_sb"][:], c["ltok"][:], AF.Exp, bias=c["l1n"][:])),
            waits=[W((P, "l1n", m))], inc="s_ract", mark=(P, "u", m))

    for p in range(NBP):
        m, n2 = p // 4, p % 4
        ph.op(g, (lambda e, p=p, m=m, n2=n2: e.dma_start(
            sink[m * 128:(m + 1) * 128, n2 * 1024:(n2 + 1) * 1024],
            c["ostage"][:, (p % 2) * 1024:(p % 2 + 1) * 1024])),
            waits=[W((P, "bev", p))], inc=f"s_bdma{p % 2}", by=16,
            mark=(P, "st", p))
    ph.op(g, None, waits=[W((P, "st", NBP - 2)), W((P, "st", NBP - 1))])
    ph.emit()


def _bn_pair(ph, c, P, p):
    m, n2 = p // 4, p % 4
    ph.op("vector", (lambda e, p=p, m=m, n2=n2: e.bn_stats(
        c["stat63"][:, m * 48 + 2 * n2 * 6:m * 48 + (2 * n2 + 1) * 6],
        c["ostage"][:, (p % 2) * 1024:(p % 2) * 1024 + 512])))
    ph.op("vector", (lambda e, p=p, m=m, n2=n2: e.bn_stats(
        c["stat63"][:, m * 48 + (2 * n2 + 1) * 6:m * 48 + (2 * n2 + 2) * 6],
        c["ostage"][:, (p % 2) * 1024 + 512:(p % 2 + 1) * 1024])))


def _ltr(ph, c, P, m):
    ph.op("tensor", (lambda e, m=m: e.transpose(
        c["wrp"][:, 0:8], c["logits_sb"][0:NE, m * 128:(m + 1) * 128],
        c["idf_sb"][0:NE, 0:NE])),
        waits=[W((P, "lg", m))], inc="s_rt", mark=(P, "ltr", m))


def _wtr(ph, c, P, m):
    wl = [W((P, "w", m))]
    if m >= 1:
        wl.append(W((P, "w8", m - 1)))
    ph.op("tensor", (lambda e, m=m: e.transpose(
        c["wrp"][0:NE, 528:656], c["wtok"][:, m * 8:(m + 1) * 8],
        c["idf_sb"][:])),
        waits=wl, inc="s_rt", mark=(P, "wtr", m))


def emit_projections(c, st):
    nc, sem = c["nc"], c["sem"]
    E = st.enter_context
    y2T = E(nc.sbuf_tensor("p_y2T", [128, NJ * TOK], BF16))
    xring = E(nc.sbuf_tensor("p_xr", [128, 2 * D], F32))
    ytm = E(nc.sbuf_tensor("p_ytm", [128, 2 * D], BF16))
    wslab = E(nc.sbuf_tensor("p_ws", [128, 2 * D], BF16))
    kvst = E(nc.sbuf_tensor("p_kvst", [128, 2 * 1024], BF16))
    vtile = E(nc.sbuf_tensor("p_vt", [128, 1024], BF16))
    vstage = E(nc.sbuf_tensor("p_vs", [128, NT * 1032], BF16))
    tpsA = E(nc.psum_tensor("p_tpsA", [128, 1024], BF16))
    tpsB = E(nc.psum_tensor("p_tpsB", [128, 1024], BF16))
    pps = [E(nc.psum_tensor("p_pps0", [128, 1024], F32)),
           E(nc.psum_tensor("p_pps1", [128, 1024], F32))]
    tps = [tpsA, tpsB]
    P = "pj"
    ph = Phase(c, P)
    t, v, g, s = "tensor", "vector", "gpsimd", "sync"
    NPROJ = 48
    NK = NT * NJ

    def slab_src(i):
        if i < 8:
            return c["wk_d"][i, :, :]
        if i < 16:
            return c["wv_d"][i - 8, :, :]
        return c["wq_d"][i - 16, :, :]

    for m in range(NT):
        wl = [] if m < 2 else [W((P, "y", m - 2))]
        ph.op(s, (lambda e, m=m: e.dma_start(
            xring[:, (m % 2) * D:(m % 2 + 1) * D],
            c["x1d"][m * 128:(m + 1) * 128, :])), waits=wl, inc=f"s_x{m % 2}",
            by=16, mark=(P, "x", m))

    ph.op(v, (lambda e: e.memset(
        vstage[:].rearrange("p (h t f) -> p h t f", h=NKV, f=129)[:, :, :, 128:129],
        1.0)), inc="s_rv", mark=(P, "ones"))

    def ynorm(m):
        wl = [W((P, "x", m))]
        if m >= 2:
            wl.append(W((P, "tr", (m - 2, 3))))
        ph.op(v, (lambda e, m=m: e.tensor_scalar(
            ytm[:, (m % 2) * D:(m % 2 + 1) * D],
            xring[:, (m % 2) * D:(m % 2 + 1) * D],
            c["mv2"][:, 2 * m:2 * m + 1], c["rstd2"][:, m:m + 1],
            op0=ALU.subtract, op1=ALU.mult)), waits=wl, inc="s_y", mark=(P, "y", m))

    ynorm(0)
    ynorm(1)
    y2T_v = y2T[:].rearrange("p (j t) -> p j t", t=TOK)
    for m in range(NT):
        for jg in range(4):
            kg = m * 4 + jg
            ph.op(v, (lambda e, m=m, jg=jg, kg=kg: e.tensor_copy(
                y2T_v[:, jg * 8:(jg + 1) * 8, m * 128:(m + 1) * 128],
                tps[kg % 2][:].rearrange("p (a b) -> p a b", b=128))),
                waits=[W((P, "tr", (m, jg)))], inc="s_trv", mark=(P, "trv", kg))
        if m + 2 < NT:
            ynorm(m + 2)
    for i in range(NPROJ):
        if i < 8:
            wl = [W((P, "mm2", i))]
            if i >= 2:
                wl.append(W((P, "kst", i - 2)))
            ph.op(v, (lambda e, i=i: e.tensor_copy(
                kvst[:, (i % 2) * 1024:(i % 2 + 1) * 1024], pps[i % 2][:, 0:1024])),
                waits=wl, inc="s_rv", mark=(P, "ev", i))
        elif i < 16:
            vc = i - 8
            ph.op(v, (lambda e, i=i, vc=vc: e.tensor_scalar(
                vtile[:], pps[i % 2][:, 0:1024], c["bv_sb"][:, vc:vc + 1], None,
                op0=ALU.add)), waits=[W((P, "mm2", i))],
                inc="s_rv", mark=(P, "ev", i))
            kgv = 32 + vc
            vst_v = vstage[:].rearrange("p (h t f) -> p h t f", h=NKV, f=129)
            ph.op(v, (lambda e, vc=vc, kgv=kgv, vst_v=vst_v: e.tensor_copy(
                vst_v[:, vc, :, 0:128],
                tps[kgv % 2][:].rearrange("p (a b) -> p a b", b=128))),
                waits=[W((P, "tr", (8, vc)))], inc="s_trv", mark=(P, "trv", kgv))
        else:
            qi = i - 16
            wl = [W((P, "mm2", i))]
            if qi >= 2:
                wl.append(W((P, "qst", qi - 2)))
            else:
                wl.append(W((P, "kvdone")))
            ph.op(v, (lambda e, i=i: e.tensor_copy(
                kvst[:, (i % 2) * 1024:(i % 2 + 1) * 1024], pps[i % 2][:, 0:1024])),
                waits=wl, inc="s_rv", mark=(P, "ev", i))

    for m in range(NT):
        for jg in range(4):
            kg = m * 4 + jg
            for jj in range(8):
                j = jg * 8 + jj
                wl = []
                if jj == 0:
                    if jg == 0:
                        wl.append(W((P, "y", m)))
                    if kg >= 2:
                        wl.append(W((P, "trv", kg - 2)))
                ph.op(t, (lambda e, m=m, j=j, jj=jj, kg=kg: e.transpose(
                    tps[kg % 2][:, jj * 128:(jj + 1) * 128],
                    ytm[:, (m % 2) * D + j * 128:(m % 2) * D + (j + 1) * 128],
                    c["idb_sb"][:])), waits=wl,
                    inc=("s_tr" if jj == 7 else None),
                    mark=((P, "tr", (m, jg)) if jj == 7 else None))
    for i in range(NPROJ):
        wl = [W((P, "kv2", i))]
        if i == 0:
            wl.append(W((P, "trv", 31)))
        if i >= 2:
            wl.append(W((P, "ev", i - 2)))
        ph.op(t, None, waits=wl)
        for j in range(NJ):
            for hh in range(2):
                last = (j == NJ - 1 and hh == 1)
                ph.op(t, (lambda e, i=i, j=j, hh=hh: e.matmul(
                    pps[i % 2][:, hh * 512:(hh + 1) * 512],
                    wslab[:, (i % 2) * D + j * 128:(i % 2) * D + (j + 1) * 128],
                    y2T[:, j * TOK + hh * 512:j * TOK + (hh + 1) * 512],
                    start=(j == 0), stop=(j == NJ - 1), skip_group_check=True)),
                    inc=("s_mm2" if last else None),
                    mark=((P, "mm2", i) if last else None))
        if 8 <= i < 16:
            vc = i - 8
            kgv = 32 + vc
            for t_ in range(NT):
                wl = []
                if t_ == 0:
                    wl.append(W((P, "ev", i)))
                    if kgv >= 2:
                        wl.append(W((P, "trv", kgv - 2)))
                ph.op(t, (lambda e, t_=t_, kgv=kgv: e.transpose(
                    tps[kgv % 2][:, t_ * 128:(t_ + 1) * 128],
                    vtile[:, t_ * 128:(t_ + 1) * 128],
                    c["idb_sb"][:])), waits=wl,
                    inc=("s_tr" if t_ == NT - 1 else None),
                    mark=((P, "tr", (8, vc)) if t_ == NT - 1 else None))

    for i in range(NPROJ):
        wl = [] if i < 2 else [W((P, "mm2", i - 2))]
        ph.op(g, (lambda e, i=i: e.dma_start(
            wslab[:, (i % 2) * D:(i % 2 + 1) * D], slab_src(i),
            max_dma_last_dim=8192)), waits=wl, inc=f"s_kv2{i % 2}", by=16,
            mark=(P, "kv2", i))
    ph.op(g, (lambda e: e.collective_compute(
        "AllGather", ALU.bypass, replica_groups=[list(range(R))],
        ins=[c["ktd"].opt()], outs=[c["ktg"].opt()])),
        waits=[W((P, "kvdone"))],
        inc="s_cc", mark=(P, "cck"))
    ph.op(g, (lambda e: e.collective_compute(
        "AllGather", ALU.bypass, replica_groups=[list(range(R))],
        ins=[c["vtd"].opt()], outs=[c["vtg"].opt()])),
        inc="s_cc", mark=(P, "ccv"))

    for i in range(8):
        ph.op(s, (lambda e, i=i: e.dma_start(
            c["ktd"][i * 128:(i + 1) * 128, :],
            kvst[:, (i % 2) * 1024:(i % 2 + 1) * 1024])),
            waits=[W((P, "ev", i))], inc=f"s_kst{i % 2}", by=16,
            mark=(P, "kst", i))
    for t_ in range(NT):
        wl = [W((P, "trv", 39)), W((P, "ones"))]
        if t_ >= 2:
            wl.append(W((P, "vst", t_ - 2)))
        else:
            wl.append(W((P, "kst", 6 + t_)))
        ph.op(s, (lambda e, t_=t_: e.dma_start(
            c["vtd"][t_ * 128:(t_ + 1) * 128, :],
            vstage[:, t_ * 1032:(t_ + 1) * 1032])), waits=wl,
            inc=f"s_kst{t_ % 2}", by=16, mark=(P, "vst", t_))
    ph.op(s, None, waits=[W((P, "vst", NT - 2)), W((P, "vst", NT - 1))],
          inc="s_pj", mark=(P, "kvdone"))
    for qi in range(32):
        ph.op(s, (lambda e, qi=qi: e.dma_start(
            c["qtd"][qi, :, :], kvst[:, ((16 + qi) % 2) * 1024:((16 + qi) % 2 + 1) * 1024])),
            waits=[W((P, "ev", 16 + qi))], inc=f"s_kst{qi % 2}", by=16,
            mark=(P, "qst", qi))
    ph.op(s, None, waits=[W((P, "qst", 30)), W((P, "qst", 31))])
    ph.emit()


def emit_attention(c, st):
    nc, sem = c["nc"], c["sem"]
    E = st.enter_context
    kT = E(nc.sbuf_tensor("a_kT", [128, 2 * S], BF16))
    vh = E(nc.sbuf_tensor("a_vh", [128, 2 * NCH * 129], BF16))
    qT = E(nc.sbuf_tensor("a_qT", [128, 2 * G4 * TOK], BF16))
    ptil = E(nc.sbuf_tensor("a_ptil", [128, 3 * 1536], BF16))
    ksq = E(nc.sbuf_tensor("a_ksq", [128, S], BF16))
    qsq = E(nc.sbuf_tensor("a_qsq", [128, G4 * TOK], BF16))
    cstage = E(nc.sbuf_tensor("a_cs", [128, 4 * 128], BF16))
    negbh = E(nc.sbuf_tensor("a_nb", [128, G4], F32))
    nt1 = E(nc.sbuf_tensor("a_nt1", [128, 1], F32))
    ntk = E(nc.sbuf_tensor("a_ntk", [1, 1], F32))
    ntq = E(nc.sbuf_tensor("a_ntq", [1, 1], F32))
    nts = E(nc.sbuf_tensor("a_nts", [1, 1], F32))
    ntb = E(nc.sbuf_tensor("a_ntb", [1, 1], BF16))
    rcol = E(nc.sbuf_tensor("a_rc", [128, 4], F32))
    stA = E(nc.psum_tensor("a_stA", [128, 1536], F32))
    stB = E(nc.psum_tensor("a_stB", [128, 1536], F32))
    av01 = E(nc.psum_tensor("a_av01", [128, 512], F32))
    av23 = E(nc.psum_tensor("a_av23", [128, 512], F32))
    stX = [stA, stB]
    avX = [av01, av01, av23, av23]
    avo = [0, 129, 0, 129]
    P = "at"
    ph = Phase(c, P)
    t, v, sc, g, s = "tensor", "vector", "scalar", "gpsimd", "sync"
    BSC = ATT_SCALE * BH_MARGIN * 0.5

    GSIZES = [3] * 20 + [2, 2]
    GPE = len(GSIZES)
    GOFF = [0]
    for z in GSIZES:
        GOFF.append(GOFF[-1] + z)

    def ep(h, gq, s_):
        return (h * G4 + gq) * 2 + s_

    for h in range(R):
        par = h % 2
        wl = [W(("pj", "ccv"))]
        if h >= 2:
            wl.append(W((P, "avm", (ep(h - 2, G4 - 1, 1) + 1) * GPE - 1)))
        ph.op(s, None, waits=wl)
        for r in range(R):
            ph.op(s, (lambda e, h=h, r=r, par=par: e.dma_start(
                kT[:, par * S + r * TOK:par * S + (r + 1) * TOK],
                c["ktg"][r * NKV * 128 + h * 128:r * NKV * 128 + (h + 1) * 128, :])),
                inc=f"s_kv{h % 2}", by=16)
        for r in range(R):
            ph.op(s, (lambda e, h=h, r=r, par=par: e.dma_start(
                vh[:, par * NCH * 129 + r * 8 * 129:
                   par * NCH * 129 + (r + 1) * 8 * 129],
                c["vtg"][r * TOK + h * 128:r * TOK + (h + 1) * 128, :])),
                inc=f"s_kv{h % 2}", by=16)
        for gq in range(G4):
            mk = (P, "kv", h) if gq == G4 - 1 else None
            ph.op(s, (lambda e, h=h, gq=gq, par=par: e.dma_start(
                qT[:, par * G4 * TOK + gq * TOK:par * G4 * TOK + (gq + 1) * TOK],
                c["qtd"][h * G4 + gq, :, :])), inc=f"s_kv{h % 2}", by=16, mark=mk)

    for n in range(8):
        wl = [W((P, "kv", 0))] if n == 0 else (
            [] if n < 2 else [W((P, "wocast", n - 2))])
        ph.op(g, (lambda e, n=n: e.dma_start(
            c["wo_bf"][n], c["wo_d"][n], max_dma_last_dim=8192)), waits=wl,
            inc=f"s_xn{n % 2}", by=16, mark=(P, "wocast", n))

    avq = []

    def _av_pop(desc):
        i, gi, par, gh, s_ = desc
        Gi = i * GPE + gi
        sz = GSIZES[gi]
        wl = [W((P, "exp", Gi))]
        if gi == 0 and i >= 1:
            wl.append(W((P, "ctv", i - 1)))
        nmm = 0
        tot = sz * 4
        for cpos in range(sz):
            ch = GOFF[gi] + cpos
            for q in range(4):
                st_flag = (gi == 0 and cpos == 0 and (q % 2 == 0))
                nmm += 1
                ph.op(t, (lambda e, Gi=Gi, ch=ch, cpos=cpos, q=q, par=par,
                          st_flag=st_flag: e.matmul(
                    avX[q][:, avo[q]:avo[q] + 129],
                    ptil[:, (Gi % 3) * 1536 + cpos * 512 + q * 128:
                         (Gi % 3) * 1536 + cpos * 512 + (q + 1) * 128],
                    vh[:, par * NCH * 129 + ch * 129:par * NCH * 129 + (ch + 1) * 129],
                    start=st_flag, stop=False, skip_group_check=True)),
                    waits=wl if (cpos == 0 and q == 0) else (),
                    inc=("s_avm" if nmm == tot else None),
                    mark=((P, "avm", Gi) if nmm == tot else None))
        if gi == GPE - 1:
            trt = av01[:].bitcast(BF16)
            for q in range(4):
                ph.op(t, (lambda e, q=q, trt=trt: e.transpose(
                    trt[:, q * 128:(q + 1) * 128],
                    cstage[:, q * 128:(q + 1) * 128],
                    c["idb_sb"][:])),
                    waits=[W((P, "cs", i))] if q == 0 else (),
                    inc=("s_ctr" if q == 3 else None),
                    mark=((P, "ctr", i) if q == 3 else None))

    def squares(h):
        par = h % 2
        wl = [W((P, "kv", h))]
        if h >= 1:
            wl.append(W((P, "norm", h - 1)))
        ph.op(v, (lambda e, par=par: e.tensor_tensor(
            ksq[:], kT[:, par * S:(par + 1) * S], kT[:, par * S:(par + 1) * S],
            op=ALU.mult)), waits=wl)
        ph.op(v, (lambda e, par=par: e.tensor_tensor(
            qsq[:], qT[:, par * G4 * TOK:(par + 1) * G4 * TOK],
            qT[:, par * G4 * TOK:(par + 1) * G4 * TOK], op=ALU.mult)),
            inc="s_sq", mark=(P, "sq", h))

    squares(0)
    for h in range(R):
        par = h % 2
        gh0 = h * G4
        while avq:
            _av_pop(avq.pop(0))
        wl = [W((P, "sq", h))]
        if h >= 1:
            lastg = (ep(h - 1, G4 - 1, 1) + 1) * GPE - 1
            wl.append(W((P, "exp", lastg)))
            wl.append(W((P, "nb", (h - 1, G4 - 1))))
        ph.op(t, None, waits=wl)
        for cc in range(NCH):
            ph.op(t, (lambda e, cc=cc, first=(cc == 0): e.matmul(
                stA[:, cc:cc + 1], ksq[:, cc * 128:(cc + 1) * 128],
                c["ones_col"][:], start=first, stop=False, skip_group_check=True)))
        for gq in range(G4):
            for t_ in range(NT):
                last = (gq == G4 - 1 and t_ == NT - 1)
                ph.op(t, (lambda e, gq=gq, t_=t_: e.matmul(
                    stA[:, 64 + gq * NT + t_:64 + gq * NT + t_ + 1],
                    qsq[:, gq * TOK + t_ * 128:gq * TOK + (t_ + 1) * 128],
                    c["ones_col"][:], start=False, stop=False,
                    skip_group_check=True)),
                    inc=("s_st" if last else None),
                    mark=((P, "norm", h) if last else None))
        ph.op(v, (lambda e: e.tensor_reduce(nt1[:], stA[:, 0:64], AX.X, ALU.max)),
              waits=[W((P, "norm", h))])
        ph.op(v, lambda e: e.drain(), inc="s_sq", mark=(P, "kcol", h))
        ph.op(t, (lambda e: e.transpose(stA[0:1, 96:224], nt1[:], c["idf_sb"][:])),
              waits=[W((P, "kcol", h))], inc="s_st", mark=(P, "ktr", h))
        ph.op(v, (lambda e: e.tensor_reduce(ntk[0:1, 0:1], stA[0:1, 96:224],
                                            AX.X, ALU.max)),
              waits=[W((P, "ktr", h))])
        ph.op(v, lambda e: e.drain())
        for gq in range(G4):
            ph.op(v, (lambda e, gq=gq: e.tensor_reduce(
                nt1[:], stA[:, 64 + gq * NT:64 + (gq + 1) * NT], AX.X, ALU.max)))
            ph.op(v, lambda e: e.drain(), inc="s_sq", mark=(P, "qcol", (h, gq)))
            ph.op(t, (lambda e: e.transpose(stA[0:1, 224:352], nt1[:],
                                            c["idf_sb"][:])),
                  waits=[W((P, "qcol", (h, gq)))], inc="s_st",
                  mark=(P, "qtr", (h, gq)))
            ph.op(v, (lambda e: e.tensor_reduce(ntq[0:1, 0:1], stA[0:1, 224:352],
                                                AX.X, ALU.max)),
                  waits=[W((P, "qtr", (h, gq)))])
            ph.op(v, lambda e: e.drain())
            ph.op(v, (lambda e: e.tensor_tensor(nts[0:1, 0:1], ntq[0:1, 0:1],
                                                ntk[0:1, 0:1], op=ALU.add)))
            ph.op(v, lambda e: e.drain())
            ph.op(v, (lambda e: e.tensor_scalar(ntb[0:1, 0:1], nts[0:1, 0:1],
                                                BSC, None, op0=ALU.mult)))
            ph.op(v, lambda e: e.drain(), inc="s_sq", mark=(P, "ntb", (h, gq)))
            ph.op(t, (lambda e, gq=gq: e.matmul(
                stA[:, 480 + gq:481 + gq], c["ones_row"][:], ntb[0:1, 0:1],
                start=False, stop=False, skip_group_check=True)),
                waits=[W((P, "ntb", (h, gq)))], inc="s_st",
                mark=(P, "bc", (h, gq)))
            ph.op(v, (lambda e, gq=gq: e.tensor_scalar(
                negbh[:, gq:gq + 1], stA[:, 480 + gq:481 + gq], -1.0, None,
                op0=ALU.mult)),
                waits=[W((P, "bc", (h, gq)))])
            ph.op(v, lambda e: e.drain(), inc="s_nb", mark=(P, "nb", (h, gq)))

        for gq in range(G4):
            for s_ in range(2):
                i = ep(h, gq, s_)
                for gi in range(GPE):
                    Gi = i * GPE + gi
                    sz = GSIZES[gi]
                    wl = []
                    if Gi >= 2:
                        wl.append(W((P, "exp", Gi - 2)))
                    if gi == 0:
                        wl.append(W((P, "nb", (h, G4 - 1))))
                        wl.append(W((P, "kv", h)))
                    for cpos in range(sz):
                        ch = GOFF[gi] + cpos
                        last = (cpos == sz - 1)
                        ph.op(t, (lambda e, Gi=Gi, ch=ch, cpos=cpos, par=par,
                                  gq=gq, s_=s_: e.matmul(
                            stX[Gi % 2][:, cpos * 512:(cpos + 1) * 512],
                            kT[:, par * S + ch * 128:par * S + (ch + 1) * 128],
                            qT[:, par * G4 * TOK + gq * TOK + s_ * 512:
                               par * G4 * TOK + gq * TOK + s_ * 512 + 512],
                            start=True, stop=True)),
                            waits=wl if cpos == 0 else (),
                            inc=("s_st" if last else None),
                            mark=((P, "stg", Gi) if last else None))
                    avq.append((i, gi, par, gh0 + gq, s_))
                    if len(avq) > 2:
                        _av_pop(avq.pop(0))
                ph.op(v, lambda e: e.drain(),
                      waits=([W((P, "avm", (i + 1) * GPE - 1))] +
                             ([W((P, "ctr", i - 1))] if i >= 1 else [])))
                ph.op(v, lambda e: e.reciprocal(rcol[:, 0:1], av01[:, 128:129]))
                ph.op(v, lambda e: e.reciprocal(rcol[:, 1:2], av01[:, 257:258]))
                ph.op(v, lambda e: e.reciprocal(rcol[:, 2:3], av23[:, 128:129]))
                ph.op(v, lambda e: e.reciprocal(rcol[:, 3:4], av23[:, 257:258]))
                ph.op(v, lambda e: e.drain())
                for q in range(4):
                    ph.op(v, (lambda e, q=q: e.tensor_scalar(
                        cstage[:, q * 128:(q + 1) * 128],
                        avX[q][:, avo[q]:avo[q] + 128], rcol[:, q:q + 1], None,
                        op0=ALU.mult)),
                        inc=("s_evac" if q == 3 else None),
                        mark=((P, "cs", i) if q == 3 else None))
                gh = gh0 + gq
                ph.op(v, (lambda e, gh=gh, s_=s_: e.tensor_copy(
                    c["ctxT"][:, gh * TOK + s_ * 512:gh * TOK + (s_ + 1) * 512],
                    av01[:].bitcast(BF16)[:, 0:512])),
                    waits=[W((P, "ctr", i))], inc="s_ctv", mark=(P, "ctv", i))
                if gq == G4 - 1 and s_ == 0 and h + 1 < R:
                    squares(h + 1)

        for gq in range(G4):
            for s_ in range(2):
                i = ep(h, gq, s_)
                for gi in range(GPE):
                    Gi = i * GPE + gi
                    sz = GSIZES[gi]
                    wl = [W((P, "stg", Gi))]
                    if Gi >= 3:
                        wl.append(W((P, "avm", Gi - 3)))
                    if gi == 0:
                        wl.append(W((P, "nb", (h, gq))))
                    ph.op(sc, (lambda e, Gi=Gi, gq=gq, sz=sz: e.activation(
                        ptil[:, (Gi % 3) * 1536:(Gi % 3) * 1536 + sz * 512],
                        stX[Gi % 2][:, 0:sz * 512], AF.Exp,
                        bias=negbh[:, gq:gq + 1], scale=ATT_SCALE)),
                        waits=wl, inc="s_exp", mark=(P, "exp", Gi))
    while avq:
        _av_pop(avq.pop(0))
    ph.emit()



def emit_oproj(c, st):
    nc, sem = c["nc"], c["sem"]
    E = st.enter_context
    woslab = E(nc.sbuf_tensor("o_ws", [128, 2 * NJ * 512], BF16))
    x1ring = E(nc.sbuf_tensor("o_x1", [128, 2 * 512], F32))
    x2st = E(nc.sbuf_tensor("o_x2", [128, 2 * 512], F32))
    ops = [E(nc.psum_tensor("o_op0", [128, 512], F32)),
           E(nc.psum_tensor("o_op1", [128, 512], F32))]
    P = "op"
    ph = Phase(c, P)
    t, v, g, s = "tensor", "vector", "gpsimd", "sync"
    NB = 64

    for n in range(8):
        wl = [W(("at", "wocast", n))]
        if n >= 2:
            wl.append(W((P, "mm", (n - 2) * 8 + 7)))
        ph.op("scalar", (lambda e, n=n: e.dma_start(
            woslab[:, (n % 2) * NJ * 512:(n % 2 + 1) * NJ * 512]
            .rearrange("p (j f) -> p j f", f=512),
            c["wo_bf"][n, :, :, :].rearrange("j pp f -> pp j f"))), waits=wl,
            inc=f"s_wo{n % 2}", by=16, mark=(P, "wo", n))

    for k in range(NB + 2):
        if k < NB:
            n, m = k // 8, k % 8
            wl = [] if k < 2 else [W((P, "bev", k - 2))]
            ph.op(s, (lambda e, n=n, m=m, k=k: e.dma_start(
                x1ring[:, (k % 2) * 512:(k % 2 + 1) * 512],
                c["x1d"][m * 128:(m + 1) * 128, n * 512:(n + 1) * 512])),
                waits=wl, inc=f"s_xa{k % 2}", by=16, mark=(P, "x1", k))
        if k >= 2:
            k2 = k - 2
            n2, m2 = k2 // 8, k2 % 8
            ph.op(s, (lambda e, n2=n2, m2=m2, k2=k2: e.dma_start(
                c["x2d"][m2 * 128:(m2 + 1) * 128, n2 * 512:(n2 + 1) * 512],
                x2st[:, (k2 % 2) * 512:(k2 % 2 + 1) * 512])),
                waits=[W((P, "bev", k2))], inc=f"s_bd2{k2 % 2}", by=16,
                mark=(P, "st", k2))
    ph.op(s, None, waits=[W((P, "st", NB - 2)), W((P, "st", NB - 1))])

    for n in range(8):
        for m in range(8):
            k = n * 8 + m
            wl = [W((P, "wo", n))] if m == 0 else []
            if k >= 2:
                wl.append(W((P, "bev", k - 2)))
            ph.op(t, None, waits=wl)
            for j in range(NJ):
                last = (j == NJ - 1)
                ph.op(t, (lambda e, n=n, m=m, j=j, k=k: e.matmul(
                    ops[k % 2][:, 0:512],
                    c["ctxT"][:, j * TOK + m * 128:j * TOK + (m + 1) * 128],
                    woslab[:, (n % 2) * NJ * 512 + j * 512:
                           (n % 2) * NJ * 512 + (j + 1) * 512],
                    start=(j == 0), stop=(j == NJ - 1))),
                    inc=("s_pst" if last else None),
                    mark=((P, "mm", k) if last else None))

    for k in range(NB):
        n, m = k // 8, k % 8
        wl = [W((P, "mm", k)), W((P, "x1", k))]
        if k >= 2:
            wl.append(W((P, "st", k - 2)))
        ph.op(v, (lambda e, k=k: e.tensor_tensor(
            x2st[:, (k % 2) * 512:(k % 2 + 1) * 512], ops[k % 2][:, 0:512],
            x1ring[:, (k % 2) * 512:(k % 2 + 1) * 512], op=ALU.add)), waits=wl)
        ph.op(v, lambda e: e.drain())
        ph.op(v, (lambda e, n=n, m=m, k=k: e.bn_stats(
            c["stat63"][:, m * 48 + n * 6:m * 48 + (n + 1) * 6],
            x2st[:, (k % 2) * 512:(k % 2 + 1) * 512])), inc="s_bev",
            mark=(P, "bev", k))
    ph.emit()


def emit_debug_stats(c):
    ph = Phase(c, "dbgs")
    k0 = c["cnt"]["s_dbg"]
    ph.op("gpsimd", (lambda e: e.dma_start(c["dbg_mv"], c["mv1"][:])),
          inc="s_dbg", by=16)
    ph.op("gpsimd", (lambda e: e.dma_start(c["dbg_rs"], c["rstd1"][:])),
          inc="s_dbg", by=16)
    ph.op("gpsimd", (lambda e: e.dma_start(c["dbg_ve"], c["veps"][:])),
          inc="s_dbg", by=16)
    ph.op("gpsimd", None, waits=[WV("s_dbg", c["cnt"]["s_dbg"])])
    ph.emit()


def emit_debug_copy(c, src, dst, nbytes):
    ph = Phase(c, "dbg")
    ph.op("gpsimd", (lambda e: e.dma_start(dst, src)), inc="s_dbg", by=16)
    ph.op("gpsimd", None, waits=[WV("s_dbg", c["cnt"]["s_dbg"])])
    ph.emit()


DBG_N = [0]


def emit_debug_cast(c, src_bf, dst_f32, rows, cols):
    nc = c["nc"]
    ph = Phase(c, "dbgc")
    DBG_N[0] += 1
    with ExitStack() as st:
        buf = st.enter_context(nc.sbuf_tensor(f"dbgbuf{DBG_N[0]}", [128, cols], F32))
        for rr in range(rows // 128):
            k0 = c["cnt"]["s_dbg"]
            ph.op("gpsimd", (lambda e, rr=rr: e.dma_start(
                buf[:], src_bf[rr * 128:(rr + 1) * 128, :], max_dma_last_dim=2048)),
                inc="s_dbg", by=16)
            ph.op("gpsimd", None, waits=[WV("s_dbg", k0 + 16)])
            ph.op("gpsimd", (lambda e, rr=rr: e.dma_start(
                dst_f32[rr * 128:(rr + 1) * 128, :], buf[:])), inc="s_dbg", by=16)
            k1 = c["cnt"]["s_dbg"]
            ph.op("gpsimd", None, waits=[WV("s_dbg", k1)])
        ph.emit()


def emit_debug_ctx(c):
    nc = c["nc"]
    ph = Phase(c, "dbgx")
    with ExitStack() as st:
        buf = st.enter_context(nc.sbuf_tensor("dbgbufx", [128, TOK], F32))
        for j in range(NJ):
            k0 = c["cnt"]["s_dbg"]
            ph.op("gpsimd", (lambda e, j=j: e.tensor_copy(
                buf[:], c["ctxT"][:, j * TOK:(j + 1) * TOK])))
            ph.op("gpsimd", (lambda e, j=j: e.dma_start(
                c["dbg_ctx"][j * 128:(j + 1) * 128, :], buf[:])), inc="s_dbg", by=16)
            ph.op("gpsimd", None, waits=[WV("s_dbg", k0 + 16)])
        ph.emit()


def kernel(**inputs):
    img = host_images(inputs)
    X = np.asarray(inputs["L0_attn_hidden_states"], np.float32).reshape(S, D)

    nc = build_nc(debug=False)
    in_maps = []
    for cid in range(R):
        in_maps.append({
            "x": np.ascontiguousarray(X[cid * TOK:(cid + 1) * TOK, :]),
            "wq": img["wq"], "wk": img["wk"], "wv": img["wv"], "wo": img["wo"],
            "at": img["at"], "bfl": img["bfl"], "rw": img["rw"], "rb": img["rb"],
            "bvc": img["bvc"], "idf": img["idf"], "idb": img["idb"],
            "e16": img["e16"],
        })
    res = run_bass_kernel_spmd(nc, in_maps, core_ids=list(range(R)))
    out = np.concatenate([res.results[i]["out"] for i in range(R)], axis=0)
    return out.reshape(1, S, D)


# version 13
# speedup vs baseline: 1.0663x; 1.0093x over previous
import numpy as np
import ml_dtypes
import concourse.bass as bass
import concourse.mybir as mybir
from concourse.bass_utils import run_bass_kernel_spmd
from contextlib import ExitStack

F32 = mybir.dt.float32
BF16 = mybir.dt.bfloat16
AF = mybir.ActivationFunctionType
ALU = mybir.AluOpType
AX = mybir.AxisListType

R = 8
S = 8192
D = 4096
TOK = S // R
NT = TOK // 128
NJ = D // 128
NQH, NKV, G4, HD = 32, 8, 4, 128
EPS = 1e-5
ATT_SCALE = float(HD) ** -0.5
BH_MARGIN = 1.04
NE, RK = 8, 16
ER = NE * RK
NCH = S // 128
NGRP = NCH // 2
NEPOCH = R * G4 * 2
TOTG = NEPOCH * NGRP
BF = ml_dtypes.bfloat16
ENGINES = ["tensor", "vector", "scalar", "gpsimd", "sync"]


def _lhsT_image(W):
    dout, din = W.shape
    OC, J = dout // 128, din // 128
    W4 = W.reshape(OC, 128, J, 128)
    return np.ascontiguousarray(W4.transpose(0, 3, 2, 1)).reshape(OC, 128, J * 128)


def host_images(inp):
    Wq = np.asarray(inp["L0_attn_Wq"], np.float32)
    Wk = np.asarray(inp["L0_attn_Wk"], np.float32)
    Wv = np.asarray(inp["L0_attn_Wv"], np.float32)
    Wo = np.asarray(inp["L0_attn_Wo"], np.float32)
    bv = np.asarray(inp["L0_attn_bv"], np.float32)
    rw = np.asarray(inp["L0_ffn_router_w"], np.float32)
    rb = np.asarray(inp["L0_ffn_router_b"], np.float32)
    A = np.asarray(inp["L0_ffn_A"], np.float32)
    Bm = np.asarray(inp["L0_ffn_Bm"], np.float32)

    img = {}
    img["wq"] = _lhsT_image(Wq)
    img["wk"] = _lhsT_image(Wk)
    img["wv"] = _lhsT_image(Wv)
    Wo4 = Wo.reshape(8, 512, NJ, 128)
    img["wo"] = np.ascontiguousarray(Wo4.transpose(0, 2, 3, 1))
    Aflat = A.reshape(ER, D)
    img["at"] = np.ascontiguousarray(
        Aflat.reshape(ER, NJ, 128).transpose(2, 1, 0)).reshape(128, NJ * ER)
    img["bfl"] = np.ascontiguousarray(Bm.transpose(0, 2, 1)).reshape(ER, D)
    img["rw"] = np.ascontiguousarray(
        rw.reshape(NE, NJ, 128).transpose(2, 1, 0)).reshape(128, NJ * NE)
    img["rb"] = np.ascontiguousarray(rb.reshape(NE, 1))
    img["bvc"] = np.ascontiguousarray(bv.reshape(NKV, 128).T)
    img["idf"] = np.eye(128, dtype=np.float32)
    img["idb"] = np.eye(128, dtype=np.float32).astype(BF)
    e16 = np.zeros((NE, 128), np.float32)
    for e in range(NE):
        e16[e, e * RK:(e + 1) * RK] = 0.25
    img["e16"] = e16
    return img


class Phase:

    def __init__(self, c, name):
        self.c = c
        self.name = name
        self.ev = {e: [] for e in ENGINES}

    def op(self, eng, build=None, waits=(), inc=None, by=1, mark=None):
        c = self.c
        if inc is not None:
            c["cnt"][inc] += by
            if mark is not None:
                c["marks"][mark] = (inc, c["cnt"][inc])
        self.ev[eng].append((tuple(waits), build, inc, by))

    def emit(self):
        c = self.c
        nc, sem, marks = c["nc"], c["sem"], c["marks"]
        with nc.Block() as blk:
            for ename in ENGINES:
                evs = self.ev[ename]
                if not evs:
                    continue

                def make(evs=evs):
                    def f(eng):
                        for waits, build, incname, by in evs:
                            for w in waits:
                                kind, a, b_ = w
                                if kind == "abs":
                                    sname, v = a, b_
                                else:
                                    sname, mv = marks[a]
                                    v = mv + b_
                                if v > 0:
                                    eng.wait_ge(sem[sname], v)
                            instr = build(eng) if build is not None else None
                            if incname is not None:
                                tgt = instr if instr is not None else eng.nop()
                                tgt.then_inc(sem[incname], by)
                    return f
                getattr(blk, ename)(make())


def W(mark, delta=0):
    return ("mark", mark, delta)


def WV(sname, val):
    return ("abs", sname, val)


def build_nc(debug=False):
    nc = bass.Bass("TRN2", target_bir_lowering=False, debug=False)
    c = {"nc": nc, "cnt": {}, "marks": {}, "debug": debug}

    def dram(name, shape, dtype, kind=None):
        if kind:
            return nc.dram_tensor(name, shape, dtype, kind=kind).ap()
        return nc.dram_tensor(name, shape, dtype).ap()

    c["x_in"] = dram("x", [TOK, D], F32, "ExternalInput")
    c["wq_d"] = dram("wq", [NJ, 128, D], F32, "ExternalInput")
    c["wk_d"] = dram("wk", [NKV, 128, D], F32, "ExternalInput")
    c["wv_d"] = dram("wv", [NKV, 128, D], F32, "ExternalInput")
    c["wo_d"] = dram("wo", [8, NJ, 128, 512], F32, "ExternalInput")
    c["at_d"] = dram("at", [128, NJ * ER], F32, "ExternalInput")
    c["bfl_d"] = dram("bfl", [ER, D], F32, "ExternalInput")
    c["rw_d"] = dram("rw", [128, NJ * NE], F32, "ExternalInput")
    c["rb_d"] = dram("rb", [NE, 1], F32, "ExternalInput")
    c["bvc_d"] = dram("bvc", [128, NKV], F32, "ExternalInput")
    c["idf_d"] = dram("idf", [128, 128], F32, "ExternalInput")
    c["idb_d"] = dram("idb", [128, 128], BF16, "ExternalInput")
    c["e16_d"] = dram("e16", [NE, 128], F32, "ExternalInput")
    c["out_d"] = dram("out", [TOK, D], F32, "ExternalOutput")

    c["x1d"] = dram("x1d", [TOK, D], F32)
    c["x2d"] = dram("x2d", [TOK, D], F32)
    c["qtd"] = dram("qtd", [NQH, 128, TOK], BF16)
    c["kvd"] = dram("kvd", [NKV * 128, TOK + NKV * 129], BF16)
    c["kvg"] = dram("kvg", [R * NKV * 128, TOK + NKV * 129], BF16)
    c["wo_bf"] = dram("wo_bf", [8, NJ, 128, 512], BF16)
    if debug:
        c["dbg_x1"] = dram("dbg_x1", [TOK, D], F32, "ExternalOutput")
        c["dbg_x2"] = dram("dbg_x2", [TOK, D], F32, "ExternalOutput")
        c["dbg_ctx"] = dram("dbg_ctx", [NJ * 128, TOK], F32, "ExternalOutput")
        c["dbg_kt"] = dram("dbg_kt", [R * NKV * 128, TOK], F32, "ExternalOutput")
        c["dbg_vt"] = dram("dbg_vt", [R * TOK, NKV * 129], F32, "ExternalOutput")
        c["dbg_qt"] = dram("dbg_qt", [NQH * 128, TOK], F32, "ExternalOutput")
        c["dbg_mv"] = dram("dbg_mv", [128, 2 * NT], F32, "ExternalOutput")
        c["dbg_rs"] = dram("dbg_rs", [128, NT], F32, "ExternalOutput")
        c["dbg_ve"] = dram("dbg_ve", [128, NT], F32, "ExternalOutput")

    with ExitStack() as ST:
        E = ST.enter_context
        c["at_sb"] = E(nc.sbuf_tensor("at_sb", [128, NJ * ER], BF16))
        c["bfl_sb"] = E(nc.sbuf_tensor("bfl_sb", [ER, D], BF16))
        c["rw_sb"] = E(nc.sbuf_tensor("rw_sb", [128, NJ * NE], BF16))
        c["rb_sb"] = E(nc.sbuf_tensor("rb_sb", [NE, 1], F32))
        c["bv_sb"] = E(nc.sbuf_tensor("bv_sb", [128, NKV], F32))
        c["idf_sb"] = E(nc.sbuf_tensor("idf_sb", [128, 128], F32))
        c["idb_sb"] = E(nc.sbuf_tensor("idb_sb", [128, 128], BF16))
        c["e16_sb"] = E(nc.sbuf_tensor("e16_sb", [NE, 128], F32))
        c["ones_row"] = E(nc.sbuf_tensor("ones_row", [1, 128], BF16))
        c["ones_col"] = E(nc.sbuf_tensor("ones_col", [128, 1], BF16))
        for i in (1, 2, 3):
            c[f"mv{i}"] = E(nc.sbuf_tensor(f"mv{i}", [128, 2 * NT], F32))
            c[f"rstd{i}"] = E(nc.sbuf_tensor(f"rstd{i}", [128, NT], F32))
        c["stat6"] = E(nc.sbuf_tensor("stat6", [128, 48], F32))
        c["stat63"] = E(nc.sbuf_tensor("stat63", [128, NT * 48], F32))
        c["veps"] = E(nc.sbuf_tensor("veps", [128, NT], F32))
        c["lnv"] = E(nc.sbuf_tensor("lnv", [128, NT], F32))

        names = ["s_w", "s_x0", "s_x1", "s_st6", "s_y", "s_tr", "s_trv", "s_mm",
                 "s_rt", "s_rv", "s_ract", "s_hw", "s_pst", "s_bev",
                 "s_xn0", "s_xn1", "s_bdma0", "s_bdma1",
                 "s_rst", "s_cc", "s_kv0", "s_kv1", "s_st", "s_exp", "s_avm",
                 "s_evac", "s_ctr", "s_ctv", "s_sq", "s_nb",
                 "s_kst0", "s_kst1", "s_mm2", "s_kv20", "s_kv21",
                 "s_wo0", "s_wo1", "s_ph", "s_dbg", "s_gw",
                 "s_bd20", "s_bd21", "s_xa0", "s_xa1", "s_pj"]
        c["sem"] = {n: E(nc.semaphore(n)) for n in names}
        for n in names:
            c["cnt"][n] = 0

        nph = [0]

        def barrier():
            nph[0] += 1
            tgt = 5 * nph[0]
            semh = c["sem"]["s_ph"]
            with nc.Block() as blk:
                for ename in ENGINES:
                    def mk(eng):
                        eng.drain().then_inc(semh, 1)
                        eng.wait_ge(semh, tgt)
                    getattr(blk, ename)(mk)

        with ExitStack() as s1:
            c["xfull"] = s1.enter_context(
                nc.sbuf_tensor("xfull1", [128, NT * D], F32))
            with nc.named_scope("ln1"):
                emit_prologue(c)
                emit_ln1(c)
                barrier()
            if debug:
                emit_debug_stats(c)
                barrier()
            alloc_ffn(c, s1)
            with nc.named_scope("ffn1"):
                emit_ffn(c, call=0)
        if debug:
            barrier()
            emit_debug_copy(c, c["x1d"], c["dbg_x1"], TOK * D * 4)
            barrier()
        with nc.named_scope("lnf2"):
            emit_ln_finish(c, idx=2)
            barrier()
        with ExitStack() as s2:
            with nc.named_scope("proj"):
                emit_projections(c, s2)
                barrier()
        if debug:
            emit_debug_cast(c, c["qtd"].rearrange("a p t -> (a p) t"), c["dbg_qt"],
                            NQH * 128, TOK)
            barrier()
        with ExitStack() as s3:
            c["ctxT"] = s3.enter_context(nc.sbuf_tensor("ctxT", [128, NJ * TOK], BF16))
            with ExitStack() as s3a:
                with nc.named_scope("attn"):
                    emit_attention(c, s3a)
                    barrier()
            if debug:
                emit_debug_ctx(c)
                barrier()
            with ExitStack() as s3b:
                with nc.named_scope("oproj"):
                    emit_oproj(c, s3b)
        if debug:
            barrier()
            emit_debug_copy(c, c["x2d"], c["dbg_x2"], TOK * D * 4)
            barrier()
        with nc.named_scope("lnf3"):
            emit_ln_finish(c, idx=3)
            barrier()
        with ExitStack() as s4:
            c["xfull"] = s4.enter_context(
                nc.sbuf_tensor("xfull2", [128, NT * D], F32))
            alloc_ffn(c, s4, tag="z")
            with nc.named_scope("ffn2"):
                emit_ffn(c, call=1)
    return nc


def emit_prologue(c):
    nc, sem = c["nc"], c["sem"]
    ph = Phase(c, "pro")
    g, s, v, sc = "gpsimd", "sync", "vector", "scalar"
    ph.op(g, lambda e: e.dma_start(c["at_sb"][:], c["at_d"], max_dma_last_dim=8192),
          inc="s_gw", by=16)
    ph.op(g, lambda e: e.dma_start(c["bfl_sb"][:], c["bfl_d"], max_dma_last_dim=8192),
          inc="s_gw", by=16)
    ph.op(g, lambda e: e.dma_start(c["rw_sb"][:], c["rw_d"], max_dma_last_dim=8192),
          inc="s_gw", by=16, mark="gall")
    ph.op(s, lambda e: e.dma_start(c["rb_sb"][:], c["rb_d"]), inc="s_w", by=16)
    ph.op(s, lambda e: e.dma_start(c["bv_sb"][:], c["bvc_d"]), inc="s_w", by=16)
    ph.op(s, lambda e: e.dma_start(c["idf_sb"][:], c["idf_d"]), inc="s_w", by=16)
    ph.op(s, lambda e: e.dma_start(c["idb_sb"][:], c["idb_d"]), inc="s_w", by=16)
    ph.op(s, lambda e: e.dma_start(c["e16_sb"][:], c["e16_d"]), inc="s_w", by=16,
          mark="all")
    ph.op(v, lambda e: e.memset(c["ones_row"][:], 1.0))
    ph.op(v, lambda e: e.memset(c["ones_col"][:], 1.0))
    ph.emit()


def _rstd_chain(ph, c, mv, rstd, tag):
    v, sc = "vector", "scalar"
    var_view = mv[:].rearrange("p (m two) -> p m two", two=2)[:, :, 1:2]

    ph.op(v, lambda e: e.tensor_scalar(c["veps"][:], var_view, EPS, None,
                                       op0=ALU.add), inc="s_rst", mark=f"{tag}_ve")
    ph.op(sc, lambda e: e.activation(c["lnv"][:], c["veps"][:], AF.Ln),
          waits=[W(f"{tag}_ve")])
    ph.op(sc, lambda e: e.drain())
    ph.op(sc, lambda e: e.activation(rstd[:], c["lnv"][:], AF.Exp, scale=-0.5),
          inc="s_rst", mark=f"{tag}_done")


def emit_ln1(c):
    nc, sem = c["nc"], c["sem"]
    xfull = c["xfull"]
    ph = Phase(c, "ln1")
    for m in range(NT):
        wl = [] if m < 2 else [W(("ln1x", m - 2))]
        ph.op("sync", (lambda e, m=m: e.dma_start(
            xfull[:, m * D:(m + 1) * D],
            c["x_in"][m * 128:(m + 1) * 128, :])), waits=wl,
            inc=f"s_x{m % 2}", by=16, mark=("ln1x", m))
    for m in range(NT):
        ph.op("vector", None, waits=[W(("ln1x", m))])
        for q in range(8):
            ph.op("vector", (lambda e, m=m, q=q: e.bn_stats(
                c["stat6"][:, 6 * q:6 * q + 6],
                xfull[:, m * D + q * 512:m * D + (q + 1) * 512])))
        ph.op("vector", lambda e: e.drain())
        ph.op("vector", (lambda e, m=m: e.bn_aggr(
            c["mv1"][:, 2 * m:2 * m + 2], c["stat6"][:])))
        ph.op("vector", lambda e: e.drain(), inc="s_st6", mark=("ln1", m))
    _rstd_chain(ph, c, c["mv1"], c["rstd1"], "ln1")
    for ename in ENGINES:
        ph.op(ename, None, waits=[W("all"), W("gall")])
    ph.emit()


def emit_ln_finish(c, idx):
    mv, rstd = c[f"mv{idx}"], c[f"rstd{idx}"]
    ph = Phase(c, f"lnf{idx}")
    for m in range(NT):
        ph.op("vector", (lambda e, m=m: e.bn_aggr(
            mv[:, 2 * m:2 * m + 2], c["stat63"][:, m * 48:(m + 1) * 48])))
    ph.op("vector", lambda e: e.drain())
    _rstd_chain(ph, c, mv, rstd, f"lnf{idx}")
    ph.emit()


def alloc_ffn(c, st, tag=""):
    nc = c["nc"]
    E = st.enter_context
    c["ytm"] = E(nc.sbuf_tensor(tag + "f_ytm", [128, 2 * D], BF16))
    c["y1t"] = E(nc.sbuf_tensor(tag + "f_y1t", [128, 2 * 1024], BF16))
    c["hw_sb"] = E(nc.sbuf_tensor(tag + "f_hw", [128, TOK], BF16))
    c["wexp_sb"] = E(nc.sbuf_tensor(tag + "f_wexp", [128, TOK], F32))
    c["logits_sb"] = E(nc.sbuf_tensor(tag + "f_logits", [NE, TOK], F32))
    c["ltok"] = E(nc.sbuf_tensor(tag + "f_ltok", [128, 8], F32))
    c["l1"] = E(nc.sbuf_tensor(tag + "f_l1", [128, 1], F32))
    c["l1n"] = E(nc.sbuf_tensor(tag + "f_l1n", [128, 1], F32))
    c["m1"] = E(nc.sbuf_tensor(tag + "f_m1", [128, 8], F32))
    c["tmp8"] = E(nc.sbuf_tensor(tag + "f_tmp8", [128, 8], F32))
    c["l2"] = E(nc.sbuf_tensor(tag + "f_l2", [128, 1], F32))
    c["mtop2"] = E(nc.sbuf_tensor(tag + "f_mtop2", [128, 8], F32))
    c["u_sb"] = E(nc.sbuf_tensor(tag + "f_u", [128, 8], F32))
    c["uw"] = E(nc.sbuf_tensor(tag + "f_uw", [128, 8], F32))
    c["ssum"] = E(nc.sbuf_tensor(tag + "f_ssum", [128, 1], F32))
    c["rr"] = E(nc.sbuf_tensor(tag + "f_rr", [128, 1], F32))
    c["wtok"] = E(nc.sbuf_tensor(tag + "f_wtok", [128, 64], F32))
    c["wT8"] = E(nc.sbuf_tensor(tag + "f_wT8", [NE, TOK], F32))
    c["ostage"] = E(nc.sbuf_tensor(tag + "f_ostage", [128, 2 * 1024], F32))
    c["hps"] = E(nc.psum_tensor(tag + "f_hps", [128, 1024], F32))
    c["rpsA"] = E(nc.psum_tensor(tag + "f_rpsA", [NE, 512], F32))
    c["rpsB"] = E(nc.psum_tensor(tag + "f_rpsB", [NE, 512], F32))
    c["wrp"] = E(nc.psum_tensor(tag + "f_wrp", [128, 1024], F32))
    c["tpsA"] = E(nc.psum_tensor(tag + "f_tpsA", [128, 1024], BF16))
    c["tpsB"] = E(nc.psum_tensor(tag + "f_tpsB", [128, 1024], BF16))


def emit_ffn(c, call):
    nc, sem = c["nc"], c["sem"]
    xfull = c["xfull"]
    sink = c["x1d"] if call == 0 else c["out_d"]
    mv = c["mv1"] if call == 0 else c["mv3"]
    rstd = c["rstd1"] if call == 0 else c["rstd3"]
    tps = [c["tpsA"], c["tpsB"]]
    rps = [c["rpsA"], c["rpsB"]]
    bps = [c["hps"][:, 0:512], c["hps"][:, 512:1024],
           c["wrp"][:, 0:512], c["wrp"][:, 512:1024]]
    bpt = [c["hps"], c["wrp"]]
    P = f"f{call}"
    ph = Phase(c, P)
    NBP = NT * 4
    t, v, sc, g, s = "tensor", "vector", "scalar", "gpsimd", "sync"

    if call == 1:
        for m in range(NT):
            wl = [] if m < 2 else [W((P, "x", m - 2))]
            ph.op(s, (lambda e, m=m: e.dma_start(
                xfull[:, m * D:(m + 1) * D],
                c["x2d"][m * 128:(m + 1) * 128, :])), waits=wl,
                inc=f"s_x{m % 2}", by=16, mark=(P, "x", m))

    def ynorm(m):
        wl = [W((P, "x", m))] if call == 1 else []
        if m >= 2:
            wl.append(W((P, "tr", (m - 2, 3))))
        ph.op(v, (lambda e, m=m: e.tensor_scalar(
            c["ytm"][:, (m % 2) * D:(m % 2 + 1) * D],
            xfull[:, m * D:(m + 1) * D],
            mv[:, 2 * m:2 * m + 1], rstd[:, m:m + 1],
            op0=ALU.subtract, op1=ALU.mult)), waits=wl, inc="s_y",
            mark=(P, "y", m))

    def chain(m):
        ph.op(v, (lambda e, m=m: e.tensor_scalar(
            c["logits_sb"][0:NE, m * 128:(m + 1) * 128],
            rps[m // 4][0:NE, (m % 4) * 128:(m % 4 + 1) * 128],
            c["rb_sb"][0:NE, 0:1], None, op0=ALU.add)),
            waits=[W((P, "mm", (m // 4) * 16 + 15))],
            inc="s_rv", mark=(P, "lg", m))
        ph.op(v, (lambda e, m=m: e.tensor_copy(c["ltok"][:], c["wrp"][:, 0:8])),
              waits=[W((P, "ltr", m))])
        ph.op(v, lambda e: e.drain())
        ph.op(v, lambda e: e.tensor_reduce(c["l1"][:], c["ltok"][:], AX.X, ALU.max))
        ph.op(v, lambda e: e.drain())
        ph.op(v, lambda e: e.tensor_scalar(c["l1n"][:], c["l1"][:], -1.0, None,
                                           op0=ALU.mult))
        ph.op(v, lambda e: e.drain(), inc="s_rv", mark=(P, "l1n", m))
        ph.op(v, lambda e: e.tensor_scalar(c["m1"][:], c["ltok"][:], c["l1"][:],
                                           None, op0=ALU.is_ge))
        ph.op(v, lambda e: e.drain())
        ph.op(v, lambda e: e.scalar_tensor_tensor(c["tmp8"][:], c["m1"][:], -1e30,
                                                  c["ltok"][:], op0=ALU.mult,
                                                  op1=ALU.add))
        ph.op(v, lambda e: e.drain())
        ph.op(v, lambda e: e.tensor_reduce(c["l2"][:], c["tmp8"][:], AX.X, ALU.max))
        ph.op(v, lambda e: e.drain())
        ph.op(v, lambda e: e.tensor_scalar(c["mtop2"][:], c["ltok"][:], c["l2"][:],
                                           None, op0=ALU.is_ge))
        ph.op(v, lambda e: e.drain())
        ph.op(v, (lambda e: e.tensor_tensor(c["uw"][:], c["u_sb"][:], c["mtop2"][:],
                                            op=ALU.mult)),
              waits=[W((P, "u", m))])
        ph.op(v, lambda e: e.drain())
        ph.op(v, lambda e: e.tensor_reduce(c["ssum"][:], c["uw"][:], AX.X, ALU.add))
        ph.op(v, lambda e: e.drain())
        ph.op(v, lambda e: e.reciprocal(c["rr"][:], c["ssum"][:]))
        ph.op(v, lambda e: e.drain())
        ph.op(v, (lambda e, m=m: e.tensor_scalar(
            c["wtok"][:, m * 8:(m + 1) * 8], c["uw"][:], c["rr"][:],
            None, op0=ALU.mult)))
        ph.op(v, lambda e: e.drain(), inc="s_rv", mark=(P, "w", m))

    ynorm(0)
    ynorm(1)
    for m in range(NT):
        for jg in range(4):
            kg = m * 4 + jg
            wl = [W((P, "tr", (m, jg)))]
            if kg >= 2:
                wl.append(W((P, "mm", kg - 2)))
            ph.op(v, (lambda e, kg=kg: e.tensor_copy(
                c["y1t"][:, (kg % 2) * 1024:(kg % 2 + 1) * 1024],
                tps[kg % 2][:, 0:1024])), waits=wl, inc="s_trv",
                mark=(P, "trv", kg))
        if m + 2 < NT:
            ynorm(m + 2)
        if m >= 4:
            chain(m - 4)
    for m in range(4, NT):
        chain(m)
    for m in range(NT):
        ph.op(v, (lambda e, m=m: e.tensor_copy(
            c["wT8"][0:NE, m * 128:(m + 1) * 128], c["wrp"][0:NE, 528:656])),
            waits=[W((P, "wtr", m))], inc="s_rv", mark=(P, "w8", m))
    ph.op(v, (lambda e: e.tensor_copy(c["wexp_sb"][:], c["wrp"][:, 0:1024])),
          waits=[W((P, "wexp"))])
    ph.op(v, lambda e: e.drain())
    ph.op(v, (lambda e: e.tensor_tensor(c["hw_sb"][:], c["hps"][:, 0:1024],
                                        c["wexp_sb"][:], op=ALU.mult)),
          inc="s_hw", mark=(P, "hw"))
    for p in range(NBP):
        m, n2 = p // 4, p % 4
        wl = [W((P, "b", 2 * p + 1))]
        if p >= 2:
            wl.append(W((P, "st", p - 2)))
        if call == 0 and p >= 1:
            _bn_pair(ph, c, P, p - 1)
            ph.op(v, lambda e: e.drain())
        ph.op(v, (lambda e, p=p, m=m, n2=n2: e.tensor_tensor(
            c["ostage"][:, (p % 2) * 1024:(p % 2 + 1) * 1024],
            bpt[p % 2][:, 0:1024],
            xfull[:, m * D + n2 * 1024:m * D + (n2 + 1) * 1024], op=ALU.add)),
            waits=wl)
        ph.op(v, lambda e: e.drain(), inc="s_bev", mark=(P, "bev", p))
    if call == 0:
        _bn_pair(ph, c, P, NBP - 1)

    for m in range(NT):
        for jg in range(4):
            kg = m * 4 + jg
            for jj in range(8):
                j = jg * 8 + jj
                wl = []
                if jj == 0:
                    if jg == 0:
                        wl.append(W((P, "y", m)))
                    if kg >= 2:
                        wl.append(W((P, "trv", kg - 2)))
                ph.op(t, (lambda e, m=m, j=j, jj=jj, kg=kg: e.transpose(
                    tps[kg % 2][:, jj * 128:(jj + 1) * 128],
                    c["ytm"][:, (m % 2) * D + j * 128:(m % 2) * D + (j + 1) * 128],
                    c["idb_sb"][:])), waits=wl,
                    inc=("s_tr" if jj == 7 else None),
                    mark=((P, "tr", (m, jg)) if jj == 7 else None))
            ph.op(t, None, waits=[W((P, "trv", kg))])
            for jj in range(8):
                j = jg * 8 + jj
                st_flag = (j == 0 and m % 4 == 0)
                ph.op(t, (lambda e, m=m, j=j, jj=jj, kg=kg, st_flag=st_flag: e.matmul(
                    c["hps"][:, m * 128:(m + 1) * 128],
                    c["at_sb"][:, j * ER:(j + 1) * ER],
                    c["y1t"][:, (kg % 2) * 1024 + jj * 128:(kg % 2) * 1024 + (jj + 1) * 128],
                    start=st_flag, stop=(j == NJ - 1), skip_group_check=True)))
                ph.op(t, (lambda e, m=m, j=j, jj=jj, kg=kg, st_flag=st_flag: e.matmul(
                    rps[m // 4][0:NE, (m % 4) * 128:(m % 4 + 1) * 128],
                    c["rw_sb"][:, j * NE:(j + 1) * NE],
                    c["y1t"][:, (kg % 2) * 1024 + jj * 128:(kg % 2) * 1024 + (jj + 1) * 128],
                    start=st_flag, stop=(j == NJ - 1), skip_group_check=True)),
                    inc=("s_mm" if jj == 7 else None),
                    mark=((P, "mm", kg) if jj == 7 else None))
        if m >= 4:
            _ltr(ph, c, P, m - 4)
    for m in range(4, NT):
        _ltr(ph, c, P, m)
    for m in range(NT):
        _wtr(ph, c, P, m)
    ph.op(t, (lambda e: e.matmul(c["wrp"][:, 0:512], c["e16_sb"][:],
                                 c["wT8"][0:NE, 0:512], start=True, stop=True)),
          waits=[W((P, "w8", NT - 1))])
    ph.op(t, (lambda e: e.matmul(c["wrp"][:, 512:1024], c["e16_sb"][:],
                                 c["wT8"][0:NE, 512:1024], start=True, stop=True)),
          inc="s_pst", mark=(P, "wexp"))
    for k in range(NT * 8):
        m, n = k // 8, k % 8
        wl = [W((P, "hw"))] if k == 0 else []
        if k >= 4:
            wl.append(W((P, "bev", k // 2 - 2)))
        ph.op(t, (lambda e, k=k, m=m, n=n: e.matmul(
            bps[k % 4],
            c["hw_sb"][:, m * 128:(m + 1) * 128],
            c["bfl_sb"][:, n * 512:(n + 1) * 512],
            start=True, stop=True)), waits=wl, inc="s_pst", mark=(P, "b", k))

    for m in range(NT):
        ph.op(sc, (lambda e, m=m: e.activation(
            c["u_sb"][:], c["ltok"][:], AF.Exp, bias=c["l1n"][:])),
            waits=[W((P, "l1n", m))], inc="s_ract", mark=(P, "u", m))

    for p in range(NBP):
        m, n2 = p // 4, p % 4
        ph.op(g, (lambda e, p=p, m=m, n2=n2: e.dma_start(
            sink[m * 128:(m + 1) * 128, n2 * 1024:(n2 + 1) * 1024],
            c["ostage"][:, (p % 2) * 1024:(p % 2 + 1) * 1024])),
            waits=[W((P, "bev", p))], inc=f"s_bdma{p % 2}", by=16,
            mark=(P, "st", p))
    ph.op(g, None, waits=[W((P, "st", NBP - 2)), W((P, "st", NBP - 1))])
    ph.emit()


def _bn_pair(ph, c, P, p):
    m, n2 = p // 4, p % 4
    ph.op("vector", (lambda e, p=p, m=m, n2=n2: e.bn_stats(
        c["stat63"][:, m * 48 + 2 * n2 * 6:m * 48 + (2 * n2 + 1) * 6],
        c["ostage"][:, (p % 2) * 1024:(p % 2) * 1024 + 512])))
    ph.op("vector", (lambda e, p=p, m=m, n2=n2: e.bn_stats(
        c["stat63"][:, m * 48 + (2 * n2 + 1) * 6:m * 48 + (2 * n2 + 2) * 6],
        c["ostage"][:, (p % 2) * 1024 + 512:(p % 2 + 1) * 1024])))


def _ltr(ph, c, P, m):
    ph.op("tensor", (lambda e, m=m: e.transpose(
        c["wrp"][:, 0:8], c["logits_sb"][0:NE, m * 128:(m + 1) * 128],
        c["idf_sb"][0:NE, 0:NE])),
        waits=[W((P, "lg", m))], inc="s_rt", mark=(P, "ltr", m))


def _wtr(ph, c, P, m):
    wl = [W((P, "w", m))]
    if m >= 1:
        wl.append(W((P, "w8", m - 1)))
    ph.op("tensor", (lambda e, m=m: e.transpose(
        c["wrp"][0:NE, 528:656], c["wtok"][:, m * 8:(m + 1) * 8],
        c["idf_sb"][:])),
        waits=wl, inc="s_rt", mark=(P, "wtr", m))


def emit_projections(c, st):
    nc, sem = c["nc"], c["sem"]
    E = st.enter_context
    y2T = E(nc.sbuf_tensor("p_y2T", [128, NJ * TOK], BF16))
    xring = E(nc.sbuf_tensor("p_xr", [128, 2 * D], F32))
    ytm = E(nc.sbuf_tensor("p_ytm", [128, 2 * D], BF16))
    wslab = E(nc.sbuf_tensor("p_ws", [128, 2 * D], BF16))
    kvst = E(nc.sbuf_tensor("p_kvst", [128, 2 * 1024], BF16))
    vtile = E(nc.sbuf_tensor("p_vt", [128, 1024], BF16))
    vstage = E(nc.sbuf_tensor("p_vs", [128, NT * 1032], BF16))
    tpsA = E(nc.psum_tensor("p_tpsA", [128, 1024], BF16))
    tpsB = E(nc.psum_tensor("p_tpsB", [128, 1024], BF16))
    pps = [E(nc.psum_tensor("p_pps0", [128, 1024], F32)),
           E(nc.psum_tensor("p_pps1", [128, 1024], F32))]
    tps = [tpsA, tpsB]
    P = "pj"
    ph = Phase(c, P)
    t, v, g, s = "tensor", "vector", "gpsimd", "sync"
    NPROJ = 48
    NK = NT * NJ

    def slab_src(i):
        if i < 8:
            return c["wk_d"][i, :, :]
        if i < 16:
            return c["wv_d"][i - 8, :, :]
        return c["wq_d"][i - 16, :, :]

    for m in range(NT):
        wl = [] if m < 2 else [W((P, "y", m - 2))]
        ph.op(s, (lambda e, m=m: e.dma_start(
            xring[:, (m % 2) * D:(m % 2 + 1) * D],
            c["x1d"][m * 128:(m + 1) * 128, :])), waits=wl, inc=f"s_x{m % 2}",
            by=16, mark=(P, "x", m))

    ph.op(v, (lambda e: e.memset(
        vstage[:].rearrange("p (h t f) -> p h t f", h=NKV, f=129)[:, :, :, 128:129],
        1.0)), inc="s_rv", mark=(P, "ones"))

    def ynorm(m):
        wl = [W((P, "x", m))]
        if m >= 2:
            wl.append(W((P, "tr", (m - 2, 3))))
        ph.op(v, (lambda e, m=m: e.tensor_scalar(
            ytm[:, (m % 2) * D:(m % 2 + 1) * D],
            xring[:, (m % 2) * D:(m % 2 + 1) * D],
            c["mv2"][:, 2 * m:2 * m + 1], c["rstd2"][:, m:m + 1],
            op0=ALU.subtract, op1=ALU.mult)), waits=wl, inc="s_y", mark=(P, "y", m))

    ynorm(0)
    ynorm(1)
    y2T_v = y2T[:].rearrange("p (j t) -> p j t", t=TOK)
    for m in range(NT):
        for jg in range(4):
            kg = m * 4 + jg
            ph.op(v, (lambda e, m=m, jg=jg, kg=kg: e.tensor_copy(
                y2T_v[:, jg * 8:(jg + 1) * 8, m * 128:(m + 1) * 128],
                tps[kg % 2][:].rearrange("p (a b) -> p a b", b=128))),
                waits=[W((P, "tr", (m, jg)))], inc="s_trv", mark=(P, "trv", kg))
        if m + 2 < NT:
            ynorm(m + 2)
    for i in range(NPROJ):
        if i < 8:
            wl = [W((P, "mm2", i))]
            if i >= 2:
                wl.append(W((P, "kst", i - 2)))
            ph.op(v, (lambda e, i=i: e.tensor_copy(
                kvst[:, (i % 2) * 1024:(i % 2 + 1) * 1024], pps[i % 2][:, 0:1024])),
                waits=wl, inc="s_rv", mark=(P, "ev", i))
        elif i < 16:
            vc = i - 8
            ph.op(v, (lambda e, i=i, vc=vc: e.tensor_scalar(
                vtile[:], pps[i % 2][:, 0:1024], c["bv_sb"][:, vc:vc + 1], None,
                op0=ALU.add)), waits=[W((P, "mm2", i))],
                inc="s_rv", mark=(P, "ev", i))
            kgv = 32 + vc
            vst_v = vstage[:].rearrange("p (h t f) -> p h t f", h=NKV, f=129)
            ph.op(v, (lambda e, vc=vc, kgv=kgv, vst_v=vst_v: e.tensor_copy(
                vst_v[:, vc, :, 0:128],
                tps[kgv % 2][:].rearrange("p (a b) -> p a b", b=128))),
                waits=[W((P, "tr", (8, vc)))], inc="s_trv", mark=(P, "trv", kgv))
        else:
            qi = i - 16
            wl = [W((P, "mm2", i))]
            if qi >= 2:
                wl.append(W((P, "qst", qi - 2)))
            else:
                wl.append(W((P, "kvdone")))
            ph.op(v, (lambda e, i=i: e.tensor_copy(
                kvst[:, (i % 2) * 1024:(i % 2 + 1) * 1024], pps[i % 2][:, 0:1024])),
                waits=wl, inc="s_rv", mark=(P, "ev", i))

    for m in range(NT):
        for jg in range(4):
            kg = m * 4 + jg
            for jj in range(8):
                j = jg * 8 + jj
                wl = []
                if jj == 0:
                    if jg == 0:
                        wl.append(W((P, "y", m)))
                    if kg >= 2:
                        wl.append(W((P, "trv", kg - 2)))
                ph.op(t, (lambda e, m=m, j=j, jj=jj, kg=kg: e.transpose(
                    tps[kg % 2][:, jj * 128:(jj + 1) * 128],
                    ytm[:, (m % 2) * D + j * 128:(m % 2) * D + (j + 1) * 128],
                    c["idb_sb"][:])), waits=wl,
                    inc=("s_tr" if jj == 7 else None),
                    mark=((P, "tr", (m, jg)) if jj == 7 else None))
    for i in range(NPROJ):
        wl = [W((P, "kv2", i))]
        if i == 0:
            wl.append(W((P, "trv", 31)))
        if i >= 2:
            wl.append(W((P, "ev", i - 2)))
        ph.op(t, None, waits=wl)
        for j in range(NJ):
            for hh in range(2):
                last = (j == NJ - 1 and hh == 1)
                ph.op(t, (lambda e, i=i, j=j, hh=hh: e.matmul(
                    pps[i % 2][:, hh * 512:(hh + 1) * 512],
                    wslab[:, (i % 2) * D + j * 128:(i % 2) * D + (j + 1) * 128],
                    y2T[:, j * TOK + hh * 512:j * TOK + (hh + 1) * 512],
                    start=(j == 0), stop=(j == NJ - 1), skip_group_check=True)),
                    inc=("s_mm2" if last else None),
                    mark=((P, "mm2", i) if last else None))
        if 8 <= i < 16:
            vc = i - 8
            kgv = 32 + vc
            for t_ in range(NT):
                wl = []
                if t_ == 0:
                    wl.append(W((P, "ev", i)))
                    if kgv >= 2:
                        wl.append(W((P, "trv", kgv - 2)))
                ph.op(t, (lambda e, t_=t_, kgv=kgv: e.transpose(
                    tps[kgv % 2][:, t_ * 128:(t_ + 1) * 128],
                    vtile[:, t_ * 128:(t_ + 1) * 128],
                    c["idb_sb"][:])), waits=wl,
                    inc=("s_tr" if t_ == NT - 1 else None),
                    mark=((P, "tr", (8, vc)) if t_ == NT - 1 else None))

    for i in range(NPROJ):
        wl = [] if i < 2 else [W((P, "mm2", i - 2))]
        ph.op(g, (lambda e, i=i: e.dma_start(
            wslab[:, (i % 2) * D:(i % 2 + 1) * D], slab_src(i),
            max_dma_last_dim=8192)), waits=wl, inc=f"s_kv2{i % 2}", by=16,
            mark=(P, "kv2", i))
        if i == 18:
            ph.op(g, (lambda e: e.collective_compute(
                "AllGather", ALU.bypass, replica_groups=[list(range(R))],
                ins=[c["kvd"].opt()], outs=[c["kvg"].opt()])),
                waits=[W((P, "kvdone"))],
                inc="s_cc", mark=(P, "ccv"))

    for i in range(8):
        ph.op(s, (lambda e, i=i: e.dma_start(
            c["kvd"][i * 128:(i + 1) * 128, 0:TOK],
            kvst[:, (i % 2) * 1024:(i % 2 + 1) * 1024])),
            waits=[W((P, "ev", i))], inc=f"s_kst{i % 2}", by=16,
            mark=(P, "kst", i))
    for t_ in range(NT):
        wl = [W((P, "trv", 39)), W((P, "ones"))]
        if t_ >= 2:
            wl.append(W((P, "vst", t_ - 2)))
        else:
            wl.append(W((P, "kst", 6 + t_)))
        ph.op(s, (lambda e, t_=t_: e.dma_start(
            c["kvd"][t_ * 128:(t_ + 1) * 128, TOK:TOK + NKV * 129],
            vstage[:, t_ * 1032:(t_ + 1) * 1032])), waits=wl,
            inc=f"s_kst{t_ % 2}", by=16, mark=(P, "vst", t_))
    ph.op(s, None, waits=[W((P, "vst", NT - 2)), W((P, "vst", NT - 1))],
          inc="s_pj", mark=(P, "kvdone"))
    for qi in range(32):
        ph.op(s, (lambda e, qi=qi: e.dma_start(
            c["qtd"][qi, :, :], kvst[:, ((16 + qi) % 2) * 1024:((16 + qi) % 2 + 1) * 1024])),
            waits=[W((P, "ev", 16 + qi))], inc=f"s_kst{qi % 2}", by=16,
            mark=(P, "qst", qi))
    ph.op(s, None, waits=[W((P, "qst", 30)), W((P, "qst", 31))])
    ph.emit()


def emit_attention(c, st):
    nc, sem = c["nc"], c["sem"]
    E = st.enter_context
    kT = E(nc.sbuf_tensor("a_kT", [128, 2 * S], BF16))
    vh = E(nc.sbuf_tensor("a_vh", [128, 2 * NCH * 129], BF16))
    qT = E(nc.sbuf_tensor("a_qT", [128, 2 * G4 * TOK], BF16))
    ptil = E(nc.sbuf_tensor("a_ptil", [128, 3 * 1536], BF16))
    ksq = E(nc.sbuf_tensor("a_ksq", [128, S], BF16))
    qsq = E(nc.sbuf_tensor("a_qsq", [128, G4 * TOK], BF16))
    cstage = E(nc.sbuf_tensor("a_cs", [128, 4 * 128], BF16))
    negbh = E(nc.sbuf_tensor("a_nb", [128, G4], F32))
    nt1 = E(nc.sbuf_tensor("a_nt1", [128, 1], F32))
    ntk = E(nc.sbuf_tensor("a_ntk", [1, 1], F32))
    ntq = E(nc.sbuf_tensor("a_ntq", [1, 1], F32))
    nts = E(nc.sbuf_tensor("a_nts", [1, 1], F32))
    ntb = E(nc.sbuf_tensor("a_ntb", [1, 1], BF16))
    rcol = E(nc.sbuf_tensor("a_rc", [128, 4], F32))
    stA = E(nc.psum_tensor("a_stA", [128, 1536], F32))
    stB = E(nc.psum_tensor("a_stB", [128, 1536], F32))
    av01 = E(nc.psum_tensor("a_av01", [128, 512], F32))
    av23 = E(nc.psum_tensor("a_av23", [128, 512], F32))
    stX = [stA, stB]
    avX = [av01, av01, av23, av23]
    avo = [0, 129, 0, 129]
    P = "at"
    ph = Phase(c, P)
    t, v, sc, g, s = "tensor", "vector", "scalar", "gpsimd", "sync"
    BSC = ATT_SCALE * BH_MARGIN * 0.5

    GSIZES = [3] * 20 + [2, 2]
    GPE = len(GSIZES)
    GOFF = [0]
    for z in GSIZES:
        GOFF.append(GOFF[-1] + z)

    def ep(h, gq, s_):
        return (h * G4 + gq) * 2 + s_

    for h in range(R):
        par = h % 2
        wl = [W(("pj", "ccv"))]
        if h >= 2:
            wl.append(W((P, "avm", (ep(h - 2, G4 - 1, 1) + 1) * GPE - 1)))
        ph.op(s, None, waits=wl)
        for r in range(R):
            ph.op(s, (lambda e, h=h, r=r, par=par: e.dma_start(
                kT[:, par * S + r * TOK:par * S + (r + 1) * TOK],
                c["kvg"][r * NKV * 128 + h * 128:
                         r * NKV * 128 + (h + 1) * 128, 0:TOK])),
                inc=f"s_kv{h % 2}", by=16)
        for r in range(R):
            ph.op(s, (lambda e, h=h, r=r, par=par: e.dma_start(
                vh[:, par * NCH * 129 + r * 8 * 129:
                   par * NCH * 129 + (r + 1) * 8 * 129],
                c["kvg"][r * NKV * 128 + h * 128:
                         r * NKV * 128 + (h + 1) * 128, TOK:TOK + NKV * 129])),
                inc=f"s_kv{h % 2}", by=16)
        for gq in range(G4):
            mk = (P, "kv", h) if gq == G4 - 1 else None
            ph.op(s, (lambda e, h=h, gq=gq, par=par: e.dma_start(
                qT[:, par * G4 * TOK + gq * TOK:par * G4 * TOK + (gq + 1) * TOK],
                c["qtd"][h * G4 + gq, :, :])), inc=f"s_kv{h % 2}", by=16, mark=mk)

    for n in range(8):
        wl = [W((P, "kv", 0))] if n == 0 else (
            [] if n < 2 else [W((P, "wocast", n - 2))])
        ph.op(g, (lambda e, n=n: e.dma_start(
            c["wo_bf"][n], c["wo_d"][n], max_dma_last_dim=8192)), waits=wl,
            inc=f"s_xn{n % 2}", by=16, mark=(P, "wocast", n))

    avq = []

    def _av_pop(desc):
        i, gi, par, gh, s_ = desc
        Gi = i * GPE + gi
        sz = GSIZES[gi]
        wl = [W((P, "exp", Gi))]
        if gi == 0 and i >= 1:
            wl.append(W((P, "ctv", i - 1)))
        nmm = 0
        tot = sz * 4
        for cpos in range(sz):
            ch = GOFF[gi] + cpos
            for q in range(4):
                st_flag = (gi == 0 and cpos == 0 and (q % 2 == 0))
                nmm += 1
                ph.op(t, (lambda e, Gi=Gi, ch=ch, cpos=cpos, q=q, par=par,
                          st_flag=st_flag: e.matmul(
                    avX[q][:, avo[q]:avo[q] + 129],
                    ptil[:, (Gi % 3) * 1536 + cpos * 512 + q * 128:
                         (Gi % 3) * 1536 + cpos * 512 + (q + 1) * 128],
                    vh[:, par * NCH * 129 + ch * 129:par * NCH * 129 + (ch + 1) * 129],
                    start=st_flag, stop=False, skip_group_check=True)),
                    waits=wl if (cpos == 0 and q == 0) else (),
                    inc=("s_avm" if nmm == tot else None),
                    mark=((P, "avm", Gi) if nmm == tot else None))
        if gi == GPE - 1:
            trt = av01[:].bitcast(BF16)
            for q in range(4):
                ph.op(t, (lambda e, q=q, trt=trt: e.transpose(
                    trt[:, q * 128:(q + 1) * 128],
                    cstage[:, q * 128:(q + 1) * 128],
                    c["idb_sb"][:])),
                    waits=[W((P, "cs", i))] if q == 0 else (),
                    inc=("s_ctr" if q == 3 else None),
                    mark=((P, "ctr", i) if q == 3 else None))

    def squares(h):
        par = h % 2
        wl = [W((P, "kv", h))]
        if h >= 1:
            wl.append(W((P, "norm", h - 1)))
        ph.op(v, (lambda e, par=par: e.tensor_tensor(
            ksq[:], kT[:, par * S:(par + 1) * S], kT[:, par * S:(par + 1) * S],
            op=ALU.mult)), waits=wl)
        ph.op(v, (lambda e, par=par: e.tensor_tensor(
            qsq[:], qT[:, par * G4 * TOK:(par + 1) * G4 * TOK],
            qT[:, par * G4 * TOK:(par + 1) * G4 * TOK], op=ALU.mult)),
            inc="s_sq", mark=(P, "sq", h))

    squares(0)
    for h in range(R):
        par = h % 2
        gh0 = h * G4
        while avq:
            _av_pop(avq.pop(0))
        wl = [W((P, "sq", h))]
        if h >= 1:
            lastg = (ep(h - 1, G4 - 1, 1) + 1) * GPE - 1
            wl.append(W((P, "exp", lastg)))
            wl.append(W((P, "nb", (h - 1, G4 - 1))))
        ph.op(t, None, waits=wl)
        for cc in range(NCH):
            ph.op(t, (lambda e, cc=cc, first=(cc == 0): e.matmul(
                stA[:, cc:cc + 1], ksq[:, cc * 128:(cc + 1) * 128],
                c["ones_col"][:], start=first, stop=False, skip_group_check=True)))
        for gq in range(G4):
            for t_ in range(NT):
                last = (gq == G4 - 1 and t_ == NT - 1)
                ph.op(t, (lambda e, gq=gq, t_=t_: e.matmul(
                    stA[:, 64 + gq * NT + t_:64 + gq * NT + t_ + 1],
                    qsq[:, gq * TOK + t_ * 128:gq * TOK + (t_ + 1) * 128],
                    c["ones_col"][:], start=False, stop=False,
                    skip_group_check=True)),
                    inc=("s_st" if last else None),
                    mark=((P, "norm", h) if last else None))
        ph.op(v, (lambda e: e.tensor_reduce(nt1[:], stA[:, 0:64], AX.X, ALU.max)),
              waits=[W((P, "norm", h))])
        ph.op(v, lambda e: e.drain(), inc="s_sq", mark=(P, "kcol", h))
        ph.op(t, (lambda e: e.transpose(stA[0:1, 96:224], nt1[:], c["idf_sb"][:])),
              waits=[W((P, "kcol", h))], inc="s_st", mark=(P, "ktr", h))
        ph.op(v, (lambda e: e.tensor_reduce(ntk[0:1, 0:1], stA[0:1, 96:224],
                                            AX.X, ALU.max)),
              waits=[W((P, "ktr", h))])
        ph.op(v, lambda e: e.drain())
        for gq in range(G4):
            ph.op(v, (lambda e, gq=gq: e.tensor_reduce(
                nt1[:], stA[:, 64 + gq * NT:64 + (gq + 1) * NT], AX.X, ALU.max)))
            ph.op(v, lambda e: e.drain(), inc="s_sq", mark=(P, "qcol", (h, gq)))
            ph.op(t, (lambda e: e.transpose(stA[0:1, 224:352], nt1[:],
                                            c["idf_sb"][:])),
                  waits=[W((P, "qcol", (h, gq)))], inc="s_st",
                  mark=(P, "qtr", (h, gq)))
            ph.op(v, (lambda e: e.tensor_reduce(ntq[0:1, 0:1], stA[0:1, 224:352],
                                                AX.X, ALU.max)),
                  waits=[W((P, "qtr", (h, gq)))])
            ph.op(v, lambda e: e.drain())
            ph.op(v, (lambda e: e.tensor_tensor(nts[0:1, 0:1], ntq[0:1, 0:1],
                                                ntk[0:1, 0:1], op=ALU.add)))
            ph.op(v, lambda e: e.drain())
            ph.op(v, (lambda e: e.tensor_scalar(ntb[0:1, 0:1], nts[0:1, 0:1],
                                                BSC, None, op0=ALU.mult)))
            ph.op(v, lambda e: e.drain(), inc="s_sq", mark=(P, "ntb", (h, gq)))
            ph.op(t, (lambda e, gq=gq: e.matmul(
                stA[:, 480 + gq:481 + gq], c["ones_row"][:], ntb[0:1, 0:1],
                start=False, stop=False, skip_group_check=True)),
                waits=[W((P, "ntb", (h, gq)))], inc="s_st",
                mark=(P, "bc", (h, gq)))
            ph.op(v, (lambda e, gq=gq: e.tensor_scalar(
                negbh[:, gq:gq + 1], stA[:, 480 + gq:481 + gq], -1.0, None,
                op0=ALU.mult)),
                waits=[W((P, "bc", (h, gq)))])
            ph.op(v, lambda e: e.drain(), inc="s_nb", mark=(P, "nb", (h, gq)))

        for gq in range(G4):
            for s_ in range(2):
                i = ep(h, gq, s_)
                for gi in range(GPE):
                    Gi = i * GPE + gi
                    sz = GSIZES[gi]
                    wl = []
                    if Gi >= 2:
                        wl.append(W((P, "exp", Gi - 2)))
                    if gi == 0:
                        wl.append(W((P, "nb", (h, G4 - 1))))
                        wl.append(W((P, "kv", h)))
                    for cpos in range(sz):
                        ch = GOFF[gi] + cpos
                        last = (cpos == sz - 1)
                        ph.op(t, (lambda e, Gi=Gi, ch=ch, cpos=cpos, par=par,
                                  gq=gq, s_=s_: e.matmul(
                            stX[Gi % 2][:, cpos * 512:(cpos + 1) * 512],
                            kT[:, par * S + ch * 128:par * S + (ch + 1) * 128],
                            qT[:, par * G4 * TOK + gq * TOK + s_ * 512:
                               par * G4 * TOK + gq * TOK + s_ * 512 + 512],
                            start=True, stop=True)),
                            waits=wl if cpos == 0 else (),
                            inc=("s_st" if last else None),
                            mark=((P, "stg", Gi) if last else None))
                    avq.append((i, gi, par, gh0 + gq, s_))
                    if len(avq) > 2:
                        _av_pop(avq.pop(0))
                ph.op(v, lambda e: e.drain(),
                      waits=([W((P, "avm", (i + 1) * GPE - 1))] +
                             ([W((P, "ctr", i - 1))] if i >= 1 else [])))
                ph.op(v, lambda e: e.reciprocal(rcol[:, 0:1], av01[:, 128:129]))
                ph.op(v, lambda e: e.reciprocal(rcol[:, 1:2], av01[:, 257:258]))
                ph.op(v, lambda e: e.reciprocal(rcol[:, 2:3], av23[:, 128:129]))
                ph.op(v, lambda e: e.reciprocal(rcol[:, 3:4], av23[:, 257:258]))
                ph.op(v, lambda e: e.drain())
                for q in range(4):
                    ph.op(v, (lambda e, q=q: e.tensor_scalar(
                        cstage[:, q * 128:(q + 1) * 128],
                        avX[q][:, avo[q]:avo[q] + 128], rcol[:, q:q + 1], None,
                        op0=ALU.mult)),
                        inc=("s_evac" if q == 3 else None),
                        mark=((P, "cs", i) if q == 3 else None))
                gh = gh0 + gq
                ph.op(v, (lambda e, gh=gh, s_=s_: e.tensor_copy(
                    c["ctxT"][:, gh * TOK + s_ * 512:gh * TOK + (s_ + 1) * 512],
                    av01[:].bitcast(BF16)[:, 0:512])),
                    waits=[W((P, "ctr", i))], inc="s_ctv", mark=(P, "ctv", i))
                if gq == G4 - 1 and s_ == 0 and h + 1 < R:
                    squares(h + 1)

        for gq in range(G4):
            for s_ in range(2):
                i = ep(h, gq, s_)
                for gi in range(GPE):
                    Gi = i * GPE + gi
                    sz = GSIZES[gi]
                    wl = [W((P, "stg", Gi))]
                    if Gi >= 3:
                        wl.append(W((P, "avm", Gi - 3)))
                    if gi == 0:
                        wl.append(W((P, "nb", (h, gq))))
                    ph.op(sc, (lambda e, Gi=Gi, gq=gq, sz=sz: e.activation(
                        ptil[:, (Gi % 3) * 1536:(Gi % 3) * 1536 + sz * 512],
                        stX[Gi % 2][:, 0:sz * 512], AF.Exp,
                        bias=negbh[:, gq:gq + 1], scale=ATT_SCALE)),
                        waits=wl, inc="s_exp", mark=(P, "exp", Gi))
    while avq:
        _av_pop(avq.pop(0))
    ph.emit()



def emit_oproj(c, st):
    nc, sem = c["nc"], c["sem"]
    E = st.enter_context
    woslab = E(nc.sbuf_tensor("o_ws", [128, 2 * NJ * 512], BF16))
    x1ring = E(nc.sbuf_tensor("o_x1", [128, 2 * 512], F32))
    x2st = E(nc.sbuf_tensor("o_x2", [128, 2 * 512], F32))
    ops = [E(nc.psum_tensor("o_op0", [128, 512], F32)),
           E(nc.psum_tensor("o_op1", [128, 512], F32))]
    P = "op"
    ph = Phase(c, P)
    t, v, g, s = "tensor", "vector", "gpsimd", "sync"
    NB = 64

    for n in range(8):
        wl = [W(("at", "wocast", n))]
        if n >= 2:
            wl.append(W((P, "mm", (n - 2) * 8 + 7)))
        ph.op("scalar", (lambda e, n=n: e.dma_start(
            woslab[:, (n % 2) * NJ * 512:(n % 2 + 1) * NJ * 512]
            .rearrange("p (j f) -> p j f", f=512),
            c["wo_bf"][n, :, :, :].rearrange("j pp f -> pp j f"))), waits=wl,
            inc=f"s_wo{n % 2}", by=16, mark=(P, "wo", n))

    for k in range(NB + 2):
        if k < NB:
            n, m = k // 8, k % 8
            wl = [] if k < 2 else [W((P, "bev", k - 2))]
            ph.op(s, (lambda e, n=n, m=m, k=k: e.dma_start(
                x1ring[:, (k % 2) * 512:(k % 2 + 1) * 512],
                c["x1d"][m * 128:(m + 1) * 128, n * 512:(n + 1) * 512])),
                waits=wl, inc=f"s_xa{k % 2}", by=16, mark=(P, "x1", k))
        if k >= 2:
            k2 = k - 2
            n2, m2 = k2 // 8, k2 % 8
            ph.op(s, (lambda e, n2=n2, m2=m2, k2=k2: e.dma_start(
                c["x2d"][m2 * 128:(m2 + 1) * 128, n2 * 512:(n2 + 1) * 512],
                x2st[:, (k2 % 2) * 512:(k2 % 2 + 1) * 512])),
                waits=[W((P, "bev", k2))], inc=f"s_bd2{k2 % 2}", by=16,
                mark=(P, "st", k2))
    ph.op(s, None, waits=[W((P, "st", NB - 2)), W((P, "st", NB - 1))])

    for n in range(8):
        for m in range(8):
            k = n * 8 + m
            wl = [W((P, "wo", n))] if m == 0 else []
            if k >= 2:
                wl.append(W((P, "bev", k - 2)))
            ph.op(t, None, waits=wl)
            for j in range(NJ):
                last = (j == NJ - 1)
                ph.op(t, (lambda e, n=n, m=m, j=j, k=k: e.matmul(
                    ops[k % 2][:, 0:512],
                    c["ctxT"][:, j * TOK + m * 128:j * TOK + (m + 1) * 128],
                    woslab[:, (n % 2) * NJ * 512 + j * 512:
                           (n % 2) * NJ * 512 + (j + 1) * 512],
                    start=(j == 0), stop=(j == NJ - 1))),
                    inc=("s_pst" if last else None),
                    mark=((P, "mm", k) if last else None))

    for k in range(NB):
        n, m = k // 8, k % 8
        wl = [W((P, "mm", k)), W((P, "x1", k))]
        if k >= 2:
            wl.append(W((P, "st", k - 2)))
        ph.op(v, (lambda e, k=k: e.tensor_tensor(
            x2st[:, (k % 2) * 512:(k % 2 + 1) * 512], ops[k % 2][:, 0:512],
            x1ring[:, (k % 2) * 512:(k % 2 + 1) * 512], op=ALU.add)), waits=wl)
        ph.op(v, lambda e: e.drain())
        ph.op(v, (lambda e, n=n, m=m, k=k: e.bn_stats(
            c["stat63"][:, m * 48 + n * 6:m * 48 + (n + 1) * 6],
            x2st[:, (k % 2) * 512:(k % 2 + 1) * 512])), inc="s_bev",
            mark=(P, "bev", k))
    ph.emit()


def emit_debug_stats(c):
    ph = Phase(c, "dbgs")
    k0 = c["cnt"]["s_dbg"]
    ph.op("gpsimd", (lambda e: e.dma_start(c["dbg_mv"], c["mv1"][:])),
          inc="s_dbg", by=16)
    ph.op("gpsimd", (lambda e: e.dma_start(c["dbg_rs"], c["rstd1"][:])),
          inc="s_dbg", by=16)
    ph.op("gpsimd", (lambda e: e.dma_start(c["dbg_ve"], c["veps"][:])),
          inc="s_dbg", by=16)
    ph.op("gpsimd", None, waits=[WV("s_dbg", c["cnt"]["s_dbg"])])
    ph.emit()


def emit_debug_copy(c, src, dst, nbytes):
    ph = Phase(c, "dbg")
    ph.op("gpsimd", (lambda e: e.dma_start(dst, src)), inc="s_dbg", by=16)
    ph.op("gpsimd", None, waits=[WV("s_dbg", c["cnt"]["s_dbg"])])
    ph.emit()


DBG_N = [0]


def emit_debug_cast(c, src_bf, dst_f32, rows, cols):
    nc = c["nc"]
    ph = Phase(c, "dbgc")
    DBG_N[0] += 1
    with ExitStack() as st:
        buf = st.enter_context(nc.sbuf_tensor(f"dbgbuf{DBG_N[0]}", [128, cols], F32))
        for rr in range(rows // 128):
            k0 = c["cnt"]["s_dbg"]
            ph.op("gpsimd", (lambda e, rr=rr: e.dma_start(
                buf[:], src_bf[rr * 128:(rr + 1) * 128, :], max_dma_last_dim=2048)),
                inc="s_dbg", by=16)
            ph.op("gpsimd", None, waits=[WV("s_dbg", k0 + 16)])
            ph.op("gpsimd", (lambda e, rr=rr: e.dma_start(
                dst_f32[rr * 128:(rr + 1) * 128, :], buf[:])), inc="s_dbg", by=16)
            k1 = c["cnt"]["s_dbg"]
            ph.op("gpsimd", None, waits=[WV("s_dbg", k1)])
        ph.emit()


def emit_debug_ctx(c):
    nc = c["nc"]
    ph = Phase(c, "dbgx")
    with ExitStack() as st:
        buf = st.enter_context(nc.sbuf_tensor("dbgbufx", [128, TOK], F32))
        for j in range(NJ):
            k0 = c["cnt"]["s_dbg"]
            ph.op("gpsimd", (lambda e, j=j: e.tensor_copy(
                buf[:], c["ctxT"][:, j * TOK:(j + 1) * TOK])))
            ph.op("gpsimd", (lambda e, j=j: e.dma_start(
                c["dbg_ctx"][j * 128:(j + 1) * 128, :], buf[:])), inc="s_dbg", by=16)
            ph.op("gpsimd", None, waits=[WV("s_dbg", k0 + 16)])
        ph.emit()


def kernel(**inputs):
    img = host_images(inputs)
    X = np.asarray(inputs["L0_attn_hidden_states"], np.float32).reshape(S, D)

    nc = build_nc(debug=False)
    in_maps = []
    for cid in range(R):
        in_maps.append({
            "x": np.ascontiguousarray(X[cid * TOK:(cid + 1) * TOK, :]),
            "wq": img["wq"], "wk": img["wk"], "wv": img["wv"], "wo": img["wo"],
            "at": img["at"], "bfl": img["bfl"], "rw": img["rw"], "rb": img["rb"],
            "bvc": img["bvc"], "idf": img["idf"], "idb": img["idb"],
            "e16": img["e16"],
        })
    res = run_bass_kernel_spmd(nc, in_maps, core_ids=list(range(R)))
    out = np.concatenate([res.results[i]["out"] for i in range(R)], axis=0)
    return out.reshape(1, S, D)


# version 14 (final)
# speedup vs baseline: 1.0720x; 1.0053x over previous
import numpy as np
import ml_dtypes
import concourse.bass as bass
import concourse.mybir as mybir
from concourse.bass_utils import run_bass_kernel_spmd
from contextlib import ExitStack

F32 = mybir.dt.float32
BF16 = mybir.dt.bfloat16
AF = mybir.ActivationFunctionType
ALU = mybir.AluOpType
AX = mybir.AxisListType

R = 8
S = 8192
D = 4096
TOK = S // R
NT = TOK // 128
NJ = D // 128
NQH, NKV, G4, HD = 32, 8, 4, 128
EPS = 1e-5
ATT_SCALE = float(HD) ** -0.5
BH_MARGIN = 1.04
NE, RK = 8, 16
ER = NE * RK
NCH = S // 128
NGRP = NCH // 2
NEPOCH = R * G4 * 2
TOTG = NEPOCH * NGRP
BF = ml_dtypes.bfloat16
ENGINES = ["tensor", "vector", "scalar", "gpsimd", "sync"]


def _lhsT_image(W):
    dout, din = W.shape
    OC, J = dout // 128, din // 128
    W4 = W.reshape(OC, 128, J, 128)
    return np.ascontiguousarray(W4.transpose(0, 3, 2, 1)).reshape(OC, 128, J * 128)


def host_images(inp):
    Wq = np.asarray(inp["L0_attn_Wq"], np.float32)
    Wk = np.asarray(inp["L0_attn_Wk"], np.float32)
    Wv = np.asarray(inp["L0_attn_Wv"], np.float32)
    Wo = np.asarray(inp["L0_attn_Wo"], np.float32)
    bv = np.asarray(inp["L0_attn_bv"], np.float32)
    rw = np.asarray(inp["L0_ffn_router_w"], np.float32)
    rb = np.asarray(inp["L0_ffn_router_b"], np.float32)
    A = np.asarray(inp["L0_ffn_A"], np.float32)
    Bm = np.asarray(inp["L0_ffn_Bm"], np.float32)

    img = {}
    img["wq"] = _lhsT_image(Wq)
    img["wk"] = _lhsT_image(Wk)
    img["wv"] = _lhsT_image(Wv)
    Wo4 = Wo.reshape(8, 512, NJ, 128)
    img["wo"] = np.ascontiguousarray(Wo4.transpose(0, 2, 3, 1))
    Aflat = A.reshape(ER, D)
    img["at"] = np.ascontiguousarray(
        Aflat.reshape(ER, NJ, 128).transpose(2, 1, 0)).reshape(128, NJ * ER)
    img["bfl"] = np.ascontiguousarray(Bm.transpose(0, 2, 1)).reshape(ER, D)
    img["rw"] = np.ascontiguousarray(
        rw.reshape(NE, NJ, 128).transpose(2, 1, 0)).reshape(128, NJ * NE)
    img["rb"] = np.ascontiguousarray(rb.reshape(NE, 1))
    img["bvc"] = np.ascontiguousarray(bv.reshape(NKV, 128).T)
    img["idf"] = np.eye(128, dtype=np.float32)
    img["idb"] = np.eye(128, dtype=np.float32).astype(BF)
    e16 = np.zeros((NE, 128), np.float32)
    for e in range(NE):
        e16[e, e * RK:(e + 1) * RK] = 0.25
    img["e16"] = e16
    return img


class Phase:

    def __init__(self, c, name):
        self.c = c
        self.name = name
        self.ev = {e: [] for e in ENGINES}

    def op(self, eng, build=None, waits=(), inc=None, by=1, mark=None):
        c = self.c
        if inc is not None:
            c["cnt"][inc] += by
            if mark is not None:
                c["marks"][mark] = (inc, c["cnt"][inc])
        self.ev[eng].append((tuple(waits), build, inc, by))

    def emit(self):
        c = self.c
        nc, sem, marks = c["nc"], c["sem"], c["marks"]
        with nc.Block() as blk:
            for ename in ENGINES:
                evs = self.ev[ename]
                if not evs:
                    continue

                def make(evs=evs):
                    def f(eng):
                        for waits, build, incname, by in evs:
                            for w in waits:
                                kind, a, b_ = w
                                if kind == "abs":
                                    sname, v = a, b_
                                else:
                                    sname, mv = marks[a]
                                    v = mv + b_
                                if v > 0:
                                    eng.wait_ge(sem[sname], v)
                            instr = build(eng) if build is not None else None
                            if incname is not None:
                                tgt = instr if instr is not None else eng.nop()
                                tgt.then_inc(sem[incname], by)
                    return f
                getattr(blk, ename)(make())


def W(mark, delta=0):
    return ("mark", mark, delta)


def WV(sname, val):
    return ("abs", sname, val)


def build_nc(debug=False):
    nc = bass.Bass("TRN2", target_bir_lowering=False, debug=False)
    c = {"nc": nc, "cnt": {}, "marks": {}, "debug": debug}

    def dram(name, shape, dtype, kind=None):
        if kind:
            return nc.dram_tensor(name, shape, dtype, kind=kind).ap()
        return nc.dram_tensor(name, shape, dtype).ap()

    c["x_in"] = dram("x", [TOK, D], F32, "ExternalInput")
    c["wq_d"] = dram("wq", [NJ, 128, D], F32, "ExternalInput")
    c["wk_d"] = dram("wk", [NKV, 128, D], F32, "ExternalInput")
    c["wv_d"] = dram("wv", [NKV, 128, D], F32, "ExternalInput")
    c["wo_d"] = dram("wo", [8, NJ, 128, 512], F32, "ExternalInput")
    c["at_d"] = dram("at", [128, NJ * ER], F32, "ExternalInput")
    c["bfl_d"] = dram("bfl", [ER, D], F32, "ExternalInput")
    c["rw_d"] = dram("rw", [128, NJ * NE], F32, "ExternalInput")
    c["rb_d"] = dram("rb", [NE, 1], F32, "ExternalInput")
    c["bvc_d"] = dram("bvc", [128, NKV], F32, "ExternalInput")
    c["idf_d"] = dram("idf", [128, 128], F32, "ExternalInput")
    c["idb_d"] = dram("idb", [128, 128], BF16, "ExternalInput")
    c["e16_d"] = dram("e16", [NE, 128], F32, "ExternalInput")
    c["out_d"] = dram("out", [TOK, D], F32, "ExternalOutput")

    c["x1d"] = dram("x1d", [TOK, D], F32)
    c["x2d"] = dram("x2d", [TOK, D], F32)
    c["qtd"] = dram("qtd", [NQH, 128, TOK], BF16)
    c["kvd"] = dram("kvd", [NKV * 128, TOK + NKV * 129], BF16)
    c["kvg"] = dram("kvg", [R * NKV * 128, TOK + NKV * 129], BF16)
    c["wo_bf"] = dram("wo_bf", [8, NJ, 128, 512], BF16)
    if debug:
        c["dbg_x1"] = dram("dbg_x1", [TOK, D], F32, "ExternalOutput")
        c["dbg_x2"] = dram("dbg_x2", [TOK, D], F32, "ExternalOutput")
        c["dbg_ctx"] = dram("dbg_ctx", [NJ * 128, TOK], F32, "ExternalOutput")
        c["dbg_kt"] = dram("dbg_kt", [R * NKV * 128, TOK], F32, "ExternalOutput")
        c["dbg_vt"] = dram("dbg_vt", [R * TOK, NKV * 129], F32, "ExternalOutput")
        c["dbg_qt"] = dram("dbg_qt", [NQH * 128, TOK], F32, "ExternalOutput")
        c["dbg_mv"] = dram("dbg_mv", [128, 2 * NT], F32, "ExternalOutput")
        c["dbg_rs"] = dram("dbg_rs", [128, NT], F32, "ExternalOutput")
        c["dbg_ve"] = dram("dbg_ve", [128, NT], F32, "ExternalOutput")

    with ExitStack() as ST:
        E = ST.enter_context
        c["at_sb"] = E(nc.sbuf_tensor("at_sb", [128, NJ * ER], BF16))
        c["bfl_sb"] = E(nc.sbuf_tensor("bfl_sb", [ER, D], BF16))
        c["rw_sb"] = E(nc.sbuf_tensor("rw_sb", [128, NJ * NE], BF16))
        c["rb_sb"] = E(nc.sbuf_tensor("rb_sb", [NE, 1], F32))
        c["bv_sb"] = E(nc.sbuf_tensor("bv_sb", [128, NKV], F32))
        c["idf_sb"] = E(nc.sbuf_tensor("idf_sb", [128, 128], F32))
        c["idb_sb"] = E(nc.sbuf_tensor("idb_sb", [128, 128], BF16))
        c["e16_sb"] = E(nc.sbuf_tensor("e16_sb", [NE, 128], F32))
        c["ones_row"] = E(nc.sbuf_tensor("ones_row", [1, 128], BF16))
        c["ones_col"] = E(nc.sbuf_tensor("ones_col", [128, 1], BF16))
        for i in (1, 2, 3):
            c[f"mv{i}"] = E(nc.sbuf_tensor(f"mv{i}", [128, 2 * NT], F32))
            c[f"rstd{i}"] = E(nc.sbuf_tensor(f"rstd{i}", [128, NT], F32))
        c["stat6"] = E(nc.sbuf_tensor("stat6", [128, 48], F32))
        c["stat63"] = E(nc.sbuf_tensor("stat63", [128, NT * 48], F32))
        c["veps"] = E(nc.sbuf_tensor("veps", [128, NT], F32))
        c["lnv"] = E(nc.sbuf_tensor("lnv", [128, NT], F32))

        names = ["s_w", "s_x0", "s_x1", "s_st6", "s_y", "s_tr", "s_trv", "s_mm",
                 "s_rt", "s_rv", "s_ract", "s_hw", "s_pst", "s_bev",
                 "s_xn0", "s_xn1", "s_bdma0", "s_bdma1",
                 "s_rst", "s_cc", "s_kv0", "s_kv1", "s_st", "s_exp", "s_avm",
                 "s_evac", "s_ctr", "s_ctv", "s_sq", "s_nb",
                 "s_kst0", "s_kst1", "s_mm2", "s_kv20", "s_kv21",
                 "s_wo0", "s_wo1", "s_ph", "s_dbg", "s_gw",
                 "s_bd20", "s_bd21", "s_xa0", "s_xa1", "s_pj"]
        c["sem"] = {n: E(nc.semaphore(n)) for n in names}
        for n in names:
            c["cnt"][n] = 0

        nph = [0]

        def barrier():
            nph[0] += 1
            tgt = 5 * nph[0]
            semh = c["sem"]["s_ph"]
            with nc.Block() as blk:
                for ename in ENGINES:
                    def mk(eng):
                        eng.drain().then_inc(semh, 1)
                        eng.wait_ge(semh, tgt)
                    getattr(blk, ename)(mk)

        with ExitStack() as s1:
            c["xfull"] = s1.enter_context(
                nc.sbuf_tensor("xfull1", [128, NT * D], F32))
            with nc.named_scope("ln1"):
                emit_prologue(c)
                emit_ln1(c)
                barrier()
            if debug:
                emit_debug_stats(c)
                barrier()
            alloc_ffn(c, s1)
            with nc.named_scope("ffn1"):
                emit_ffn(c, call=0)
        if debug:
            barrier()
            emit_debug_copy(c, c["x1d"], c["dbg_x1"], TOK * D * 4)
            barrier()
        with nc.named_scope("lnf2"):
            emit_ln_finish(c, idx=2)
            barrier()
        with ExitStack() as s2:
            with nc.named_scope("proj"):
                emit_projections(c, s2)
                barrier()
        if debug:
            emit_debug_cast(c, c["qtd"].rearrange("a p t -> (a p) t"), c["dbg_qt"],
                            NQH * 128, TOK)
            barrier()
        with ExitStack() as s3:
            c["ctxT"] = s3.enter_context(nc.sbuf_tensor("ctxT", [128, NJ * TOK], BF16))
            with ExitStack() as s3a:
                with nc.named_scope("attn"):
                    emit_attention(c, s3a)
                    barrier()
            if debug:
                emit_debug_ctx(c)
                barrier()
            with ExitStack() as s3b:
                with nc.named_scope("oproj"):
                    emit_oproj(c, s3b)
        if debug:
            barrier()
            emit_debug_copy(c, c["x2d"], c["dbg_x2"], TOK * D * 4)
            barrier()
        with nc.named_scope("lnf3"):
            emit_ln_finish(c, idx=3)
            barrier()
        with ExitStack() as s4:
            c["xfull"] = s4.enter_context(
                nc.sbuf_tensor("xfull2", [128, NT * D], F32))
            alloc_ffn(c, s4, tag="z")
            with nc.named_scope("ffn2"):
                emit_ffn(c, call=1)
    return nc


def emit_prologue(c):
    nc, sem = c["nc"], c["sem"]
    ph = Phase(c, "pro")
    g, s, v, sc = "gpsimd", "sync", "vector", "scalar"
    ph.op(g, lambda e: e.dma_start(c["at_sb"][:], c["at_d"], max_dma_last_dim=8192),
          inc="s_gw", by=16)
    ph.op(g, lambda e: e.dma_start(c["bfl_sb"][:], c["bfl_d"], max_dma_last_dim=8192),
          inc="s_gw", by=16)
    ph.op(g, lambda e: e.dma_start(c["rw_sb"][:], c["rw_d"], max_dma_last_dim=8192),
          inc="s_gw", by=16, mark="gall")
    ph.op(s, lambda e: e.dma_start(c["rb_sb"][:], c["rb_d"]), inc="s_w", by=16)
    ph.op(s, lambda e: e.dma_start(c["bv_sb"][:], c["bvc_d"]), inc="s_w", by=16)
    ph.op(s, lambda e: e.dma_start(c["idf_sb"][:], c["idf_d"]), inc="s_w", by=16)
    ph.op(s, lambda e: e.dma_start(c["idb_sb"][:], c["idb_d"]), inc="s_w", by=16)
    ph.op(s, lambda e: e.dma_start(c["e16_sb"][:], c["e16_d"]), inc="s_w", by=16,
          mark="all")
    ph.op(v, lambda e: e.memset(c["ones_row"][:], 1.0))
    ph.op(v, lambda e: e.memset(c["ones_col"][:], 1.0))
    ph.emit()


def _rstd_chain(ph, c, mv, rstd, tag):
    v, sc = "vector", "scalar"
    var_view = mv[:].rearrange("p (m two) -> p m two", two=2)[:, :, 1:2]

    ph.op(v, lambda e: e.tensor_scalar(c["veps"][:], var_view, EPS, None,
                                       op0=ALU.add), inc="s_rst", mark=f"{tag}_ve")
    ph.op(sc, lambda e: e.activation(c["lnv"][:], c["veps"][:], AF.Ln),
          waits=[W(f"{tag}_ve")])
    ph.op(sc, lambda e: e.drain())
    ph.op(sc, lambda e: e.activation(rstd[:], c["lnv"][:], AF.Exp, scale=-0.5),
          inc="s_rst", mark=f"{tag}_done")


def emit_ln1(c):
    nc, sem = c["nc"], c["sem"]
    xfull = c["xfull"]
    ph = Phase(c, "ln1")
    lnsems = ["s_x0", "s_x1", "s_xa0", "s_xa1"]
    for m in range(NT):
        wl = [] if m < 4 else [W(("ln1x", m - 4))]
        ph.op("sync", (lambda e, m=m: e.dma_start(
            xfull[:, m * D:(m + 1) * D],
            c["x_in"][m * 128:(m + 1) * 128, :])), waits=wl,
            inc=lnsems[m % 4], by=16, mark=("ln1x", m))
    for m in range(NT):
        ph.op("vector", None, waits=[W(("ln1x", m))])
        for q in range(8):
            ph.op("vector", (lambda e, m=m, q=q: e.bn_stats(
                c["stat6"][:, 6 * q:6 * q + 6],
                xfull[:, m * D + q * 512:m * D + (q + 1) * 512])))
        ph.op("vector", lambda e: e.drain())
        ph.op("vector", (lambda e, m=m: e.bn_aggr(
            c["mv1"][:, 2 * m:2 * m + 2], c["stat6"][:])))
        ph.op("vector", lambda e: e.drain(), inc="s_st6", mark=("ln1", m))
    _rstd_chain(ph, c, c["mv1"], c["rstd1"], "ln1")
    for ename in ENGINES:
        ph.op(ename, None, waits=[W("all"), W("gall")])
    ph.emit()


def emit_ln_finish(c, idx):
    mv, rstd = c[f"mv{idx}"], c[f"rstd{idx}"]
    ph = Phase(c, f"lnf{idx}")
    for m in range(NT):
        ph.op("vector", (lambda e, m=m: e.bn_aggr(
            mv[:, 2 * m:2 * m + 2], c["stat63"][:, m * 48:(m + 1) * 48])))
    ph.op("vector", lambda e: e.drain())
    _rstd_chain(ph, c, mv, rstd, f"lnf{idx}")
    ph.emit()


def alloc_ffn(c, st, tag=""):
    nc = c["nc"]
    E = st.enter_context
    c["ytm"] = E(nc.sbuf_tensor(tag + "f_ytm", [128, 2 * D], BF16))
    c["y1t"] = E(nc.sbuf_tensor(tag + "f_y1t", [128, 2 * 1024], BF16))
    c["hw_sb"] = E(nc.sbuf_tensor(tag + "f_hw", [128, TOK], BF16))
    c["wexp_sb"] = E(nc.sbuf_tensor(tag + "f_wexp", [128, TOK], F32))
    c["logits_sb"] = E(nc.sbuf_tensor(tag + "f_logits", [NE, TOK], F32))
    c["ltok"] = E(nc.sbuf_tensor(tag + "f_ltok", [128, 8], F32))
    c["l1"] = E(nc.sbuf_tensor(tag + "f_l1", [128, 1], F32))
    c["l1n"] = E(nc.sbuf_tensor(tag + "f_l1n", [128, 1], F32))
    c["m1"] = E(nc.sbuf_tensor(tag + "f_m1", [128, 8], F32))
    c["tmp8"] = E(nc.sbuf_tensor(tag + "f_tmp8", [128, 8], F32))
    c["l2"] = E(nc.sbuf_tensor(tag + "f_l2", [128, 1], F32))
    c["mtop2"] = E(nc.sbuf_tensor(tag + "f_mtop2", [128, 8], F32))
    c["u_sb"] = E(nc.sbuf_tensor(tag + "f_u", [128, 8], F32))
    c["uw"] = E(nc.sbuf_tensor(tag + "f_uw", [128, 8], F32))
    c["ssum"] = E(nc.sbuf_tensor(tag + "f_ssum", [128, 1], F32))
    c["rr"] = E(nc.sbuf_tensor(tag + "f_rr", [128, 1], F32))
    c["wtok"] = E(nc.sbuf_tensor(tag + "f_wtok", [128, 64], F32))
    c["wT8"] = E(nc.sbuf_tensor(tag + "f_wT8", [NE, TOK], F32))
    c["ostage"] = E(nc.sbuf_tensor(tag + "f_ostage", [128, 2 * 1024], F32))
    c["hps"] = E(nc.psum_tensor(tag + "f_hps", [128, 1024], F32))
    c["rpsA"] = E(nc.psum_tensor(tag + "f_rpsA", [NE, 512], F32))
    c["rpsB"] = E(nc.psum_tensor(tag + "f_rpsB", [NE, 512], F32))
    c["wrp"] = E(nc.psum_tensor(tag + "f_wrp", [128, 1024], F32))
    c["tpsA"] = E(nc.psum_tensor(tag + "f_tpsA", [128, 1024], BF16))
    c["tpsB"] = E(nc.psum_tensor(tag + "f_tpsB", [128, 1024], BF16))


def emit_ffn(c, call):
    nc, sem = c["nc"], c["sem"]
    xfull = c["xfull"]
    sink = c["x1d"] if call == 0 else c["out_d"]
    mv = c["mv1"] if call == 0 else c["mv3"]
    rstd = c["rstd1"] if call == 0 else c["rstd3"]
    tps = [c["tpsA"], c["tpsB"]]
    rps = [c["rpsA"], c["rpsB"]]
    bps = [c["hps"][:, 0:512], c["hps"][:, 512:1024],
           c["wrp"][:, 0:512], c["wrp"][:, 512:1024]]
    bpt = [c["hps"], c["wrp"]]
    P = f"f{call}"
    ph = Phase(c, P)
    NBP = NT * 4
    t, v, sc, g, s = "tensor", "vector", "scalar", "gpsimd", "sync"

    if call == 1:
        f2sems = ["s_x0", "s_x1", "s_xa0", "s_xa1"]
        for m in range(NT):
            wl = [] if m < 4 else [W((P, "x", m - 4))]
            ph.op(s, (lambda e, m=m: e.dma_start(
                xfull[:, m * D:(m + 1) * D],
                c["x2d"][m * 128:(m + 1) * 128, :])), waits=wl,
                inc=f2sems[m % 4], by=16, mark=(P, "x", m))

    def ynorm(m):
        wl = [W((P, "x", m))] if call == 1 else []
        if m >= 2:
            wl.append(W((P, "tr", (m - 2, 3))))
        ph.op(v, (lambda e, m=m: e.tensor_scalar(
            c["ytm"][:, (m % 2) * D:(m % 2 + 1) * D],
            xfull[:, m * D:(m + 1) * D],
            mv[:, 2 * m:2 * m + 1], rstd[:, m:m + 1],
            op0=ALU.subtract, op1=ALU.mult)), waits=wl, inc="s_y",
            mark=(P, "y", m))

    def chain(m):
        ph.op(v, (lambda e, m=m: e.tensor_scalar(
            c["logits_sb"][0:NE, m * 128:(m + 1) * 128],
            rps[m // 4][0:NE, (m % 4) * 128:(m % 4 + 1) * 128],
            c["rb_sb"][0:NE, 0:1], None, op0=ALU.add)),
            waits=[W((P, "mm", (m // 4) * 16 + 15))],
            inc="s_rv", mark=(P, "lg", m))
        ph.op(v, (lambda e, m=m: e.tensor_copy(c["ltok"][:], c["wrp"][:, 0:8])),
              waits=[W((P, "ltr", m))])
        ph.op(v, lambda e: e.drain())
        ph.op(v, lambda e: e.tensor_reduce(c["l1"][:], c["ltok"][:], AX.X, ALU.max))
        ph.op(v, lambda e: e.drain())
        ph.op(v, lambda e: e.tensor_scalar(c["l1n"][:], c["l1"][:], -1.0, None,
                                           op0=ALU.mult))
        ph.op(v, lambda e: e.drain(), inc="s_rv", mark=(P, "l1n", m))
        ph.op(v, lambda e: e.tensor_scalar(c["m1"][:], c["ltok"][:], c["l1"][:],
                                           None, op0=ALU.is_ge))
        ph.op(v, lambda e: e.drain())
        ph.op(v, lambda e: e.scalar_tensor_tensor(c["tmp8"][:], c["m1"][:], -1e30,
                                                  c["ltok"][:], op0=ALU.mult,
                                                  op1=ALU.add))
        ph.op(v, lambda e: e.drain())
        ph.op(v, lambda e: e.tensor_reduce(c["l2"][:], c["tmp8"][:], AX.X, ALU.max))
        ph.op(v, lambda e: e.drain())
        ph.op(v, lambda e: e.tensor_scalar(c["mtop2"][:], c["ltok"][:], c["l2"][:],
                                           None, op0=ALU.is_ge))
        ph.op(v, lambda e: e.drain())
        ph.op(v, (lambda e: e.tensor_tensor(c["uw"][:], c["u_sb"][:], c["mtop2"][:],
                                            op=ALU.mult)),
              waits=[W((P, "u", m))])
        ph.op(v, lambda e: e.drain())
        ph.op(v, lambda e: e.tensor_reduce(c["ssum"][:], c["uw"][:], AX.X, ALU.add))
        ph.op(v, lambda e: e.drain())
        ph.op(v, lambda e: e.reciprocal(c["rr"][:], c["ssum"][:]))
        ph.op(v, lambda e: e.drain())
        ph.op(v, (lambda e, m=m: e.tensor_scalar(
            c["wtok"][:, m * 8:(m + 1) * 8], c["uw"][:], c["rr"][:],
            None, op0=ALU.mult)))
        ph.op(v, lambda e: e.drain(), inc="s_rv", mark=(P, "w", m))

    ynorm(0)
    ynorm(1)
    for m in range(NT):
        for jg in range(4):
            kg = m * 4 + jg
            wl = [W((P, "tr", (m, jg)))]
            if kg >= 2:
                wl.append(W((P, "mm", kg - 2)))
            ph.op(v, (lambda e, kg=kg: e.tensor_copy(
                c["y1t"][:, (kg % 2) * 1024:(kg % 2 + 1) * 1024],
                tps[kg % 2][:, 0:1024])), waits=wl, inc="s_trv",
                mark=(P, "trv", kg))
        if m + 2 < NT:
            ynorm(m + 2)
        if m >= 4:
            chain(m - 4)
    for m in range(4, NT):
        chain(m)
    for m in range(NT):
        ph.op(v, (lambda e, m=m: e.tensor_copy(
            c["wT8"][0:NE, m * 128:(m + 1) * 128], c["wrp"][0:NE, 528:656])),
            waits=[W((P, "wtr", m))], inc="s_rv", mark=(P, "w8", m))
    ph.op(v, (lambda e: e.tensor_copy(c["wexp_sb"][:], c["wrp"][:, 0:1024])),
          waits=[W((P, "wexp"))])
    ph.op(v, lambda e: e.drain())
    ph.op(v, (lambda e: e.tensor_tensor(c["hw_sb"][:], c["hps"][:, 0:1024],
                                        c["wexp_sb"][:], op=ALU.mult)),
          inc="s_hw", mark=(P, "hw"))
    for p in range(NBP):
        m, n2 = p // 4, p % 4
        wl = [W((P, "b", 2 * p + 1))]
        if p >= 2:
            wl.append(W((P, "st", p - 2)))
        if call == 0 and p >= 1:
            _bn_pair(ph, c, P, p - 1)
            ph.op(v, lambda e: e.drain())
        ph.op(v, (lambda e, p=p, m=m, n2=n2: e.tensor_tensor(
            c["ostage"][:, (p % 2) * 1024:(p % 2 + 1) * 1024],
            bpt[p % 2][:, 0:1024],
            xfull[:, m * D + n2 * 1024:m * D + (n2 + 1) * 1024], op=ALU.add)),
            waits=wl)
        ph.op(v, lambda e: e.drain(), inc="s_bev", mark=(P, "bev", p))
    if call == 0:
        _bn_pair(ph, c, P, NBP - 1)

    for m in range(NT):
        for jg in range(4):
            kg = m * 4 + jg
            for jj in range(8):
                j = jg * 8 + jj
                wl = []
                if jj == 0:
                    if jg == 0:
                        wl.append(W((P, "y", m)))
                    if kg >= 2:
                        wl.append(W((P, "trv", kg - 2)))
                ph.op(t, (lambda e, m=m, j=j, jj=jj, kg=kg: e.transpose(
                    tps[kg % 2][:, jj * 128:(jj + 1) * 128],
                    c["ytm"][:, (m % 2) * D + j * 128:(m % 2) * D + (j + 1) * 128],
                    c["idb_sb"][:])), waits=wl,
                    inc=("s_tr" if jj == 7 else None),
                    mark=((P, "tr", (m, jg)) if jj == 7 else None))
            ph.op(t, None, waits=[W((P, "trv", kg))])
            for jj in range(8):
                j = jg * 8 + jj
                st_flag = (j == 0 and m % 4 == 0)
                ph.op(t, (lambda e, m=m, j=j, jj=jj, kg=kg, st_flag=st_flag: e.matmul(
                    c["hps"][:, m * 128:(m + 1) * 128],
                    c["at_sb"][:, j * ER:(j + 1) * ER],
                    c["y1t"][:, (kg % 2) * 1024 + jj * 128:(kg % 2) * 1024 + (jj + 1) * 128],
                    start=st_flag, stop=(j == NJ - 1), skip_group_check=True)))
                ph.op(t, (lambda e, m=m, j=j, jj=jj, kg=kg, st_flag=st_flag: e.matmul(
                    rps[m // 4][0:NE, (m % 4) * 128:(m % 4 + 1) * 128],
                    c["rw_sb"][:, j * NE:(j + 1) * NE],
                    c["y1t"][:, (kg % 2) * 1024 + jj * 128:(kg % 2) * 1024 + (jj + 1) * 128],
                    start=st_flag, stop=(j == NJ - 1), skip_group_check=True)),
                    inc=("s_mm" if jj == 7 else None),
                    mark=((P, "mm", kg) if jj == 7 else None))
        if m >= 4:
            _ltr(ph, c, P, m - 4)
    for m in range(4, NT):
        _ltr(ph, c, P, m)
    for m in range(NT):
        _wtr(ph, c, P, m)
    ph.op(t, (lambda e: e.matmul(c["wrp"][:, 0:512], c["e16_sb"][:],
                                 c["wT8"][0:NE, 0:512], start=True, stop=True)),
          waits=[W((P, "w8", NT - 1))])
    ph.op(t, (lambda e: e.matmul(c["wrp"][:, 512:1024], c["e16_sb"][:],
                                 c["wT8"][0:NE, 512:1024], start=True, stop=True)),
          inc="s_pst", mark=(P, "wexp"))
    for k in range(NT * 8):
        m, n = k // 8, k % 8
        wl = [W((P, "hw"))] if k == 0 else []
        if k >= 4:
            wl.append(W((P, "bev", k // 2 - 2)))
        ph.op(t, (lambda e, k=k, m=m, n=n: e.matmul(
            bps[k % 4],
            c["hw_sb"][:, m * 128:(m + 1) * 128],
            c["bfl_sb"][:, n * 512:(n + 1) * 512],
            start=True, stop=True)), waits=wl, inc="s_pst", mark=(P, "b", k))

    for m in range(NT):
        ph.op(sc, (lambda e, m=m: e.activation(
            c["u_sb"][:], c["ltok"][:], AF.Exp, bias=c["l1n"][:])),
            waits=[W((P, "l1n", m))], inc="s_ract", mark=(P, "u", m))

    for p in range(NBP):
        m, n2 = p // 4, p % 4
        ph.op(g, (lambda e, p=p, m=m, n2=n2: e.dma_start(
            sink[m * 128:(m + 1) * 128, n2 * 1024:(n2 + 1) * 1024],
            c["ostage"][:, (p % 2) * 1024:(p % 2 + 1) * 1024])),
            waits=[W((P, "bev", p))], inc=f"s_bdma{p % 2}", by=16,
            mark=(P, "st", p))
    ph.op(g, None, waits=[W((P, "st", NBP - 2)), W((P, "st", NBP - 1))])
    ph.emit()


def _bn_pair(ph, c, P, p):
    m, n2 = p // 4, p % 4
    ph.op("vector", (lambda e, p=p, m=m, n2=n2: e.bn_stats(
        c["stat63"][:, m * 48 + 2 * n2 * 6:m * 48 + (2 * n2 + 1) * 6],
        c["ostage"][:, (p % 2) * 1024:(p % 2) * 1024 + 512])))
    ph.op("vector", (lambda e, p=p, m=m, n2=n2: e.bn_stats(
        c["stat63"][:, m * 48 + (2 * n2 + 1) * 6:m * 48 + (2 * n2 + 2) * 6],
        c["ostage"][:, (p % 2) * 1024 + 512:(p % 2 + 1) * 1024])))


def _ltr(ph, c, P, m):
    ph.op("tensor", (lambda e, m=m: e.transpose(
        c["wrp"][:, 0:8], c["logits_sb"][0:NE, m * 128:(m + 1) * 128],
        c["idf_sb"][0:NE, 0:NE])),
        waits=[W((P, "lg", m))], inc="s_rt", mark=(P, "ltr", m))


def _wtr(ph, c, P, m):
    wl = [W((P, "w", m))]
    if m >= 1:
        wl.append(W((P, "w8", m - 1)))
    ph.op("tensor", (lambda e, m=m: e.transpose(
        c["wrp"][0:NE, 528:656], c["wtok"][:, m * 8:(m + 1) * 8],
        c["idf_sb"][:])),
        waits=wl, inc="s_rt", mark=(P, "wtr", m))


def emit_projections(c, st):
    nc, sem = c["nc"], c["sem"]
    E = st.enter_context
    y2T = E(nc.sbuf_tensor("p_y2T", [128, NJ * TOK], BF16))
    xring = E(nc.sbuf_tensor("p_xr", [128, 2 * D], F32))
    ytm = E(nc.sbuf_tensor("p_ytm", [128, 2 * D], BF16))
    wslab = E(nc.sbuf_tensor("p_ws", [128, 2 * D], BF16))
    kvst = E(nc.sbuf_tensor("p_kvst", [128, 2 * 1024], BF16))
    vtile = E(nc.sbuf_tensor("p_vt", [128, 1024], BF16))
    vstage = E(nc.sbuf_tensor("p_vs", [128, NT * 1032], BF16))
    tpsA = E(nc.psum_tensor("p_tpsA", [128, 1024], BF16))
    tpsB = E(nc.psum_tensor("p_tpsB", [128, 1024], BF16))
    pps = [E(nc.psum_tensor("p_pps0", [128, 1024], F32)),
           E(nc.psum_tensor("p_pps1", [128, 1024], F32))]
    tps = [tpsA, tpsB]
    P = "pj"
    ph = Phase(c, P)
    t, v, g, s = "tensor", "vector", "gpsimd", "sync"
    NPROJ = 48
    NK = NT * NJ

    def slab_src(i):
        if i < 8:
            return c["wk_d"][i, :, :]
        if i < 16:
            return c["wv_d"][i - 8, :, :]
        return c["wq_d"][i - 16, :, :]

    for m in range(NT):
        wl = [] if m < 2 else [W((P, "y", m - 2))]
        ph.op(s, (lambda e, m=m: e.dma_start(
            xring[:, (m % 2) * D:(m % 2 + 1) * D],
            c["x1d"][m * 128:(m + 1) * 128, :])), waits=wl, inc=f"s_x{m % 2}",
            by=16, mark=(P, "x", m))

    ph.op(v, (lambda e: e.memset(
        vstage[:].rearrange("p (h t f) -> p h t f", h=NKV, f=129)[:, :, :, 128:129],
        1.0)), inc="s_rv", mark=(P, "ones"))

    def ynorm(m):
        wl = [W((P, "x", m))]
        if m >= 2:
            wl.append(W((P, "tr", (m - 2, 3))))
        ph.op(v, (lambda e, m=m: e.tensor_scalar(
            ytm[:, (m % 2) * D:(m % 2 + 1) * D],
            xring[:, (m % 2) * D:(m % 2 + 1) * D],
            c["mv2"][:, 2 * m:2 * m + 1], c["rstd2"][:, m:m + 1],
            op0=ALU.subtract, op1=ALU.mult)), waits=wl, inc="s_y", mark=(P, "y", m))

    ynorm(0)
    ynorm(1)
    y2T_v = y2T[:].rearrange("p (j t) -> p j t", t=TOK)
    for m in range(NT):
        for jg in range(4):
            kg = m * 4 + jg
            ph.op(v, (lambda e, m=m, jg=jg, kg=kg: e.tensor_copy(
                y2T_v[:, jg * 8:(jg + 1) * 8, m * 128:(m + 1) * 128],
                tps[kg % 2][:].rearrange("p (a b) -> p a b", b=128))),
                waits=[W((P, "tr", (m, jg)))], inc="s_trv", mark=(P, "trv", kg))
        if m + 2 < NT:
            ynorm(m + 2)
    for i in range(NPROJ):
        if i < 8:
            wl = [W((P, "mm2", i))]
            if i >= 2:
                wl.append(W((P, "kst", i - 2)))
            ph.op(v, (lambda e, i=i: e.tensor_copy(
                kvst[:, (i % 2) * 1024:(i % 2 + 1) * 1024], pps[i % 2][:, 0:1024])),
                waits=wl, inc="s_rv", mark=(P, "ev", i))
        elif i < 16:
            vc = i - 8
            ph.op(v, (lambda e, i=i, vc=vc: e.tensor_scalar(
                vtile[:], pps[i % 2][:, 0:1024], c["bv_sb"][:, vc:vc + 1], None,
                op0=ALU.add)), waits=[W((P, "mm2", i))],
                inc="s_rv", mark=(P, "ev", i))
            kgv = 32 + vc
            vst_v = vstage[:].rearrange("p (h t f) -> p h t f", h=NKV, f=129)
            ph.op(v, (lambda e, vc=vc, kgv=kgv, vst_v=vst_v: e.tensor_copy(
                vst_v[:, vc, :, 0:128],
                tps[kgv % 2][:].rearrange("p (a b) -> p a b", b=128))),
                waits=[W((P, "tr", (8, vc)))], inc="s_trv", mark=(P, "trv", kgv))
        else:
            qi = i - 16
            wl = [W((P, "mm2", i))]
            if qi >= 2:
                wl.append(W((P, "qst", qi - 2)))
            else:
                wl.append(W((P, "kvdone")))
            ph.op(v, (lambda e, i=i: e.tensor_copy(
                kvst[:, (i % 2) * 1024:(i % 2 + 1) * 1024], pps[i % 2][:, 0:1024])),
                waits=wl, inc="s_rv", mark=(P, "ev", i))

    for m in range(NT):
        for jg in range(4):
            kg = m * 4 + jg
            for jj in range(8):
                j = jg * 8 + jj
                wl = []
                if jj == 0:
                    if jg == 0:
                        wl.append(W((P, "y", m)))
                    if kg >= 2:
                        wl.append(W((P, "trv", kg - 2)))
                ph.op(t, (lambda e, m=m, j=j, jj=jj, kg=kg: e.transpose(
                    tps[kg % 2][:, jj * 128:(jj + 1) * 128],
                    ytm[:, (m % 2) * D + j * 128:(m % 2) * D + (j + 1) * 128],
                    c["idb_sb"][:])), waits=wl,
                    inc=("s_tr" if jj == 7 else None),
                    mark=((P, "tr", (m, jg)) if jj == 7 else None))
    for i in range(NPROJ):
        wl = [W((P, "kv2", i))]
        if i == 0:
            wl.append(W((P, "trv", 31)))
        if i >= 2:
            wl.append(W((P, "ev", i - 2)))
        ph.op(t, None, waits=wl)
        for j in range(NJ):
            for hh in range(2):
                last = (j == NJ - 1 and hh == 1)
                ph.op(t, (lambda e, i=i, j=j, hh=hh: e.matmul(
                    pps[i % 2][:, hh * 512:(hh + 1) * 512],
                    wslab[:, (i % 2) * D + j * 128:(i % 2) * D + (j + 1) * 128],
                    y2T[:, j * TOK + hh * 512:j * TOK + (hh + 1) * 512],
                    start=(j == 0), stop=(j == NJ - 1), skip_group_check=True)),
                    inc=("s_mm2" if last else None),
                    mark=((P, "mm2", i) if last else None))
        if 8 <= i < 16:
            vc = i - 8
            kgv = 32 + vc
            for t_ in range(NT):
                wl = []
                if t_ == 0:
                    wl.append(W((P, "ev", i)))
                    if kgv >= 2:
                        wl.append(W((P, "trv", kgv - 2)))
                ph.op(t, (lambda e, t_=t_, kgv=kgv: e.transpose(
                    tps[kgv % 2][:, t_ * 128:(t_ + 1) * 128],
                    vtile[:, t_ * 128:(t_ + 1) * 128],
                    c["idb_sb"][:])), waits=wl,
                    inc=("s_tr" if t_ == NT - 1 else None),
                    mark=((P, "tr", (8, vc)) if t_ == NT - 1 else None))

    for i in range(NPROJ):
        wl = [] if i < 2 else [W((P, "mm2", i - 2))]
        ph.op(g, (lambda e, i=i: e.dma_start(
            wslab[:, (i % 2) * D:(i % 2 + 1) * D], slab_src(i),
            max_dma_last_dim=8192)), waits=wl, inc=f"s_kv2{i % 2}", by=16,
            mark=(P, "kv2", i))
        if i == 18:
            ph.op(g, (lambda e: e.collective_compute(
                "AllGather", ALU.bypass, replica_groups=[list(range(R))],
                ins=[c["kvd"].opt()], outs=[c["kvg"].opt()])),
                waits=[W((P, "kvdone"))],
                inc="s_cc", mark=(P, "ccv"))

    for i in range(8):
        ph.op(s, (lambda e, i=i: e.dma_start(
            c["kvd"][i * 128:(i + 1) * 128, 0:TOK],
            kvst[:, (i % 2) * 1024:(i % 2 + 1) * 1024])),
            waits=[W((P, "ev", i))], inc=f"s_kst{i % 2}", by=16,
            mark=(P, "kst", i))
    for t_ in range(NT):
        wl = [W((P, "trv", 39)), W((P, "ones"))]
        if t_ >= 2:
            wl.append(W((P, "vst", t_ - 2)))
        else:
            wl.append(W((P, "kst", 6 + t_)))
        ph.op(s, (lambda e, t_=t_: e.dma_start(
            c["kvd"][t_ * 128:(t_ + 1) * 128, TOK:TOK + NKV * 129],
            vstage[:, t_ * 1032:(t_ + 1) * 1032])), waits=wl,
            inc=f"s_kst{t_ % 2}", by=16, mark=(P, "vst", t_))
    ph.op(s, None, waits=[W((P, "vst", NT - 2)), W((P, "vst", NT - 1))],
          inc="s_pj", mark=(P, "kvdone"))
    for qi in range(32):
        ph.op(s, (lambda e, qi=qi: e.dma_start(
            c["qtd"][qi, :, :], kvst[:, ((16 + qi) % 2) * 1024:((16 + qi) % 2 + 1) * 1024])),
            waits=[W((P, "ev", 16 + qi))], inc=f"s_kst{qi % 2}", by=16,
            mark=(P, "qst", qi))
    ph.op(s, None, waits=[W((P, "qst", 30)), W((P, "qst", 31))])
    ph.emit()


def emit_attention(c, st):
    nc, sem = c["nc"], c["sem"]
    E = st.enter_context
    kT = E(nc.sbuf_tensor("a_kT", [128, 2 * S], BF16))
    vh = E(nc.sbuf_tensor("a_vh", [128, 2 * NCH * 129], BF16))
    qT = E(nc.sbuf_tensor("a_qT", [128, 2 * G4 * TOK], BF16))
    ptil = E(nc.sbuf_tensor("a_ptil", [128, 3 * 1536], BF16))
    ksq = E(nc.sbuf_tensor("a_ksq", [128, S], BF16))
    qsq = E(nc.sbuf_tensor("a_qsq", [128, G4 * TOK], BF16))
    cstage = E(nc.sbuf_tensor("a_cs", [128, 4 * 128], BF16))
    negbh = E(nc.sbuf_tensor("a_nb", [128, G4], F32))
    nt1 = E(nc.sbuf_tensor("a_nt1", [128, 1], F32))
    ntk = E(nc.sbuf_tensor("a_ntk", [1, 1], F32))
    ntq = E(nc.sbuf_tensor("a_ntq", [1, 1], F32))
    nts = E(nc.sbuf_tensor("a_nts", [1, 1], F32))
    ntb = E(nc.sbuf_tensor("a_ntb", [1, 1], BF16))
    rcol = E(nc.sbuf_tensor("a_rc", [128, 4], F32))
    stA = E(nc.psum_tensor("a_stA", [128, 1536], F32))
    stB = E(nc.psum_tensor("a_stB", [128, 1536], F32))
    av01 = E(nc.psum_tensor("a_av01", [128, 512], F32))
    av23 = E(nc.psum_tensor("a_av23", [128, 512], F32))
    stX = [stA, stB]
    avX = [av01, av01, av23, av23]
    avo = [0, 129, 0, 129]
    P = "at"
    ph = Phase(c, P)
    t, v, sc, g, s = "tensor", "vector", "scalar", "gpsimd", "sync"
    BSC = ATT_SCALE * BH_MARGIN * 0.5

    GSIZES = [3] * 20 + [2, 2]
    GPE = len(GSIZES)
    GOFF = [0]
    for z in GSIZES:
        GOFF.append(GOFF[-1] + z)

    def ep(h, gq, s_):
        return (h * G4 + gq) * 2 + s_

    for h in range(R):
        par = h % 2
        wl = [W(("pj", "ccv"))]
        if h >= 2:
            wl.append(W((P, "avm", (ep(h - 2, G4 - 1, 1) + 1) * GPE - 1)))
        ph.op(s, None, waits=wl)
        for r in range(R):
            ph.op(s, (lambda e, h=h, r=r, par=par: e.dma_start(
                kT[:, par * S + r * TOK:par * S + (r + 1) * TOK],
                c["kvg"][r * NKV * 128 + h * 128:
                         r * NKV * 128 + (h + 1) * 128, 0:TOK])),
                inc=f"s_kv{h % 2}", by=16)
        for r in range(R):
            ph.op(s, (lambda e, h=h, r=r, par=par: e.dma_start(
                vh[:, par * NCH * 129 + r * 8 * 129:
                   par * NCH * 129 + (r + 1) * 8 * 129],
                c["kvg"][r * NKV * 128 + h * 128:
                         r * NKV * 128 + (h + 1) * 128, TOK:TOK + NKV * 129])),
                inc=f"s_kv{h % 2}", by=16)
        for gq in range(G4):
            mk = (P, "kv", h) if gq == G4 - 1 else None
            ph.op(s, (lambda e, h=h, gq=gq, par=par: e.dma_start(
                qT[:, par * G4 * TOK + gq * TOK:par * G4 * TOK + (gq + 1) * TOK],
                c["qtd"][h * G4 + gq, :, :])), inc=f"s_kv{h % 2}", by=16, mark=mk)

    for n in range(8):
        wl = [W((P, "kv", 0))] if n == 0 else (
            [] if n < 2 else [W((P, "wocast", n - 2))])
        ph.op(g, (lambda e, n=n: e.dma_start(
            c["wo_bf"][n], c["wo_d"][n], max_dma_last_dim=8192)), waits=wl,
            inc=f"s_xn{n % 2}", by=16, mark=(P, "wocast", n))

    avq = []

    def _av_pop(desc):
        i, gi, par, gh, s_ = desc
        Gi = i * GPE + gi
        sz = GSIZES[gi]
        wl = [W((P, "exp", Gi))]
        if gi == 0 and i >= 1:
            wl.append(W((P, "ctv", i - 1)))
        nmm = 0
        tot = sz * 4
        for cpos in range(sz):
            ch = GOFF[gi] + cpos
            for q in range(4):
                st_flag = (gi == 0 and cpos == 0 and (q % 2 == 0))
                nmm += 1
                ph.op(t, (lambda e, Gi=Gi, ch=ch, cpos=cpos, q=q, par=par,
                          st_flag=st_flag: e.matmul(
                    avX[q][:, avo[q]:avo[q] + 129],
                    ptil[:, (Gi % 3) * 1536 + cpos * 512 + q * 128:
                         (Gi % 3) * 1536 + cpos * 512 + (q + 1) * 128],
                    vh[:, par * NCH * 129 + ch * 129:par * NCH * 129 + (ch + 1) * 129],
                    start=st_flag, stop=False, skip_group_check=True)),
                    waits=wl if (cpos == 0 and q == 0) else (),
                    inc=("s_avm" if nmm == tot else None),
                    mark=((P, "avm", Gi) if nmm == tot else None))
        if gi == GPE - 1:
            trt = av01[:].bitcast(BF16)
            for q in range(4):
                ph.op(t, (lambda e, q=q, trt=trt: e.transpose(
                    trt[:, q * 128:(q + 1) * 128],
                    cstage[:, q * 128:(q + 1) * 128],
                    c["idb_sb"][:])),
                    waits=[W((P, "cs", i))] if q == 0 else (),
                    inc=("s_ctr" if q == 3 else None),
                    mark=((P, "ctr", i) if q == 3 else None))

    def squares(h):
        par = h % 2
        wl = [W((P, "kv", h))]
        if h >= 1:
            wl.append(W((P, "norm", h - 1)))
        ph.op(v, (lambda e, par=par: e.tensor_tensor(
            ksq[:], kT[:, par * S:(par + 1) * S], kT[:, par * S:(par + 1) * S],
            op=ALU.mult)), waits=wl)
        ph.op(v, (lambda e, par=par: e.tensor_tensor(
            qsq[:], qT[:, par * G4 * TOK:(par + 1) * G4 * TOK],
            qT[:, par * G4 * TOK:(par + 1) * G4 * TOK], op=ALU.mult)),
            inc="s_sq", mark=(P, "sq", h))

    squares(0)
    for h in range(R):
        par = h % 2
        gh0 = h * G4
        while avq:
            _av_pop(avq.pop(0))
        wl = [W((P, "sq", h))]
        if h >= 1:
            lastg = (ep(h - 1, G4 - 1, 1) + 1) * GPE - 1
            wl.append(W((P, "exp", lastg)))
            wl.append(W((P, "nb", (h - 1, G4 - 1))))
        ph.op(t, None, waits=wl)
        for cc in range(NCH):
            ph.op(t, (lambda e, cc=cc, first=(cc == 0): e.matmul(
                stA[:, cc:cc + 1], ksq[:, cc * 128:(cc + 1) * 128],
                c["ones_col"][:], start=first, stop=False, skip_group_check=True)))
        for gq in range(G4):
            for t_ in range(NT):
                last = (gq == G4 - 1 and t_ == NT - 1)
                ph.op(t, (lambda e, gq=gq, t_=t_: e.matmul(
                    stA[:, 64 + gq * NT + t_:64 + gq * NT + t_ + 1],
                    qsq[:, gq * TOK + t_ * 128:gq * TOK + (t_ + 1) * 128],
                    c["ones_col"][:], start=False, stop=False,
                    skip_group_check=True)),
                    inc=("s_st" if last else None),
                    mark=((P, "norm", h) if last else None))
        ph.op(v, (lambda e: e.tensor_reduce(nt1[:], stA[:, 0:64], AX.X, ALU.max)),
              waits=[W((P, "norm", h))])
        ph.op(v, lambda e: e.drain(), inc="s_sq", mark=(P, "kcol", h))
        ph.op(t, (lambda e: e.transpose(stA[0:1, 96:224], nt1[:], c["idf_sb"][:])),
              waits=[W((P, "kcol", h))], inc="s_st", mark=(P, "ktr", h))
        ph.op(v, (lambda e: e.tensor_reduce(ntk[0:1, 0:1], stA[0:1, 96:224],
                                            AX.X, ALU.max)),
              waits=[W((P, "ktr", h))])
        ph.op(v, lambda e: e.drain())
        for gq in range(G4):
            ph.op(v, (lambda e, gq=gq: e.tensor_reduce(
                nt1[:], stA[:, 64 + gq * NT:64 + (gq + 1) * NT], AX.X, ALU.max)))
            ph.op(v, lambda e: e.drain(), inc="s_sq", mark=(P, "qcol", (h, gq)))
            ph.op(t, (lambda e: e.transpose(stA[0:1, 224:352], nt1[:],
                                            c["idf_sb"][:])),
                  waits=[W((P, "qcol", (h, gq)))], inc="s_st",
                  mark=(P, "qtr", (h, gq)))
            ph.op(v, (lambda e: e.tensor_reduce(ntq[0:1, 0:1], stA[0:1, 224:352],
                                                AX.X, ALU.max)),
                  waits=[W((P, "qtr", (h, gq)))])
            ph.op(v, lambda e: e.drain())
            ph.op(v, (lambda e: e.tensor_tensor(nts[0:1, 0:1], ntq[0:1, 0:1],
                                                ntk[0:1, 0:1], op=ALU.add)))
            ph.op(v, lambda e: e.drain())
            ph.op(v, (lambda e: e.tensor_scalar(ntb[0:1, 0:1], nts[0:1, 0:1],
                                                BSC, None, op0=ALU.mult)))
            ph.op(v, lambda e: e.drain(), inc="s_sq", mark=(P, "ntb", (h, gq)))
            ph.op(t, (lambda e, gq=gq: e.matmul(
                stA[:, 480 + gq:481 + gq], c["ones_row"][:], ntb[0:1, 0:1],
                start=False, stop=False, skip_group_check=True)),
                waits=[W((P, "ntb", (h, gq)))], inc="s_st",
                mark=(P, "bc", (h, gq)))
            ph.op(v, (lambda e, gq=gq: e.tensor_scalar(
                negbh[:, gq:gq + 1], stA[:, 480 + gq:481 + gq], -1.0, None,
                op0=ALU.mult)),
                waits=[W((P, "bc", (h, gq)))])
            ph.op(v, lambda e: e.drain(), inc="s_nb", mark=(P, "nb", (h, gq)))

        for gq in range(G4):
            for s_ in range(2):
                i = ep(h, gq, s_)
                for gi in range(GPE):
                    Gi = i * GPE + gi
                    sz = GSIZES[gi]
                    wl = []
                    if Gi >= 2:
                        wl.append(W((P, "exp", Gi - 2)))
                    if gi == 0:
                        wl.append(W((P, "nb", (h, G4 - 1))))
                        wl.append(W((P, "kv", h)))
                    for cpos in range(sz):
                        ch = GOFF[gi] + cpos
                        last = (cpos == sz - 1)
                        ph.op(t, (lambda e, Gi=Gi, ch=ch, cpos=cpos, par=par,
                                  gq=gq, s_=s_: e.matmul(
                            stX[Gi % 2][:, cpos * 512:(cpos + 1) * 512],
                            kT[:, par * S + ch * 128:par * S + (ch + 1) * 128],
                            qT[:, par * G4 * TOK + gq * TOK + s_ * 512:
                               par * G4 * TOK + gq * TOK + s_ * 512 + 512],
                            start=True, stop=True)),
                            waits=wl if cpos == 0 else (),
                            inc=("s_st" if last else None),
                            mark=((P, "stg", Gi) if last else None))
                    avq.append((i, gi, par, gh0 + gq, s_))
                    if len(avq) > 2:
                        _av_pop(avq.pop(0))
                ph.op(v, lambda e: e.drain(),
                      waits=([W((P, "avm", (i + 1) * GPE - 1))] +
                             ([W((P, "ctr", i - 1))] if i >= 1 else [])))
                ph.op(v, lambda e: e.reciprocal(rcol[:, 0:1], av01[:, 128:129]))
                ph.op(v, lambda e: e.reciprocal(rcol[:, 1:2], av01[:, 257:258]))
                ph.op(v, lambda e: e.reciprocal(rcol[:, 2:3], av23[:, 128:129]))
                ph.op(v, lambda e: e.reciprocal(rcol[:, 3:4], av23[:, 257:258]))
                ph.op(v, lambda e: e.drain())
                for q in range(4):
                    ph.op(v, (lambda e, q=q: e.tensor_scalar(
                        cstage[:, q * 128:(q + 1) * 128],
                        avX[q][:, avo[q]:avo[q] + 128], rcol[:, q:q + 1], None,
                        op0=ALU.mult)),
                        inc=("s_evac" if q == 3 else None),
                        mark=((P, "cs", i) if q == 3 else None))
                gh = gh0 + gq
                ph.op(v, (lambda e, gh=gh, s_=s_: e.tensor_copy(
                    c["ctxT"][:, gh * TOK + s_ * 512:gh * TOK + (s_ + 1) * 512],
                    av01[:].bitcast(BF16)[:, 0:512])),
                    waits=[W((P, "ctr", i))], inc="s_ctv", mark=(P, "ctv", i))
                if gq == G4 - 1 and s_ == 0 and h + 1 < R:
                    squares(h + 1)

        for gq in range(G4):
            for s_ in range(2):
                i = ep(h, gq, s_)
                for gi in range(GPE):
                    Gi = i * GPE + gi
                    sz = GSIZES[gi]
                    wl = [W((P, "stg", Gi))]
                    if Gi >= 3:
                        wl.append(W((P, "avm", Gi - 3)))
                    if gi == 0:
                        wl.append(W((P, "nb", (h, gq))))
                    ph.op(sc, (lambda e, Gi=Gi, gq=gq, sz=sz: e.activation(
                        ptil[:, (Gi % 3) * 1536:(Gi % 3) * 1536 + sz * 512],
                        stX[Gi % 2][:, 0:sz * 512], AF.Exp,
                        bias=negbh[:, gq:gq + 1], scale=ATT_SCALE)),
                        waits=wl, inc="s_exp", mark=(P, "exp", Gi))
    while avq:
        _av_pop(avq.pop(0))
    ph.emit()



def emit_oproj(c, st):
    nc, sem = c["nc"], c["sem"]
    E = st.enter_context
    woslab = E(nc.sbuf_tensor("o_ws", [128, 2 * NJ * 512], BF16))
    x1ring = E(nc.sbuf_tensor("o_x1", [128, 2 * 512], F32))
    x2st = E(nc.sbuf_tensor("o_x2", [128, 2 * 512], F32))
    ops = [E(nc.psum_tensor("o_op0", [128, 512], F32)),
           E(nc.psum_tensor("o_op1", [128, 512], F32))]
    P = "op"
    ph = Phase(c, P)
    t, v, g, s = "tensor", "vector", "gpsimd", "sync"
    NB = 64

    for n in range(8):
        wl = [W(("at", "wocast", n))]
        if n >= 2:
            wl.append(W((P, "mm", (n - 2) * 8 + 7)))
        ph.op("scalar", (lambda e, n=n: e.dma_start(
            woslab[:, (n % 2) * NJ * 512:(n % 2 + 1) * NJ * 512]
            .rearrange("p (j f) -> p j f", f=512),
            c["wo_bf"][n, :, :, :].rearrange("j pp f -> pp j f"))), waits=wl,
            inc=f"s_wo{n % 2}", by=16, mark=(P, "wo", n))

    for k in range(NB + 2):
        if k < NB:
            n, m = k // 8, k % 8
            wl = [] if k < 2 else [W((P, "bev", k - 2))]
            ph.op(s, (lambda e, n=n, m=m, k=k: e.dma_start(
                x1ring[:, (k % 2) * 512:(k % 2 + 1) * 512],
                c["x1d"][m * 128:(m + 1) * 128, n * 512:(n + 1) * 512])),
                waits=wl, inc=f"s_xa{k % 2}", by=16, mark=(P, "x1", k))
        if k >= 2:
            k2 = k - 2
            n2, m2 = k2 // 8, k2 % 8
            ph.op(s, (lambda e, n2=n2, m2=m2, k2=k2: e.dma_start(
                c["x2d"][m2 * 128:(m2 + 1) * 128, n2 * 512:(n2 + 1) * 512],
                x2st[:, (k2 % 2) * 512:(k2 % 2 + 1) * 512])),
                waits=[W((P, "bev", k2))], inc=f"s_bd2{k2 % 2}", by=16,
                mark=(P, "st", k2))
    ph.op(s, None, waits=[W((P, "st", NB - 2)), W((P, "st", NB - 1))])

    for n in range(8):
        for m in range(8):
            k = n * 8 + m
            wl = [W((P, "wo", n))] if m == 0 else []
            if k >= 2:
                wl.append(W((P, "bev", k - 2)))
            ph.op(t, None, waits=wl)
            for j in range(NJ):
                last = (j == NJ - 1)
                ph.op(t, (lambda e, n=n, m=m, j=j, k=k: e.matmul(
                    ops[k % 2][:, 0:512],
                    c["ctxT"][:, j * TOK + m * 128:j * TOK + (m + 1) * 128],
                    woslab[:, (n % 2) * NJ * 512 + j * 512:
                           (n % 2) * NJ * 512 + (j + 1) * 512],
                    start=(j == 0), stop=(j == NJ - 1))),
                    inc=("s_pst" if last else None),
                    mark=((P, "mm", k) if last else None))

    for k in range(NB):
        n, m = k // 8, k % 8
        wl = [W((P, "mm", k)), W((P, "x1", k))]
        if k >= 2:
            wl.append(W((P, "st", k - 2)))
        ph.op(v, (lambda e, k=k: e.tensor_tensor(
            x2st[:, (k % 2) * 512:(k % 2 + 1) * 512], ops[k % 2][:, 0:512],
            x1ring[:, (k % 2) * 512:(k % 2 + 1) * 512], op=ALU.add)), waits=wl)
        ph.op(v, lambda e: e.drain())
        ph.op(v, (lambda e, n=n, m=m, k=k: e.bn_stats(
            c["stat63"][:, m * 48 + n * 6:m * 48 + (n + 1) * 6],
            x2st[:, (k % 2) * 512:(k % 2 + 1) * 512])), inc="s_bev",
            mark=(P, "bev", k))
    ph.emit()


def emit_debug_stats(c):
    ph = Phase(c, "dbgs")
    k0 = c["cnt"]["s_dbg"]
    ph.op("gpsimd", (lambda e: e.dma_start(c["dbg_mv"], c["mv1"][:])),
          inc="s_dbg", by=16)
    ph.op("gpsimd", (lambda e: e.dma_start(c["dbg_rs"], c["rstd1"][:])),
          inc="s_dbg", by=16)
    ph.op("gpsimd", (lambda e: e.dma_start(c["dbg_ve"], c["veps"][:])),
          inc="s_dbg", by=16)
    ph.op("gpsimd", None, waits=[WV("s_dbg", c["cnt"]["s_dbg"])])
    ph.emit()


def emit_debug_copy(c, src, dst, nbytes):
    ph = Phase(c, "dbg")
    ph.op("gpsimd", (lambda e: e.dma_start(dst, src)), inc="s_dbg", by=16)
    ph.op("gpsimd", None, waits=[WV("s_dbg", c["cnt"]["s_dbg"])])
    ph.emit()


DBG_N = [0]


def emit_debug_cast(c, src_bf, dst_f32, rows, cols):
    nc = c["nc"]
    ph = Phase(c, "dbgc")
    DBG_N[0] += 1
    with ExitStack() as st:
        buf = st.enter_context(nc.sbuf_tensor(f"dbgbuf{DBG_N[0]}", [128, cols], F32))
        for rr in range(rows // 128):
            k0 = c["cnt"]["s_dbg"]
            ph.op("gpsimd", (lambda e, rr=rr: e.dma_start(
                buf[:], src_bf[rr * 128:(rr + 1) * 128, :], max_dma_last_dim=2048)),
                inc="s_dbg", by=16)
            ph.op("gpsimd", None, waits=[WV("s_dbg", k0 + 16)])
            ph.op("gpsimd", (lambda e, rr=rr: e.dma_start(
                dst_f32[rr * 128:(rr + 1) * 128, :], buf[:])), inc="s_dbg", by=16)
            k1 = c["cnt"]["s_dbg"]
            ph.op("gpsimd", None, waits=[WV("s_dbg", k1)])
        ph.emit()


def emit_debug_ctx(c):
    nc = c["nc"]
    ph = Phase(c, "dbgx")
    with ExitStack() as st:
        buf = st.enter_context(nc.sbuf_tensor("dbgbufx", [128, TOK], F32))
        for j in range(NJ):
            k0 = c["cnt"]["s_dbg"]
            ph.op("gpsimd", (lambda e, j=j: e.tensor_copy(
                buf[:], c["ctxT"][:, j * TOK:(j + 1) * TOK])))
            ph.op("gpsimd", (lambda e, j=j: e.dma_start(
                c["dbg_ctx"][j * 128:(j + 1) * 128, :], buf[:])), inc="s_dbg", by=16)
            ph.op("gpsimd", None, waits=[WV("s_dbg", k0 + 16)])
        ph.emit()


def kernel(**inputs):
    img = host_images(inputs)
    X = np.asarray(inputs["L0_attn_hidden_states"], np.float32).reshape(S, D)

    nc = build_nc(debug=False)
    in_maps = []
    for cid in range(R):
        in_maps.append({
            "x": np.ascontiguousarray(X[cid * TOK:(cid + 1) * TOK, :]),
            "wq": img["wq"], "wk": img["wk"], "wv": img["wv"], "wo": img["wo"],
            "at": img["at"], "bfl": img["bfl"], "rw": img["rw"], "rb": img["rb"],
            "bvc": img["bvc"], "idf": img["idf"], "idb": img["idb"],
            "e16": img["e16"],
        })
    res = run_bass_kernel_spmd(nc, in_maps, core_ids=list(range(R)))
    out = np.concatenate([res.results[i]["out"] for i in range(R)], axis=0)
    return out.reshape(1, S, D)
